# version 02
# baseline (speedup 1.0000x reference)
import math
import numpy as np
from contextlib import ExitStack

import concourse.bass as bass
import concourse.mybir as mybir
from concourse.bass_utils import run_bass_kernel_spmd

F32 = mybir.dt.float32
BF16 = mybir.dt.bfloat16
I32 = mybir.dt.int32
AF = mybir.ActivationFunctionType
ALU = mybir.AluOpType
AX = mybir.AxisListType

T = 2048
D = 1024
KDEBUG_SERIAL_DMA = False
NCORES = 8
LN_EPS = 1e-5
NF = T // 512



class Sem:
    def __init__(self, h):
        self.h = h
        self.n = 0

    def inc(self, inst, k=1):
        inst.then_inc(self.h, k)
        self.n += k
        return self.n

    def dma(self, inst):
        return self.inc(inst, 16)

    def wait(self, eng, v=None):
        eng.wait_ge(self.h, self.n if v is None else v)


class Prog:
    def __init__(self):
        self.nc = bass.Bass("TRN2", target_bir_lowering=False)
        self.ctx = ExitStack()
        self._names = {}

    def _nm(self, base):
        i = self._names.get(base, 0)
        self._names[base] = i + 1
        return f"{base}_{i}" if i else base

    def sem(self, name="s"):
        return Sem(self.ctx.enter_context(self.nc.semaphore(self._nm(name))))

    def sbuf(self, name, shape, dt=F32, stack=None):
        return (stack or self.ctx).enter_context(
            self.nc.sbuf_tensor(self._nm(name), list(shape), dt))

    def psum(self, name, shape, dt=F32, stack=None):
        return (stack or self.ctx).enter_context(
            self.nc.psum_tensor(self._nm(name), list(shape), dt))

    def din(self, name, shape, dt=F32):
        return self.nc.dram_tensor(name, list(shape), dt, kind="ExternalInput").ap()

    def dout(self, name, shape, dt=F32):
        return self.nc.dram_tensor(name, list(shape), dt, kind="ExternalOutput").ap()

    def dtmp(self, name, shape, dt=F32):
        return self.nc.dram_tensor(self._nm(name), list(shape), dt,
                                   kind="Internal").ap()

    def block(self):
        return self.ctx.enter_context(self.nc.Block())


def fm_ap(t_hbm, p=128):
    return t_hbm.rearrange("(j p) t -> p j t", p=p)


class B:

    def __init__(self, tag=""):
        self.p = Prog()
        self.nc = self.p.nc
        self.tag = tag
        p = self.p
        self.ld = p.sem("ld")
        self.gld = p.sem("gld")
        self.st = p.sem("st")
        self.gst = p.sem("gst")
        self.pe = p.sem("pe")
        self.dve = p.sem("dve")
        self.act = p.sem("act")
        self.pool = p.sem("pool")
        self.blk = p.block()
        self.const_inputs = {}
        self.ones_row_f = p.sbuf("c_ones_row_f", (1, 128), F32)
        self.ones_row_b = p.sbuf("c_ones_row_b", (1, 512), BF16)
        self.onesf_col = p.sbuf("c_onesf_col", (128, 2), F32)

    def init_consts(self):
        def f(v):
            v.memset(self.ones_row_f[:], 1.0)
            v.memset(self.ones_row_b[:], 1.0)
            self.dve.inc(v.memset(self.onesf_col[:], 1.0))
        self.blk.vector(f)

    def barrier(self):
        vals = [(s, s.n) for s in
                (self.pe, self.dve, self.act, self.pool, self.ld, self.st,
                 self.gld, self.gst)]

        def mk(eng):
            for s, v in vals:
                s.wait(eng, v)
        for sec in (self.blk.tensor, self.blk.vector, self.blk.scalar,
                    self.blk.gpsimd, self.blk.sync):
            sec(mk)

    def load(self, dst_ap, src_ap, cast=False, sem=None, **kw):
        sem = sem or (self.gld if cast else self.ld)

        def f(eng):
            if KDEBUG_SERIAL_DMA:
                sem.wait(eng)
            sem.dma(eng.dma_start(out=dst_ap, in_=src_ap, **kw))
        (self.blk.gpsimd if cast else self.blk.sync)(f)

    def store(self, dst_ap, src_ap, cast=False, **kw):
        sem = self.gst if cast else self.st

        def f(eng):
            if KDEBUG_SERIAL_DMA:
                sem.wait(eng)
            sem.dma(eng.dma_start(out=dst_ap, in_=src_ap, **kw))
        (self.blk.gpsimd if cast else self.blk.sync)(f)

    def finish(self):
        st, gst = self.st, self.gst

        def f(eng):
            st.wait(eng)
            gst.wait(eng)
        self.blk.sync(f)

    def const(self, name, arr, dt=F32, sbuf_dt=None):
        arr = np.asarray(arr, np.float32 if dt is F32 else None)
        t = self.p.din(name, arr.shape, dt)
        sb = self.p.sbuf("c_" + name, arr.shape, sbuf_dt or dt)
        self.const_inputs[name] = arr
        self.load(sb[:], t[:], cast=(sbuf_dt is not None and sbuf_dt is not dt))
        return sb



def emit_mm_tiled(b, psum_pool, out_write, lhsT_tile, rhs_tile, nc_tiles,
                  out_rows, out_cols, fsz=512, extra_rank1=None,
                  evac="dve", evac_dt=None):
    p = b.p
    with ExitStack() as ph:
        nrt = (out_rows + 127) // 128
        nft = (out_cols + fsz - 1) // fsz
        pp = [p.psum("mmp", (128, fsz), F32, stack=ph) for _ in range(2)]
        order = [(rt, ft) for rt in range(nrt) for ft in range(nft)]
        pe0, ev0 = b.pe.n, (b.dve.n if evac == "dve" else b.act.n)
        evs = b.dve if evac == "dve" else b.act

        def pe(t):
            for i, (rt, ft) in enumerate(order):
                ps = pp[i % 2]
                if i >= 2:
                    evs.wait(t, ev0 + i - 1)
                mm = None
                for kt in range(nc_tiles):
                    mm = t.matmul(ps[:], lhsT_tile(kt, rt), rhs_tile(kt, ft),
                                  start=(kt == 0),
                                  stop=(kt == nc_tiles - 1 and extra_rank1 is None))
                if extra_rank1 is not None:
                    l1, r1 = extra_rank1(rt, ft)
                    mm = t.matmul(ps[:], l1, r1, start=False, stop=True)
                b.pe.inc(mm)

        def ev(eng):
            for i, (rt, ft) in enumerate(order):
                ps = pp[i % 2]
                b.pe.wait(eng, pe0 + i + 1)
                evs.inc(out_write(eng, rt, ft, ps))

        b.blk.tensor(pe)
        (b.blk.vector if evac == "dve" else b.blk.scalar)(ev)
        b.barrier()


def ln_fm_stream(b, x_hbm_fm, xn_fm, nt=8, eps=LN_EPS, keep_x=None,
                 hilo_gamma=None):
    p = b.p
    nD = nt * 128
    with ExitStack() as ph:
        xt = p.sbuf("ln_xt", (128, 2, T), F32, stack=ph)
        x2 = p.sbuf("ln_x2", (128, T), BF16, stack=ph)
        r_mean = p.sbuf("ln_rmean", (1, T), F32, stack=ph)
        r_msq = p.sbuf("ln_rmsq", (1, T), F32, stack=ph)
        r_a = p.sbuf("ln_ra", (1, T), F32, stack=ph)
        r_b = p.sbuf("ln_rb", (1, T), F32, stack=ph)
        pa_s = p.sbuf("ln_pas", (128, T), F32, stack=ph)
        pb_s = p.sbuf("ln_pbs", (128, T), F32, stack=ph)
        onesb = p.sbuf("ln_onesb", (128, 1), BF16, stack=ph)
        ps = p.psum("ln_ps", (33, T), F32, stack=ph)
        pbc = p.psum("ln_pbc", (128, 512), F32, stack=ph)

        def vinit(v):
            b.dve.inc(v.memset(onesb[:], 1.0))
        b.blk.vector(vinit)

        ld0 = b.ld.n
        act0 = b.act.n
        pe0 = b.pe.n

        def loads1(sy):
            for j in range(nt):
                if j >= 2:
                    b.pe.wait(sy, pe0 + j - 1)
                if KDEBUG_SERIAL_DMA:
                    b.ld.wait(sy)
                b.ld.dma(sy.dma_start(out=xt[:, j % 2, :],
                                      in_=x_hbm_fm[:, j, :]))
        b.blk.sync(loads1)

        def act1b(a):
            for j in range(nt):
                b.ld.wait(a, ld0 + 16 * (j + 1))
                if j >= 1:
                    b.pe.wait(a, pe0 + j)
                b.act.inc(a.activation(x2[:], xt[:, j % 2, :], AF.Square))
        b.blk.scalar(act1b)

        def pe1(t):
            b.dve.wait(t)
            for j in range(nt):
                b.ld.wait(t, ld0 + 16 * (j + 1))
                for f in range(NF):
                    sl = slice(f * 512, (f + 1) * 512)
                    t.matmul(ps[0:1, sl], b.onesf_col[:, 0:1], xt[:, j % 2, sl],
                             start=(j == 0), stop=(j == nt - 1))
                b.act.wait(t, act0 + j + 1)
                for f in range(NF):
                    sl = slice(f * 512, (f + 1) * 512)
                    mm = t.matmul(ps[32:33, sl], onesb[:], x2[:, sl],
                                  start=(j == 0), stop=(j == nt - 1))
                b.pe.inc(mm)
        b.blk.tensor(pe1)
        b.barrier()

        def dve1(v):
            v.tensor_scalar_mul(r_mean[:], ps[0:1, :], 1.0 / nD)
            v.tensor_scalar_mul(r_msq[:], ps[32:33, :], 1.0 / nD)
            v.tensor_mul(r_a[:], r_mean[:], r_mean[:])
            b.dve.inc(v.tensor_sub(r_a[:], r_msq[:], r_a[:]))
        b.blk.vector(dve1)

        def dve1b(v):
            b.dve.inc(v.tensor_scalar_add(r_a[:], r_a[:], float(eps)))
        b.blk.vector(dve1b)

        def act2(a):
            b.dve.wait(a)
            a.activation(r_a[:], r_a[:], AF.Ln)
            b.act.inc(a.activation(r_a[:], r_a[:], AF.Exp, scale=-0.5))
        b.blk.scalar(act2)

        def dve2(v):
            b.act.wait(v)
            v.tensor_mul(r_b[:], r_mean[:], r_a[:])
            b.dve.inc(v.tensor_scalar_mul(r_b[:], r_b[:], -1.0))
        b.blk.vector(dve2)

        pe2_0 = b.pe.n
        dve3_0 = b.dve.n

        def pe2(t):
            b.dve.wait(t, dve3_0)
            for i in range(2 * NF):
                f, which = i // 2, i % 2
                sl = slice(f * 512, (f + 1) * 512)
                if i >= 1:
                    b.dve.wait(t, dve3_0 + i)
                b.pe.inc(t.matmul(pbc[:], b.ones_row_f[:],
                                  (r_a if which == 0 else r_b)[:, sl],
                                  start=True, stop=True))
        b.blk.tensor(pe2)

        def dve3(v):
            for i in range(2 * NF):
                f, which = i // 2, i % 2
                sl = slice(f * 512, (f + 1) * 512)
                b.pe.wait(v, pe2_0 + i + 1)
                dst = pa_s if which == 0 else pb_s
                b.dve.inc(v.tensor_copy(dst[:, sl], pbc[:]))
        b.blk.vector(dve3)
        b.barrier()

        ld1 = b.ld.n
        dve4_0 = b.dve.n
        kper = 5 if hilo_gamma is not None else (3 if keep_x is not None else 2)

        def loads2(sy):
            for j in range(nt):
                if j >= 2:
                    b.dve.wait(sy, dve4_0 + kper * (j - 1))
                if KDEBUG_SERIAL_DMA:
                    b.ld.wait(sy)
                b.ld.dma(sy.dma_start(out=xt[:, j % 2, :],
                                      in_=x_hbm_fm[:, j, :]))
        b.blk.sync(loads2)

        if hilo_gamma is None:
            def dve4(v):
                for j in range(nt):
                    b.ld.wait(v, ld1 + 16 * (j + 1))
                    if keep_x is not None:
                        v.tensor_copy(keep_x[:, j, :], xt[:, j % 2, :])
                    v.tensor_mul(xn_fm[:, j, :], xt[:, j % 2, :], pa_s[:])
                    b.dve.inc(v.tensor_add(xn_fm[:, j, :], xn_fm[:, j, :],
                                           pb_s[:]),
                              3 if keep_x is not None else 2)
            b.blk.vector(dve4)
        else:
            hi, lo, gcol = hilo_gamma
            tf = p.sbuf("ln_tf", (128, T), F32, stack=ph)

            def dve4(v):
                for j in range(nt):
                    b.ld.wait(v, ld1 + 16 * (j + 1))
                    v.tensor_mul(tf[:], xt[:, j % 2, :], pa_s[:])
                    v.tensor_add(tf[:], tf[:], pb_s[:])
                    v.tensor_scalar(tf[:], tf[:], gcol[:, j:j + 1], None,
                                    ALU.mult)
                    v.tensor_copy(hi[:, j, :], tf[:])
                    b.dve.inc(v.tensor_sub(lo[:, j, :], tf[:], hi[:, j, :]),
                              kper)
            b.blk.vector(dve4)
        b.barrier()



def build_l1_mixer(debug=False, upto=99):
    H, DH = 16, 64
    SCALE = DH ** -0.5
    b = B("l1mix")
    p = b.p

    t_x = p.din("xT", (D, T))
    t_qkvwT = p.din("qkv_wT", (D, 3 * D))
    t_qnw = p.din("q_nw", (DH, 1))
    t_qnb = p.din("q_nb", (DH, 1))
    t_knw = p.din("k_nw", (DH, 1))
    t_knb = p.din("k_nb", (DH, 1))
    t_projwT = p.din("proj_wT", (D, D))
    t_projb = p.din("proj_b", (128, 8))
    t_xn = p.dout("xnT", (D, T), BF16)
    t_s = p.dout("sT", (D, T), BF16)

    xn = p.sbuf("xn", (128, 8, T), BF16)
    qnw = p.sbuf("qnw", (128, 2), F32)
    knw = p.sbuf("knw", (128, 2), F32)
    projb = p.sbuf("projb", (128, 8), F32)
    b.init_consts()
    np_blk2T = np.zeros((2, 128), np.float32)
    np_blk2T[0, 0:64] = 1.0
    np_blk2T[1, 64:128] = 1.0
    np_blk2 = np_blk2T.T.copy()
    np_sel0 = np.zeros((2, 128), np.float32)
    np_sel0[0] = 1.0
    np_sel1 = np.zeros((2, 128), np.float32)
    np_sel1[1] = 1.0
    blk2T = b.const("c_blk2T", np_blk2T)
    blk2 = b.const("c_blk2", np_blk2, sbuf_dt=BF16)
    sel = [b.const("c_sel0", np_sel0, sbuf_dt=BF16),
           b.const("c_sel1", np_sel1, sbuf_dt=BF16)]

    b.load(qnw[0:64, 0:1], t_qnw)
    b.load(qnw[64:128, 0:1], t_qnw)
    b.load(qnw[0:64, 1:2], t_qnb)
    b.load(qnw[64:128, 1:2], t_qnb)
    b.load(knw[0:64, 0:1], t_knw)
    b.load(knw[64:128, 0:1], t_knw)
    b.load(knw[0:64, 1:2], t_knb)
    b.load(knw[64:128, 1:2], t_knb)
    b.load(projb[:], t_projb)
    b.barrier()

    ln_fm_stream(b, fm_ap(t_x), xn)
    b.store(fm_ap(t_xn), xn[:])


    qf = p.sbuf("qf", (128, 8, T), BF16)
    kf = p.sbuf("kf", (128, 8, T), BF16)
    vt = p.sbuf("vt", (128, 16, H * 65), BF16)
    yn = p.sbuf("yn", (128, 8, T), BF16)
    vt_h = vt[:].rearrange("p a (h c) -> p a h c", h=H)

    def cinit2(v):
        b.dve.inc(v.memset(vt_h[:, :, :, 64:65], 1.0))
    b.blk.vector(cinit2)
    b.barrier()

    def cp_ev(dst):
        def w(eng, rt, ft, ps):
            return eng.tensor_copy(dst[:, rt, ft * 512:(ft + 1) * 512], ps[:])
        return w

    with ExitStack() as sB:
        qkwT = p.sbuf("qkwT", (128, 8, 2 * D), BF16, stack=sB)
        b.load(qkwT[:], fm_ap(t_qkvwT)[:, :, 0:2 * D], cast=True)
        b.barrier()
        emit_mm_tiled(b, None, cp_ev(qf),
                      lambda kt, rt: qkwT[:, kt, rt * 128:(rt + 1) * 128],
                      lambda kt, ft: xn[:, kt, ft * 512:(ft + 1) * 512],
                      8, D, T)
        emit_mm_tiled(b, None, cp_ev(kf),
                      lambda kt, rt: qkwT[:, kt, D + rt * 128:
                                          D + (rt + 1) * 128],
                      lambda kt, ft: xn[:, kt, ft * 512:(ft + 1) * 512],
                      8, D, T)

    with ExitStack() as sB2:
        vwT = p.sbuf("vwT", (128, 8, D), BF16, stack=sB2)
        b.load(vwT[:], fm_ap(t_qkvwT)[:, :, 2 * D:3 * D], cast=True)
        b.barrier()

        def vev(eng, rt, ft, ps):
            return eng.tensor_copy(
                vt_h[:, rt, ft * 8:(ft + 1) * 8, 0:64],
                ps[:].rearrange("p (h c) -> p h c", h=8))
        emit_mm_tiled(b, None, vev,
                      lambda kt, rt: xn[:, kt, rt * 128:(rt + 1) * 128],
                      lambda kt, ft: vwT[:, kt, ft * 512:(ft + 1) * 512],
                      8, T, D)

    def qk_ln_full(xf, gb, is_k):
        with ExitStack() as ph:
            x2 = p.sbuf("qkln_x2", (128, T), BF16, stack=ph)
            sums = p.sbuf("qkln_sums", (2, T), F32, stack=ph)
            sqs = p.sbuf("qkln_sqs", (2, T), F32, stack=ph)
            rA = p.sbuf("qkln_rA", (2, T), F32, stack=ph)
            rB = p.sbuf("qkln_rB", (2, T), F32, stack=ph)
            pss = p.psum("qkln_ps", (2, 512), F32, stack=ph)
            pn2 = p.psum("qkln_pn", (2, 512), F32, stack=ph)
            pa = p.psum("qkln_pa", (128, 512), F32, stack=ph)
            pb = p.psum("qkln_pb", (128, 512), F32, stack=ph)
            kmcol = p.sbuf("qkln_km", (128, 1), F32, stack=ph)
            for j in range(8):
                def act1(a, j=j):
                    b.act.inc(a.activation(x2[:], xf[:, j, :], AF.Square))
                b.blk.scalar(act1)

                pe0 = b.pe.n
                dve0 = b.dve.n

                def pe1(t, j=j, pe0=pe0, dve0=dve0):
                    b.act.wait(t)
                    for f in range(NF):
                        sl = slice(f * 512, (f + 1) * 512)
                        if f >= 1:
                            b.dve.wait(t, dve0 + 2 * f)
                        t.matmul(pss[:], blk2[:], xf[:, j, sl], start=True,
                                 stop=True)
                        b.pe.inc(t.matmul(pn2[:], blk2[:], x2[:, sl],
                                          start=True, stop=True))
                b.blk.tensor(pe1)

                def dve1(v, pe0=pe0):
                    for f in range(NF):
                        sl = slice(f * 512, (f + 1) * 512)
                        b.pe.wait(v, pe0 + f + 1)
                        b.dve.inc(v.tensor_copy(sums[:, sl], pss[:]))
                        b.dve.inc(v.tensor_copy(sqs[:, sl], pn2[:]))
                b.blk.vector(dve1)

                def dve2(v):
                    v.tensor_scalar_mul(sums[:], sums[:], 1.0 / DH)
                    v.tensor_scalar_mul(sqs[:], sqs[:], 1.0 / DH)
                    v.tensor_mul(rA[:], sums[:], sums[:])
                    b.dve.inc(v.tensor_sub(rA[:], sqs[:], rA[:]))
                b.blk.vector(dve2)

                def dve2b(v):
                    b.dve.inc(v.tensor_scalar_add(rA[:], rA[:],
                                                  float(LN_EPS)))
                b.blk.vector(dve2b)

                def act2(a):
                    b.dve.wait(a)
                    a.activation(rA[:], rA[:], AF.Ln)
                    b.act.inc(a.activation(rA[:], rA[:], AF.Exp, scale=-0.5))
                b.blk.scalar(act2)

                def dve3(v):
                    b.act.wait(v)
                    v.tensor_mul(rB[:], sums[:], rA[:])
                    b.dve.inc(v.tensor_scalar_mul(rB[:], rB[:], -1.0))
                b.blk.vector(dve3)

                pe2_0 = b.pe.n
                dve4_0 = b.dve.n

                def pe2(t, pe2_0=pe2_0, dve4_0=dve4_0):
                    b.dve.wait(t, dve4_0)
                    for f in range(NF):
                        sl = slice(f * 512, (f + 1) * 512)
                        if f >= 1:
                            b.dve.wait(t, dve4_0 + 3 * f)
                        t.matmul(pa[:], blk2T[:], rA[:, sl], start=True,
                                 stop=True)
                        b.pe.inc(t.matmul(pb[:], blk2T[:], rB[:, sl],
                                          start=True, stop=True))
                b.blk.tensor(pe2)

                def dve4(v, j=j, pe2_0=pe2_0, gb=gb):
                    for f in range(NF):
                        sl = slice(f * 512, (f + 1) * 512)
                        b.pe.wait(v, pe2_0 + f + 1)
                        v.tensor_mul(xf[:, j, sl], xf[:, j, sl], pa[:])
                        v.tensor_add(xf[:, j, sl], xf[:, j, sl], pb[:])
                        b.dve.inc(v.tensor_scalar(xf[:, j, sl], xf[:, j, sl],
                                                  gb[:, 0:1], gb[:, 1:2],
                                                  ALU.mult, ALU.add), 3)
                b.blk.vector(dve4)
                b.barrier()

                if is_k:
                    def dve6(v, j=j):
                        b.dve.inc(v.tensor_reduce(kmcol[:], xf[:, j, :],
                                                   AX.X, ALU.add))
                        b.dve.wait(v)
                        b.dve.inc(v.tensor_scalar_mul(kmcol[:], kmcol[:],
                                                      1.0 / T))
                        b.dve.wait(v)
                        b.dve.inc(v.tensor_scalar(xf[:, j, :], xf[:, j, :],
                                                  kmcol[:], None,
                                                  ALU.subtract))
                    b.blk.vector(dve6)
                    b.barrier()
    qk_ln_full(qf, qnw, False)
    qk_ln_full(kf, knw, True)

    if upto <= 1:
        t_dq = p.dout("dbg_q", (D, T), BF16)
        t_dk = p.dout("dbg_k", (D, T), BF16)
        b.store(fm_ap(t_dq), qf[:])
        b.store(fm_ap(t_dk), kf[:])
        b.finish()
        return b

    with ExitStack() as sD:
        mneg = p.sbuf("mneg", (2, T), BF16, stack=sD)
        x2m = p.sbuf("mh_x2", (128, T), BF16, stack=sD)
        qn2 = p.sbuf("mh_qn2", (2, T), F32, stack=sD)
        kn2 = p.sbuf("mh_kn2", (2, T), F32, stack=sD)
        kmx = p.sbuf("mh_kmx", (2, 1), F32, stack=sD)
        pS = [p.psum("at_ps", (128, 512), F32, stack=sD) for _ in range(2)]
        pY = p.psum("at_py", (65, T), F32, stack=sD)
        pR = p.psum("at_pr", (64, 512), F32, stack=sD)
        es = [p.sbuf("at_es", (128, 512), BF16, stack=sD) for _ in range(2)]
        rrow = p.sbuf("at_rrow", (1, T), F32, stack=sD)
        rbc = p.sbuf("at_rbc", (64, 512), F32, stack=sD)

        for jh in range(8):
            for which, (src, dst2) in enumerate(((qf, qn2), (kf, kn2))):
                def act1(a, jh=jh, src=src):
                    b.act.inc(a.activation(x2m[:], src[:, jh, :], AF.Square))
                b.blk.scalar(act1)
                pe0 = b.pe.n
                dve0 = b.dve.n

                def pe1(t, pe0=pe0, dve0=dve0):
                    b.act.wait(t)
                    for f in range(NF):
                        if f >= 1:
                            b.dve.wait(t, dve0 + f)
                        b.pe.inc(t.matmul(pS[0][0:2, :], blk2[:],
                                          x2m[:, f * 512:(f + 1) * 512],
                                          start=True, stop=True))
                b.blk.tensor(pe1)

                def dve1(v, pe0=pe0, dst2=dst2):
                    for f in range(NF):
                        b.pe.wait(v, pe0 + f + 1)
                        b.dve.inc(v.tensor_copy(
                            dst2[0:2, f * 512:(f + 1) * 512], pS[0][0:2, :]))
                b.blk.vector(dve1)
                b.barrier()

            def dve3(v):
                b.dve.inc(v.tensor_reduce(kmx[:], kn2[:], AX.X, ALU.max))
            b.blk.vector(dve3)

            def act3(a):
                b.dve.wait(a)
                b.act.inc(a.activation(kn2[:], qn2[:], AF.Sqrt, scale=kmx[:]))
            b.blk.scalar(act3)

            def dve4(v):
                b.act.wait(v)
                b.dve.inc(v.tensor_scalar_mul(mneg[:], kn2[:], -1.0))
            b.blk.vector(dve4)
            b.barrier()

            for sh in range(2):
                h = 2 * jh + sh
                base = 64 * sh
                it = [(tk, ff) for tk in range(16) for ff in range(4)]
                pe0, act0 = b.pe.n, b.act.n

                def pe_a(t, jh=jh, sh=sh, base=base, it=it, pe0=pe0,
                         act0=act0, h=h):
                    def y_mm(pi):
                        ptk, pff = it[pi]
                        b.act.wait(t, act0 + pi + 1)
                        b.pe.inc(t.matmul(
                            pY[:, pff * 512:(pff + 1) * 512],
                            vt_h[:, ptk, h, :], es[pi % 2][:],
                            start=(ptk == 0), stop=(ptk == 15)))
                    for i, (tk, ff) in enumerate(it):
                        ps = pS[i % 2]
                        if i >= 2:
                            b.act.wait(t, act0 + i - 1)
                        t.matmul(ps[:], kf[base:base + 64, jh,
                                           tk * 128:(tk + 1) * 128],
                                 qf[base:base + 64, jh,
                                    ff * 512:(ff + 1) * 512],
                                 start=True, stop=False)
                        b.pe.inc(t.matmul(ps[:], sel[sh][:],
                                          mneg[:, ff * 512:(ff + 1) * 512],
                                          start=False, stop=True))
                        if i >= 1:
                            y_mm(i - 1)
                    y_mm(len(it) - 1)

                def act_a(a, it=it, pe0=pe0, act0=act0):
                    for i in range(len(it)):
                        cnt = pe0 + (2 * i if i >= 1 else 1)
                        b.pe.wait(a, cnt)
                        b.act.inc(a.activation(es[i % 2][:], pS[i % 2][:],
                                               AF.Exp, scale=float(SCALE)))
                b.blk.tensor(pe_a)
                b.blk.scalar(act_a)
                b.barrier()

                def act_n(a):
                    a.activation(rrow[:], pY[64:65, :], AF.Ln)
                    b.act.inc(a.activation(rrow[:], rrow[:], AF.Exp,
                                           scale=-1.0))
                b.blk.scalar(act_n)

                pe1_0 = b.pe.n
                dve1_0 = b.dve.n

                def pe_n(t, pe1_0=pe1_0, dve1_0=dve1_0):
                    b.act.wait(t)
                    for f in range(NF):
                        if f >= 1:
                            b.dve.wait(t, dve1_0 + 2 * f)
                        b.pe.inc(t.matmul(pR[:], b.ones_row_f[0:1, 0:64],
                                          rrow[0:1, f * 512:(f + 1) * 512],
                                          start=True, stop=True))

                def dve_n(v, jh=jh, base=base, pe1_0=pe1_0):
                    for f in range(NF):
                        sl = slice(f * 512, (f + 1) * 512)
                        b.pe.wait(v, pe1_0 + f + 1)
                        v.tensor_copy(rbc[:], pR[:])
                        b.dve.inc(v.tensor_mul(yn[base:base + 64, jh, sl],
                                               pY[0:64, sl], rbc[:]), 2)
                b.blk.tensor(pe_n)
                b.blk.vector(dve_n)
                b.barrier()

    if debug:
        t_dq = p.dout("dbg_q", (D, T), BF16)
        t_dk = p.dout("dbg_k", (D, T), BF16)
        t_dyn = p.dout("dbg_yn", (D, T), BF16)
        t_dvt = p.dout("dbg_vt", (128, 16 * H * 65), BF16)
        b.store(fm_ap(t_dq), qf[:])
        b.store(fm_ap(t_dk), kf[:])
        b.store(fm_ap(t_dyn), yn[:])
        b.store(t_dvt.rearrange("p (a c) -> p a c", a=16), vt[:])
        b.barrier()

    with ExitStack() as sE:
        projwT = p.sbuf("projwT", (128, 8, D), BF16, stack=sE)
        b.load(projwT[:], fm_ap(t_projwT), cast=True)
        b.barrier()
        sfm = xn

        def pev(eng, rt, ft, ps):
            return eng.tensor_scalar_add(sfm[:, rt, ft * 512:(ft + 1) * 512],
                                         ps[:], projb[:, rt:rt + 1])
        emit_mm_tiled(b, None, pev,
                      lambda kt, rt: projwT[:, kt, rt * 128:(rt + 1) * 128],
                      lambda kt, ft: yn[:, kt, ft * 512:(ft + 1) * 512],
                      8, D, T, evac="dve")
        b.store(fm_ap(t_s), sfm[:])
        b.finish()
    return b



def kernel(**inputs):
    raise NotImplementedError


def ln_stats_resident(b, x_fm, nt, pa_s, pb_s, eps=LN_EPS):
    p = b.p
    nD = nt * 128
    with ExitStack() as ph:
        x2 = p.sbuf("lnr_x2", (128, T), BF16, stack=ph)
        r_mean = p.sbuf("lnr_rm", (1, T), F32, stack=ph)
        r_msq = p.sbuf("lnr_rq", (1, T), F32, stack=ph)
        r_a = p.sbuf("lnr_ra", (1, T), F32, stack=ph)
        r_b = p.sbuf("lnr_rb", (1, T), F32, stack=ph)
        onesb = p.sbuf("lnr_ones", (128, 1), BF16, stack=ph)
        ps = p.psum("lnr_ps", (33, T), F32, stack=ph)
        pbc = p.psum("lnr_pbc", (128, 512), F32, stack=ph)

        def vinit(v):
            b.dve.inc(v.memset(onesb[:], 1.0))
        b.blk.vector(vinit)

        act0 = b.act.n

        def act1(a):
            for j in range(nt):
                if j >= 1:
                    b.pe.wait(a, pe0 + j)
                b.act.inc(a.activation(x2[:], x_fm[:, j, :], AF.Square))
        pe0 = b.pe.n

        def pe1(t):
            b.dve.wait(t)
            for j in range(nt):
                for f in range(NF):
                    sl = slice(f * 512, (f + 1) * 512)
                    t.matmul(ps[0:1, sl], onesb[:], x_fm[:, j, sl],
                             start=(j == 0), stop=(j == nt - 1))
                b.act.wait(t, act0 + j + 1)
                mm = None
                for f in range(NF):
                    sl = slice(f * 512, (f + 1) * 512)
                    mm = t.matmul(ps[32:33, sl], onesb[:], x2[:, sl],
                                  start=(j == 0), stop=(j == nt - 1))
                b.pe.inc(mm)
        b.blk.scalar(act1)
        b.blk.tensor(pe1)
        b.barrier()

        def dve1(v):
            v.tensor_scalar_mul(r_mean[:], ps[0:1, :], 1.0 / nD)
            v.tensor_scalar_mul(r_msq[:], ps[32:33, :], 1.0 / nD)
            v.tensor_mul(r_a[:], r_mean[:], r_mean[:])
            v.tensor_sub(r_a[:], r_msq[:], r_a[:])
            b.dve.inc(v.tensor_scalar_add(r_a[:], r_a[:], float(eps)))
        b.blk.vector(dve1)

        def act2(a):
            b.dve.wait(a)
            a.activation(r_a[:], r_a[:], AF.Ln)
            b.act.inc(a.activation(r_a[:], r_a[:], AF.Exp, scale=-0.5))
        b.blk.scalar(act2)

        def dve2(v):
            b.act.wait(v)
            v.tensor_mul(r_b[:], r_mean[:], r_a[:])
            b.dve.inc(v.tensor_scalar_mul(r_b[:], r_b[:], -1.0))
        b.blk.vector(dve2)

        pe2_0 = b.pe.n
        dve3_0 = b.dve.n

        def pe2(t):
            b.dve.wait(t, dve3_0)
            for i in range(2 * NF):
                f, which = i // 2, i % 2
                sl = slice(f * 512, (f + 1) * 512)
                if i >= 1:
                    b.dve.wait(t, dve3_0 + i)
                b.pe.inc(t.matmul(pbc[:], b.ones_row_f[:],
                                  (r_a if which == 0 else r_b)[:, sl],
                                  start=True, stop=True))
        b.blk.tensor(pe2)

        def dve3(v):
            for i in range(2 * NF):
                f, which = i // 2, i % 2
                sl = slice(f * 512, (f + 1) * 512)
                b.pe.wait(v, pe2_0 + i + 1)
                dst = pa_s if which == 0 else pb_s
                b.dve.inc(v.tensor_copy(dst[:, sl], pbc[:]))
        b.blk.vector(dve3)
        b.barrier()



def build_l2_ffn():
    b = B("l2ffn")
    p = b.p
    FH = 4096

    t_x = p.din("xT", (D, T))
    t_xn = p.din("xnT", (D, T), BF16)
    t_s = p.din("sT", (D, T), BF16)
    t_q1 = p.din("fc1_q", (D, FH), I32)
    t_s1 = p.din("fc1_scales", (8, FH))
    t_b1 = p.din("fc1_bias", (128, 32))
    t_q2 = p.din("fc2_q", (FH, D), I32)
    t_s2 = p.din("fc2_scales", (32, D))
    t_b2 = p.din("fc2_bias", (128, 8))
    t_lng = p.din("ln_g", (128, 8))
    t_lnb = p.din("ln_b", (128, 8))
    t_out = p.dout("xoutT", (D, T))

    b.init_consts()

    xn = p.sbuf("xn", (128, 8, T), BF16)
    h2 = p.sbuf("h2", (128, 8, T), BF16)
    b1c = p.sbuf("b1c", (128, 32), F32)
    b2c = p.sbuf("b2c", (128, 8), F32)
    lng = p.sbuf("lng", (128, 8), F32)
    lnb = p.sbuf("lnb", (128, 8), F32)

    b.load(xn[:], fm_ap(t_xn))
    b.load(b1c[:], t_b1)
    b.load(b2c[:], t_b2)
    b.load(lng[:], t_lng)
    b.load(lnb[:], t_lnb)
    b.barrier()

    with ExitStack() as ph:
        pa_s = p.sbuf("pa_s", (128, T), F32, stack=ph)
        pb_s = p.sbuf("pb_s", (128, T), F32, stack=ph)
        ln_stats_resident(b, xn, 8, pa_s, pb_s)

        def dveh(v):
            for j in range(8):
                v.tensor_mul(h2[:, j, :], xn[:, j, :], pa_s[:])
                v.tensor_add(h2[:, j, :], h2[:, j, :], pb_s[:])
                b.dve.inc(v.tensor_scalar(h2[:, j, :], h2[:, j, :],
                                          lng[:, j:j + 1], lnb[:, j:j + 1],
                                          ALU.mult, ALU.add), 3)
        b.blk.vector(dveh)
        b.barrier()

    with ExitStack() as ph0:
        b2h = p.sbuf("b2h", (128, 8), F32, stack=ph0)

        def dveb(v):
            b.dve.inc(v.tensor_scalar_mul(b2h[:], b2c[:], 0.5))
            b.dve.wait(v)
            for j in range(8):
                b.dve.inc(v.tensor_scalar_add(xn[:, j, :], xn[:, j, :],
                                              b2h[:, j:j + 1]))
        b.blk.vector(dveb)
        b.barrier()

        t_ytmp = p.dtmp("l2f_ytmp", (FH, T), BF16)

        def dq(w, t_sc, nkt, ncols):
            with ExitStack() as ph:
                sbc = p.sbuf("dq_sbc", (128, 512), BF16, stack=ph)
                srow = p.sbuf("dq_srow", (1, 512), F32, stack=ph)
                pq = p.psum("dq_ps", (128, 512), F32, stack=ph)
                for kt in range(nkt):
                    for f in range(ncols // 512):
                        sl = slice(f * 512, (f + 1) * 512)
                        b.load(srow[:], t_sc[kt:kt + 1, sl])
                        b.barrier()

                        def pe1(t):
                            b.pe.inc(t.matmul(pq[:], b.ones_row_f[:],
                                              srow[:], start=True, stop=True))
                        b.blk.tensor(pe1)

                        def dve1(v, kt=kt, sl=sl):
                            b.pe.wait(v)
                            b.dve.inc(v.tensor_copy(sbc[:], pq[:]))
                            v.tensor_scalar_add(w[:, kt, sl], w[:, kt, sl],
                                                -128.0)
                            b.dve.inc(v.tensor_mul(w[:, kt, sl], w[:, kt, sl],
                                                   sbc[:]), 2)
                        b.blk.vector(dve1)
                        b.barrier()

        TC = 512
        with ExitStack() as ph:
            w1 = p.sbuf("w1", (128, 8, FH), BF16, stack=ph)
            b.load(w1[:], fm_ap(t_q1), cast=True)
            b.barrier()
            dq(w1, t_s1, 8, FH)
            ych = p.sbuf("ych", (128, 32, TC), BF16, stack=ph)
            py = [p.psum("l2f_py", (128, 512), F32, stack=ph)
                  for _ in range(2)]
            for tc in range(T // TC):
                tsl = slice(tc * TC, (tc + 1) * TC)
                pe0 = b.pe.n
                act0 = b.act.n

                def pe1(t, tsl=tsl, pe0=pe0, act0=act0):
                    for nt_ in range(32):
                        ps = py[nt_ % 2]
                        if nt_ >= 2:
                            b.act.wait(t, act0 + nt_ - 1)
                        mm = None
                        for kt in range(8):
                            mm = t.matmul(ps[:],
                                          w1[:, kt, nt_ * 128:(nt_ + 1) * 128],
                                          h2[:, kt, tsl],
                                          start=(kt == 0), stop=(kt == 7))
                        b.pe.inc(mm)
                b.blk.tensor(pe1)

                def act1(a, pe0=pe0):
                    for nt_ in range(32):
                        b.pe.wait(a, pe0 + nt_ + 1)
                        b.act.inc(a.activation(ych[:, nt_, :],
                                               py[nt_ % 2][:], AF.Silu,
                                               bias=b1c[:, nt_:nt_ + 1]))
                b.blk.scalar(act1)
                b.barrier()
                b.store(fm_ap(t_ytmp)[:, :, tsl], ych[:])
                b.barrier()

        with ExitStack() as ph:
            w2 = p.sbuf("w2", (128, 32, D), BF16, stack=ph)
            b.load(w2[:], fm_ap(t_q2), cast=True)
            b.barrier()
            dq(w2, t_s2, 32, D)
            TC2 = 256
            ych2 = p.sbuf("ych2", (128, 32, TC2), BF16, stack=ph)
            xch = p.sbuf("xch", (128, 8, TC2), F32, stack=ph)
            sch = p.sbuf("sch", (128, 8, TC2), BF16, stack=ph)
            och = p.sbuf("och", (128, 8, TC2), F32, stack=ph)
            gch = p.sbuf("gch", (128, TC2), BF16, stack=ph)
            tch = p.sbuf("tch", (128, TC2), F32, stack=ph)
            po = p.psum("l2f_po", (128, 8, TC2), F32, stack=ph)

            for tc in range(T // TC2):
                tsl = slice(tc * TC2, (tc + 1) * TC2)
                b.load(ych2[:], fm_ap(t_ytmp)[:, :, tsl])
                b.load(xch[:], fm_ap(t_x)[:, :, tsl])
                b.load(sch[:], fm_ap(t_s)[:, :, tsl])
                b.barrier()

                def pe2(t):
                    for dt_ in range(8):
                        mm = None
                        for kt in range(32):
                            mm = t.matmul(po[:, dt_, :],
                                          w2[:, kt, dt_ * 128:(dt_ + 1) * 128],
                                          ych2[:, kt, :],
                                          start=(kt == 0), stop=(kt == 31))
                        b.pe.inc(mm)
                b.blk.tensor(pe2)
                b.barrier()

                for dt_ in range(8):
                    def dvec(v, dt_=dt_, tsl=tsl):
                        b.dve.inc(v.scalar_tensor_tensor(
                            tch[:], po[:, dt_, :], 0.5, xn[:, dt_, tsl],
                            ALU.mult, ALU.add))
                    b.blk.vector(dvec)

                    def actc(a, dt_=dt_):
                        b.dve.wait(a)
                        b.act.inc(a.activation(gch[:], tch[:], AF.Silu))
                    b.blk.scalar(actc)

                    def dvec2(v, dt_=dt_):
                        b.act.wait(v)
                        v.tensor_mul(och[:, dt_, :], sch[:, dt_, :], gch[:])
                        b.dve.inc(v.tensor_add(och[:, dt_, :], och[:, dt_, :],
                                               xch[:, dt_, :]), 2)
                    b.blk.vector(dvec2)
                b.barrier()
                b.store(fm_ap(t_out)[:, :, tsl], och[:])
                b.barrier()

    b.finish()
    return b


FP4_THRESH = [0.25, 0.75, 1.25, 1.75, 2.5, 2.5, 3.5, 3.5, 5.0, 5.0, 5.0, 5.0]


def emit_fp4_quant(b, x_ap, xq_ap, ncols, scr, in_dt=F32):
    nb = ncols // 16
    bm, inv, axn, acc, sgn = (scr["bm"], scr["inv"], scr["axn"], scr["acc"],
                              scr["sgn"])

    def dve1(v):
        b.dve.inc(v.tensor_reduce(bm[:, 0:nb], x_ap.rearrange(
            "p (a c) -> p a c", c=16), AX.X, ALU.max,
            apply_absolute_value=True))
        v.tensor_scalar_mul(bm[:, 0:nb], bm[:, 0:nb], 1.0 / 6.0)
        b.dve.inc(v.tensor_scalar_max(bm[:, 0:nb], bm[:, 0:nb], 1e-12))
    b.blk.vector(dve1)

    def act1(a):
        b.dve.wait(a)
        a.activation(inv[:, 0:nb], bm[:, 0:nb], AF.Ln)
        a.activation(inv[:, 0:nb], inv[:, 0:nb], AF.Exp, scale=-1.0)
        b.act.inc(a.activation(sgn[:, 0:ncols], x_ap, AF.Sign))
    b.blk.scalar(act1)

    def dve2(v):
        b.act.wait(v)
        invb = inv[:, 0:nb].rearrange("p (a u) -> p a u", u=1).broadcast_to(
            (128, nb, 16))
        v.tensor_tensor(axn[:, 0:ncols].rearrange("p (a c) -> p a c", c=16),
                        x_ap.rearrange("p (a c) -> p a c", c=16), invb,
                        ALU.mult)
        b.dve.inc(v.nop())
    b.blk.vector(dve2)

    def act2(a):
        b.dve.wait(a)
        b.act.inc(a.activation(axn[:, 0:ncols], axn[:, 0:ncols], AF.Abs))
    b.blk.scalar(act2)

    def dve3(v):
        b.act.wait(v)
        v.tensor_scalar(acc[:, 0:ncols], axn[:, 0:ncols], FP4_THRESH[0], None,
                        ALU.is_gt)
        for th in FP4_THRESH[1:]:
            v.scalar_tensor_tensor(acc[:, 0:ncols], axn[:, 0:ncols], float(th),
                                   acc[:, 0:ncols], ALU.is_gt, ALU.add)
        v.tensor_scalar_mul(bm[:, 0:nb], bm[:, 0:nb], 0.5)
        b.dve.inc(v.tensor_mul(acc[:, 0:ncols], acc[:, 0:ncols],
                               sgn[:, 0:ncols]))
        b.dve.wait(v)
        bmb = bm[:, 0:nb].rearrange("p (a u) -> p a u", u=1).broadcast_to(
            (128, nb, 16))
        b.dve.inc(v.tensor_tensor(
            xq_ap.rearrange("p (a c) -> p a c", c=16),
            acc[:, 0:ncols].rearrange("p (a c) -> p a c", c=16), bmb,
            ALU.mult))
    b.blk.vector(dve3)


def mk_fp4_scratch(p, ncols, stack):
    return {
        "bm": p.sbuf("q_bm", (128, ncols // 16), F32, stack=stack),
        "inv": p.sbuf("q_inv", (128, ncols // 16), F32, stack=stack),
        "axn": p.sbuf("q_axn", (128, ncols), F32, stack=stack),
        "acc": p.sbuf("q_acc", (128, ncols), F32, stack=stack),
        "sgn": p.sbuf("q_sgn", (128, ncols), F32, stack=stack),
    }


def emit_transpose(b, src_ap_fn, dst_ap_fn, nrt, nct, ident, ptile):
    pe0 = b.pe.n
    dve0 = b.dve.n
    n = nrt * nct

    def pet(t):
        for k in range(n):
            i, j = k // nct, k % nct
            if k >= 1:
                b.dve.wait(t, dve0 + k)
            b.pe.inc(t.transpose(ptile[:], src_ap_fn(i, j), ident[:]))

    def dvet(v):
        for k in range(n):
            i, j = k // nct, k % nct
            b.pe.wait(v, pe0 + k + 1)
            b.dve.inc(v.tensor_copy(dst_ap_fn(j, i), ptile[:]))
    b.blk.tensor(pet)
    b.blk.vector(dvet)
    b.barrier()


def build_l1_ffn():
    IH = 2816
    NI = IH // 128
    b = B("l1ffn")
    p = b.p

    t_x = p.din("xT", (D, T))
    t_xn = p.din("xnT", (D, T), BF16)
    t_s = p.din("sT", (D, T), BF16)
    t_wg = p.din("w_gate", (IH, D))
    t_wu = p.din("w_up", (IH, D))
    t_wd = p.din("w_down", (D, IH))
    t_out = p.dout("xoutT", (D, T))
    t_hq = p.dtmp("l1f_hq", (T, IH), BF16)

    b.init_consts()
    ident = b.const("c_ident", np.eye(128, dtype=np.float32), sbuf_dt=BF16)

    xqT = p.sbuf("xqT", (128, 8, T), BF16)

    with ExitStack() as ph:
        xn = p.sbuf("xn", (128, 8, T), BF16, stack=ph)
        xtm = p.sbuf("xtm", (128, 16, D), BF16, stack=ph)
        ptile = p.psum("ptile", (128, 128), BF16, stack=ph)
        b.load(xn[:], fm_ap(t_xn))
        b.barrier()
        emit_transpose(b, lambda i, j: xn[:, i, j * 128:(j + 1) * 128],
                       lambda j, i: xtm[:, j, i * 128:(i + 1) * 128],
                       8, 16, ident, ptile)
        scr = mk_fp4_scratch(p, D, ph)
        for tt in range(16):
            emit_fp4_quant(b, xtm[:, tt, :], xtm[:, tt, :], D, scr)
            b.barrier()
        emit_transpose(b, lambda i, j: xtm[:, i, j * 128:(j + 1) * 128],
                       lambda j, i: xqT[:, j, i * 128:(i + 1) * 128],
                       16, 8, ident, ptile)

    def quant_weight_T(t_w, nrows_t, ncols, wqT):
        with ExitStack() as ph:
            wrow = p.sbuf("w_raw", (128, ncols), F32, stack=ph)
            wqb = p.sbuf("w_qb", (128, ncols), BF16, stack=ph)
            ptile = p.psum("ptile", (128, 128), BF16, stack=ph)
            scr = mk_fp4_scratch(p, ncols, ph)
            for rt in range(nrows_t):
                b.load(wrow[:], t_w.rearrange("(a p) c -> p a c", p=128)[:, rt, :])
                b.barrier()
                emit_fp4_quant(b, wrow[:], wqb[:], ncols, scr)
                b.barrier()
                emit_transpose(b, lambda i, j, : wqb[:, j * 128:(j + 1) * 128],
                               lambda j, i, rt=rt: wqT[:, j, rt * 128:(rt + 1) * 128],
                               1, ncols // 128, ident, ptile)

    with ExitStack() as phB:
        wgT = p.sbuf("wgT", (128, 8, IH), BF16, stack=phB)
        wuT = p.sbuf("wuT", (128, 8, IH), BF16, stack=phB)
        quant_weight_T(t_wg, NI, D, wgT)
        quant_weight_T(t_wu, NI, D, wuT)

        with ExitStack() as ph:
            gt = p.sbuf("gt", (128, IH), F32, stack=ph)
            ut = p.sbuf("ut", (128, IH), F32, stack=ph)
            ht = p.sbuf("ht", (128, IH), F32, stack=ph)
            hq = p.sbuf("hqt", (128, IH), BF16, stack=ph)
            pg = [p.psum("l1f_pg", (128, 512), F32, stack=ph) for _ in range(2)]
            scr = mk_fp4_scratch(p, IH, ph)
            NFI = IH // 256
            for tt in range(16):
                pe0 = b.pe.n
                dve0 = b.dve.n
                order = [(w, f) for w in range(2) for f in range(NFI)]

                def pe1(t, tt=tt, pe0=pe0, dve0=dve0, order=order):
                    for i, (w, f) in enumerate(order):
                        ps = pg[i % 2]
                        if i >= 2:
                            b.dve.wait(t, dve0 + i - 1)
                        mm = None
                        wsrc = wgT if w == 0 else wuT
                        for kt in range(8):
                            mm = t.matmul(
                                ps[:, 0:256],
                                xqT[:, kt, tt * 128:(tt + 1) * 128],
                                wsrc[:, kt, f * 256:(f + 1) * 256],
                                start=(kt == 0), stop=(kt == 7))
                        b.pe.inc(mm)
                b.blk.tensor(pe1)

                def dve1(v, pe0=pe0, order=order):
                    for i, (w, f) in enumerate(order):
                        b.pe.wait(v, pe0 + i + 1)
                        dst = gt if w == 0 else ut
                        b.dve.inc(v.tensor_copy(
                            dst[:, f * 256:(f + 1) * 256], pg[i % 2][:, 0:256]))
                b.blk.vector(dve1)
                b.barrier()

                def acth(a):
                    b.act.inc(a.activation(ht[:], gt[:], AF.Silu))
                b.blk.scalar(acth)

                def dveh(v):
                    b.act.wait(v)
                    b.dve.inc(v.tensor_mul(ht[:], ht[:], ut[:]))
                b.blk.vector(dveh)
                b.barrier()
                emit_fp4_quant(b, ht[:], hq[:], IH, scr)
                b.barrier()
                b.store(t_hq.rearrange("(a p) c -> p a c", p=128)[:, tt, :],
                        hq[:])
                b.barrier()

    with ExitStack() as phC:
        wdT = p.sbuf("wdT", (128, NI, D), BF16, stack=phC)
        quant_weight_T(t_wd, 8, IH, wdT)

        TC = 512
        with ExitStack() as ph:
            htm = p.sbuf("htm", (128, 4, IH), BF16, stack=ph)
            hqT = p.sbuf("hqT", (128, NI, TC), BF16, stack=ph)
            xch = p.sbuf("xch", (128, 8, TC), F32, stack=ph)
            sch = p.sbuf("sch", (128, 8, TC), BF16, stack=ph)
            och = p.sbuf("och", (128, 8, TC), F32, stack=ph)
            gch = p.sbuf("gch", (128, TC), BF16, stack=ph)
            ptile = p.psum("ptile", (128, 128), BF16, stack=ph)
            po = p.psum("l1f_po", (128, 4, TC), F32, stack=ph)
            for tc in range(T // TC):
                tsl = slice(tc * TC, (tc + 1) * TC)
                b.load(htm[:], t_hq.rearrange("(a p) c -> p a c", p=128)
                       [:, tc * 4:(tc + 1) * 4, :])
                b.load(xch[:], fm_ap(t_x)[:, :, tsl])
                b.load(sch[:], fm_ap(t_s)[:, :, tsl])
                b.barrier()
                emit_transpose(
                    b,
                    lambda i, j: htm[:, i, j * 128:(j + 1) * 128],
                    lambda j, i: hqT[:, j, i * 128:(i + 1) * 128],
                    4, NI, ident, ptile)

                for half in range(2):
                    def pe2(t, half=half):
                        for q_ in range(4):
                            dt_ = half * 4 + q_
                            mm = None
                            for kt in range(NI):
                                mm = t.matmul(
                                    po[:, q_, :],
                                    wdT[:, kt, dt_ * 128:(dt_ + 1) * 128],
                                    hqT[:, kt, :],
                                    start=(kt == 0), stop=(kt == NI - 1))
                            b.pe.inc(mm)
                    b.blk.tensor(pe2)
                    b.barrier()

                    for q_ in range(4):
                        dt_ = half * 4 + q_

                        def actc(a, q_=q_):
                            b.act.inc(a.activation(
                                gch[:], po[:, q_, :], AF.Silu))
                        b.blk.scalar(actc)

                        def dvec2(v, dt_=dt_):
                            b.act.wait(v)
                            v.tensor_mul(och[:, dt_, :], sch[:, dt_, :],
                                         gch[:])
                            b.dve.inc(v.tensor_add(och[:, dt_, :],
                                                   och[:, dt_, :],
                                                   xch[:, dt_, :]), 2)
                        b.blk.vector(dvec2)
                    b.barrier()
                b.store(fm_ap(t_out)[:, :, tsl], och[:])
                b.barrier()
    b.finish()
    return b


def build_l0_ffn(debug=False):
    ET = 4096
    NET = ET // 128
    FH = 4096
    b = B("l0ffn")
    p = b.p

    t_x = p.din("xT", (D, T))
    t_xn = p.din("xnT", (D, T), BF16)
    t_s = p.din("sT", (D, T), BF16)
    t_gx = p.din("gamma_x", (128, 8))
    t_gs = p.din("gamma_slot", (1, D))
    t_se = p.din("slot_embeds", (ET, D))
    t_w1 = p.din("w1", (D, FH))
    t_w2 = p.din("w2", (FH, D))
    t_out = p.dout("xoutT", (D, T))
    mk = p.dout if debug else p.dtmp
    t_lt = mk("l0f_lt", (ET, T))
    t_dt = mk("l0f_dt", (T, ET), BF16)
    t_eb = mk("l0f_eb", (ET, T), BF16)
    t_proc = mk("l0f_proc", (ET, D), BF16)

    b.init_consts()
    ident = b.const("c_ident", np.eye(128, dtype=np.float32), sbuf_dt=BF16)

    se_hbm = t_se.rearrange("(a p) c -> p a c", p=128)
    lt_hbm = t_lt.rearrange("(a p) c -> p a c", p=128)
    dt_hbm = t_dt.rearrange("(a p) c -> p a c", p=128)
    eb_hbm = t_eb.rearrange("(a p) c -> p a c", p=128)
    proc_hbm = t_proc.rearrange("(a p) c -> p a c", p=128)

    xrt = p.sbuf("xrt", (128, 16, D), BF16)
    rbc = p.sbuf("rbc", (128, T), F32)

    def bcast_row(src_row, dst_bc, ncols=T):
        with ExitStack() as ph:
            pbc = p.psum("bc_ps", (128, 512), F32, stack=ph)
            pe0 = b.pe.n
            dve0 = b.dve.n

            def pe(t):
                for f in range(ncols // 512):
                    if f >= 1:
                        b.dve.wait(t, dve0 + f)
                    b.pe.inc(t.matmul(pbc[:], b.ones_row_f[:],
                                      src_row[:, f * 512:(f + 1) * 512],
                                      start=True, stop=True))

            def dv(v):
                for f in range(ncols // 512):
                    b.pe.wait(v, pe0 + f + 1)
                    b.dve.inc(v.tensor_copy(
                        dst_bc[:, f * 512:(f + 1) * 512], pbc[:]))
            b.blk.tensor(pe)
            b.blk.vector(dv)
            b.barrier()

    phXP1 = ExitStack()
    xr_hi = p.sbuf("xr_hi", (128, 8, T), BF16, stack=phXP1)
    xr_lo = p.sbuf("xr_lo", (128, 8, T), BF16, stack=phXP1)
    with ExitStack() as ph0:
        gx = p.sbuf("gx", (128, 8), F32, stack=ph0)
        ptile = p.psum("ptileX", (128, 128), BF16, stack=ph0)
        b.load(gx[:], t_gx)
        b.barrier()
        ln_fm_stream(b, fm_ap(t_x), None, eps=0.0,
                     hilo_gamma=(xr_hi, xr_lo, gx))
        emit_transpose(b, lambda i, j: xr_hi[:, i, j * 128:(j + 1) * 128],
                       lambda j, i: xrt[:, j, i * 128:(i + 1) * 128],
                       8, 16, ident, ptile)

    with ExitStack() as p1:
        se = p.sbuf("se", (128, D), F32, stack=p1)
        sq_t = p.sbuf("sq_t", (128, D), F32, stack=p1)
        se_hi = p.sbuf("se_hi", (128, D), BF16, stack=p1)
        se_lo = p.sbuf("se_lo", (128, D), BF16, stack=p1)
        seT_hi = p.sbuf("seT_hi", (128, 8, 128), BF16, stack=p1)
        seT_lo = p.sbuf("seT_lo", (128, 8, 128), BF16, stack=p1)
        gsrow = p.sbuf("gsrow", (1, D), F32, stack=p1)
        gsbc = p.sbuf("gsbc", (128, D), F32, stack=p1)
        scol = p.sbuf("scol", (128, 1), F32, stack=p1)
        lt = p.sbuf("lt", (128, T), F32, stack=p1)
        dtile = p.sbuf("dtile", (128, T), BF16, stack=p1)
        dtt = p.sbuf("dtt", (128, 16, 128), BF16, stack=p1)
        nmax = p.sbuf("nmax", (128, 1), F32, stack=p1)
        rsum = p.sbuf("rsum", (128, 1), F32, stack=p1)
        plt = p.psum("plt", (128, 512), F32, stack=p1)
        ptile = p.psum("ptile1", (128, 128), BF16, stack=p1)

        b.load(gsrow[:], t_gs)
        b.barrier()
        bcast_row(gsrow, gsbc, D)

        for et in range(NET):
            b.load(se[:], se_hbm[:, et, :])
            b.barrier()

            def act3(a):
                b.act.inc(a.activation(sq_t[:], se[:], AF.Square))
            b.blk.scalar(act3)

            def dv2(v):
                b.act.wait(v)
                b.dve.inc(v.tensor_reduce(scol[:], sq_t[:], AX.X, ALU.add))
            b.blk.vector(dv2)

            def act4(a):
                b.dve.wait(a)
                a.activation(scol[:], scol[:], AF.Ln)
                b.act.inc(a.activation(scol[:], scol[:], AF.Exp, scale=-0.5))
            b.blk.scalar(act4)

            def dv3(v):
                b.act.wait(v)
                b.dve.inc(v.tensor_scalar_mul(scol[:], scol[:], 32.0))
                b.dve.wait(v)
                v.tensor_scalar(se[:], se[:], scol[:], None, ALU.mult)
                v.tensor_mul(se[:], se[:], gsbc[:])
                v.tensor_copy(se_hi[:], se[:])
                b.dve.inc(v.tensor_sub(se_lo[:], se[:], se_hi[:]), 4)
            b.blk.vector(dv3)
            b.barrier()
            emit_transpose(b, lambda i, j: se_hi[:, j * 128:(j + 1) * 128],
                           lambda j, i: seT_hi[:, j, :],
                           1, 8, ident, ptile)
            emit_transpose(b, lambda i, j: se_lo[:, j * 128:(j + 1) * 128],
                           lambda j, i: seT_lo[:, j, :],
                           1, 8, ident, ptile)

            pe3_0 = b.pe.n
            dve4_0 = b.dve.n

            def pe3(t, pe3_0=pe3_0, dve4_0=dve4_0):
                for f in range(NF):
                    sl = slice(f * 512, (f + 1) * 512)
                    if f >= 1:
                        b.dve.wait(t, dve4_0 + f)
                    mm = None
                    for kt in range(8):
                        t.matmul(plt[:], seT_hi[:, kt, :], xr_hi[:, kt, sl],
                                 start=(kt == 0), stop=False)
                        t.matmul(plt[:], seT_hi[:, kt, :], xr_lo[:, kt, sl],
                                 start=False, stop=False)
                        mm = t.matmul(plt[:], seT_lo[:, kt, :],
                                      xr_hi[:, kt, sl],
                                      start=False, stop=(kt == 7))
                    b.pe.inc(mm)
            b.blk.tensor(pe3)

            def dve4(v, pe3_0=pe3_0):
                for f in range(NF):
                    sl = slice(f * 512, (f + 1) * 512)
                    b.pe.wait(v, pe3_0 + f + 1)
                    b.dve.inc(v.tensor_copy(lt[:, sl], plt[:]))
            b.blk.vector(dve4)
            b.barrier()
            b.store(lt_hbm[:, et, :], lt[:])

            def dv5(v):
                b.dve.inc(v.tensor_reduce(nmax[:], lt[:], AX.X, ALU.max,
                                          negate=True))
            b.blk.vector(dv5)

            def act5(a):
                b.dve.wait(a)
                b.act.inc(a.activation(dtile[:], lt[:], AF.Exp,
                                       bias=nmax[:], accum_out=rsum[:]))
            b.blk.scalar(act5)

            def act6(a):
                a.activation(rsum[:], rsum[:], AF.Ln)
                b.act.inc(a.activation(rsum[:], rsum[:], AF.Exp, scale=-1.0))
            b.blk.scalar(act6)

            def dv6(v):
                b.act.wait(v)
                b.dve.inc(v.tensor_scalar(dtile[:], dtile[:], rsum[:],
                                          None, ALU.mult))
            b.blk.vector(dv6)
            b.barrier()
            emit_transpose(b, lambda i, j: dtile[:, j * 128:(j + 1) * 128],
                           lambda j, i: dtt[:, j, :],
                           1, 16, ident, ptile)
            b.store(dt_hbm[:, :, et * 128:(et + 1) * 128], dtt[:])
            b.barrier()

    phXP1.close()

    with ExitStack() as ph:
        brow = p.sbuf("brow", (1, T), F32, stack=ph)
        rrow = p.sbuf("rrow", (1, T), F32, stack=ph)
        bbc = p.sbuf("bbc", (128, T), F32, stack=ph)
        lt = p.sbuf("lt2", (128, T), F32, stack=ph)
        etile = p.sbuf("etile", (128, T), BF16, stack=ph)
        srow = p.sbuf("srow", (1, T), F32, stack=ph)
        onesb = p.sbuf("onesb2", (128, 1), BF16, stack=ph)
        pss = p.psum("pss2", (1, T), F32, stack=ph)

        def vinit(v):
            b.dve.inc(v.memset(onesb[:], 1.0))
        b.blk.vector(vinit)
        b.barrier()

        def lse_pass(scale, sub_bound, out_row, store_eb):
            for et in range(NET):
                b.load(lt[:], lt_hbm[:, et, :])
                b.barrier()
                if sub_bound:
                    def dvs(v):
                        b.dve.inc(v.tensor_sub(lt[:], lt[:], bbc[:]))
                    b.blk.vector(dvs)

                def acte(a, et=et):
                    b.dve.wait(a)
                    b.act.inc(a.activation(etile[:], lt[:], AF.Exp,
                                           scale=float(scale)))
                b.blk.scalar(acte)
                b.barrier()

                def pes(t, et=et):
                    mm = None
                    for f in range(NF):
                        sl = slice(f * 512, (f + 1) * 512)
                        mm = t.matmul(pss[0:1, sl], onesb[:], etile[:, sl],
                                      start=(et == 0), stop=(et == NET - 1))
                    b.pe.inc(mm)
                b.blk.tensor(pes)
                if store_eb:
                    b.store(eb_hbm[:, et, :], etile[:])
                b.barrier()

            def dvr(v):
                b.dve.inc(v.tensor_copy(out_row[:], pss[0:1, :]))
            b.blk.vector(dvr)
            b.barrier()

        lse_pass(1.0 / 64, False, srow, False)

        def actb1(a):
            b.act.inc(a.activation(brow[:], srow[:], AF.Ln, scale=1.0))
        b.blk.scalar(actb1)

        def dvb1(v):
            b.act.wait(v)
            b.dve.inc(v.tensor_scalar_mul(brow[:], brow[:], 64.0))
        b.blk.vector(dvb1)
        b.barrier()
        bcast_row(brow, bbc)

        lse_pass(1.0 / 8, True, srow, False)

        def actb2(a):
            b.act.inc(a.activation(srow[:], srow[:], AF.Ln))
        b.blk.scalar(actb2)

        def dvb2(v):
            b.act.wait(v)
            v.tensor_scalar_mul(srow[:], srow[:], 8.0)
            v.tensor_add(brow[:], brow[:], srow[:])
            b.dve.inc(v.tensor_scalar_add(brow[:], brow[:], -40.0))
        b.blk.vector(dvb2)
        b.barrier()
        bcast_row(brow, bbc)

        lse_pass(1.0, True, srow, True)

        def dvr2(v):
            b.dve.inc(v.reciprocal(rrow[:], srow[:]))
        b.blk.vector(dvr2)
        b.barrier()
        bcast_row(rrow, rbc)
        if debug:
            t_dbgr = p.dout("dbg_r", (4, T))
            b.store(t_dbgr[0:1, :], srow[:])
            b.store(t_dbgr[1:2, :], rrow[:])
            b.store(t_dbgr[2:3, :], brow[:])
            b.store(t_dbgr[3:4, :], rbc[0:1, :])
            b.barrier()

    with ExitStack() as ph:
        w1s = p.sbuf("w1s", (128, 8, FH), BF16, stack=ph)
        w2s = p.sbuf("w2s", (128, 32, D), BF16, stack=ph)
        b.load(w1s[:], fm_ap(t_w1), cast=True)
        b.load(w2s[:], fm_ap(t_w2), cast=True)
        ECH = 256
        dts = p.sbuf("dts", (128, 16, ECH), BF16, stack=ph)
        slT = p.sbuf("slT", (128, 8, ECH), BF16, stack=ph)
        hT = p.sbuf("hT", (128, 32, ECH), BF16, stack=ph)
        prc = p.sbuf("prc", (128, 2, D), BF16, stack=ph)
        pp = [p.psum("p2p", (128, 512), F32, stack=ph) for _ in range(2)]
        b.barrier()

        for ec in range(ET // ECH):
            esl = slice(ec * ECH, (ec + 1) * ECH)
            b.load(dts[:], dt_hbm[:, :, esl])
            b.barrier()

            pe0 = b.pe.n
            dve0 = b.dve.n

            def pe1(t, pe0=pe0, dve0=dve0):
                for dt_ in range(8):
                    ps = pp[dt_ % 2]
                    if dt_ >= 2:
                        b.dve.wait(t, dve0 + dt_ - 1)
                    mm = None
                    for st in range(16):
                        mm = t.matmul(ps[:, 0:256],
                                      xrt[:, st, dt_ * 128:(dt_ + 1) * 128],
                                      dts[:, st, :],
                                      start=(st == 0), stop=(st == 15))
                    b.pe.inc(mm)
            b.blk.tensor(pe1)

            def dv1(v, pe0=pe0):
                for dt_ in range(8):
                    b.pe.wait(v, pe0 + dt_ + 1)
                    b.dve.inc(v.tensor_copy(slT[:, dt_, :],
                                            pp[dt_ % 2][:, 0:256]))
            b.blk.vector(dv1)
            b.barrier()

            pe0 = b.pe.n
            act0 = b.act.n

            def pe2(t, pe0=pe0, act0=act0):
                for ht_ in range(32):
                    ps = pp[ht_ % 2]
                    if ht_ >= 2:
                        b.act.wait(t, act0 + ht_ - 1)
                    mm = None
                    for kt in range(8):
                        mm = t.matmul(ps[:, 0:256],
                                      w1s[:, kt, ht_ * 128:(ht_ + 1) * 128],
                                      slT[:, kt, :],
                                      start=(kt == 0), stop=(kt == 7))
                    b.pe.inc(mm)
            b.blk.tensor(pe2)

            def ac2(a, pe0=pe0):
                for ht_ in range(32):
                    b.pe.wait(a, pe0 + ht_ + 1)
                    b.act.inc(a.activation(hT[:, ht_, :],
                                           pp[ht_ % 2][:, 0:256], AF.Gelu))
            b.blk.scalar(ac2)
            b.barrier()

            pe0 = b.pe.n
            dve0 = b.dve.n
            order = [(es, f) for es in range(2) for f in range(2)]

            def pe3(t, pe0=pe0, dve0=dve0, order=order):
                for i, (es, f) in enumerate(order):
                    ps = pp[i % 2]
                    if i >= 2:
                        b.dve.wait(t, dve0 + i - 1)
                    mm = None
                    for kt in range(32):
                        mm = t.matmul(ps[:], hT[:, kt, es * 128:(es + 1) * 128],
                                      w2s[:, kt, f * 512:(f + 1) * 512],
                                      start=(kt == 0), stop=(kt == 31))
                    b.pe.inc(mm)
            b.blk.tensor(pe3)

            def dv3(v, pe0=pe0, order=order):
                for i, (es, f) in enumerate(order):
                    b.pe.wait(v, pe0 + i + 1)
                    b.dve.inc(v.tensor_copy(
                        prc[:, es, f * 512:(f + 1) * 512], pp[i % 2][:]))
            b.blk.vector(dv3)
            b.barrier()
            b.store(proc_hbm[:, ec * 2:(ec + 1) * 2, :], prc[:])
            b.barrier()

    t_dbgg = p.dout("dbg_g", (D, T), BF16) if debug else None
    with ExitStack() as ph:
        ebt = p.sbuf("ebt", (128, 2, T), BF16, stack=ph)
        pct = p.sbuf("pct", (128, 2, D), BF16, stack=ph)
        xch = p.sbuf("xch3", (128, 2, T), F32, stack=ph)
        sch = p.sbuf("sch3", (128, 2, T), BF16, stack=ph)
        och = p.sbuf("och3", (128, 2, T), F32, stack=ph)
        gch = p.sbuf("gch3", (128, T), F32, stack=ph)
        pout = p.psum("pout", (128, 2, T), F32, stack=ph)

        for rd in range(4):
            dpair = (2 * rd, 2 * rd + 1)
            ld0 = b.ld.n
            pe0 = b.pe.n

            def loads(sy, rd=rd, ld0=ld0, pe0=pe0):
                for et in range(NET):
                    if et >= 1:
                        b.pe.wait(sy, pe0 + et)
                    if KDEBUG_SERIAL_DMA:
                        b.ld.wait(sy)
                    b.ld.dma(sy.dma_start(out=ebt[:, et % 2, :],
                                          in_=eb_hbm[:, et, :]))
                    if KDEBUG_SERIAL_DMA:
                        b.ld.wait(sy)
                    b.ld.dma(sy.dma_start(out=pct[:, et % 2, :],
                                          in_=proc_hbm[:, et, :]))
            b.blk.sync(loads)

            def pe1(t, rd=rd, ld0=ld0, pe0=pe0):
                for et in range(NET):
                    b.ld.wait(t, ld0 + 32 * (et + 1))
                    mm = None
                    for dd in range(2):
                        for f in range(NF):
                            sl = slice(f * 512, (f + 1) * 512)
                            mm = t.matmul(
                                pout[:, dd, sl],
                                pct[:, et % 2,
                                    (2 * rd + dd) * 128:(2 * rd + dd + 1) * 128],
                                ebt[:, et % 2, sl],
                                start=(et == 0), stop=(et == NET - 1))
                    b.pe.inc(mm)
            b.blk.tensor(pe1)
            b.barrier()

            b.load(xch[:], fm_ap(t_x)[:, 2 * rd:2 * rd + 2, :])
            b.load(sch[:], fm_ap(t_s)[:, 2 * rd:2 * rd + 2, :])
            b.barrier()
            for dd in range(2):
                def dvc(v, dd=dd):
                    b.dve.inc(v.tensor_mul(gch[:], pout[:, dd, :], rbc[:]))
                b.blk.vector(dvc)

                def acc(a):
                    b.dve.wait(a)
                    b.act.inc(a.activation(gch[:], gch[:], AF.Silu))
                b.blk.scalar(acc)

                def dvc2(v, dd=dd):
                    b.act.wait(v)
                    v.tensor_mul(och[:, dd, :], sch[:, dd, :], gch[:])
                    b.dve.inc(v.tensor_add(och[:, dd, :], och[:, dd, :],
                                           xch[:, dd, :]), 2)
                b.blk.vector(dvc2)
                if debug:
                    b.store(fm_ap(t_dbgg)[:, 2 * rd + dd, :], gch[:], cast=True)
                b.barrier()
            b.store(fm_ap(t_out)[:, 2 * rd:2 * rd + 2, :], och[:])
            b.barrier()
    b.finish()
    return b


def build_l3_mixer():
    H, DH, L = 16, 64, 64
    DHE = 32
    SCALE = DH ** -0.5
    b = B("l3mix")
    p = b.p

    t_x = p.din("xT", (D, T))
    t_wqd = p.din("wq_down", (D, D))
    t_wkd = p.din("wk_down", (D, D))
    t_wvd = p.din("wv_down", (D, D))
    t_wquc = p.din("wq_up_c", (L, DHE))
    t_wque = p.din("wq_up_e", (L, DHE))
    t_wkuc = p.din("wk_up_c", (L, DHE))
    t_wkue = p.din("wk_up_e", (L, DHE))
    t_wvu = p.din("wv_up", (L, DH))
    t_wc = p.din("wc", (D, D))
    t_xn = p.dout("xnT", (D, T), BF16)
    t_s = p.dout("sT", (D, T), BF16)

    inv_freq = 1.0 / 10000.0 ** (np.arange(0, DHE, 2, dtype=np.float64) / DHE)
    freqs = np.arange(T, dtype=np.float64)[:, None] * inv_freq[None, :]
    emb = np.concatenate([freqs, freqs], axis=-1)
    cosT = np.cos(emb).T.astype(np.float32)
    sinT = np.sin(emb).T.astype(np.float32)
    M = np.zeros((DHE, DHE), np.float32)
    for i in range(DHE // 2):
        M[2 * i, 2 * i + 1] = -1.0
        M[2 * i + 1, 2 * i] = 1.0
    masks = []
    for dlt in range(4):
        mk = (np.arange(512)[None, :] >= (dlt * 128 + np.arange(128))[:, None])
        masks.append(mk.astype(np.float32))

    b.init_consts()
    cosb = b.const("c_cosT", cosT, sbuf_dt=BF16)
    sinb = b.const("c_sinT", sinT, sbuf_dt=BF16)
    rotm = b.const("c_rotm", M.T.copy(), sbuf_dt=BF16)
    cmask = [b.const(f"c_cmask{i}", masks[i], sbuf_dt=BF16) for i in range(4)]

    xn = p.sbuf("xn", (128, 8, T), BF16)

    ln_fm_stream(b, fm_ap(t_x), xn)
    b.store(fm_ap(t_xn), xn[:])


    qh = p.sbuf("qh", (128, 8, T), BF16)
    kh = p.sbuf("kh", (128, 8, T), BF16)
    vt = p.sbuf("vt", (128, 16, H * 65), BF16)
    vt_h = vt[:].rearrange("p a (h c) -> p a h c", h=H)

    def cinit2(v):
        b.dve.inc(v.memset(vt_h[:, :, :, 64:65], 1.0))
    b.blk.vector(cinit2)
    b.barrier()

    def lat_matmul(wT_t, dst):
        with ExitStack() as ph:
            w = p.sbuf("latw", (128, 8, D), BF16, stack=ph)
            b.load(w[:], fm_ap(wT_t), cast=True)
            b.barrier()

            def ev(eng, rt, ft, ps):
                return eng.tensor_copy(dst[:, rt, ft * 512:(ft + 1) * 512],
                                       ps[:])
            emit_mm_tiled(b, None, ev,
                          lambda kt, rt: w[:, kt, rt * 128:(rt + 1) * 128],
                          lambda kt, ft: xn[:, kt, ft * 512:(ft + 1) * 512],
                          8, D, T)

    for which, (t_down, dst, t_uc, t_ue) in enumerate(
            ((t_wqd, qh, t_wquc, t_wque), (t_wkd, kh, t_wkuc, t_wkue))):
        phA = ExitStack()
        lat = p.sbuf("lat", (128, 8, T), BF16, stack=phA)
        lat_matmul(t_down, lat)

        upw = p.sbuf("upw", (128, 2 * DHE), BF16, stack=phA)
        esb = p.sbuf("esb", (32, T), BF16, stack=phA)
        rsb = p.sbuf("rsb", (32, T), BF16, stack=phA)
        rop = p.sbuf("rop", (32, T), BF16, stack=phA)
        pup = p.psum("pup", (32, 512), F32, stack=phA)
        prt = p.psum("prt", (32, 512), F32, stack=phA)
        if True:
            b.load(upw[0:64, 0:DHE], t_uc, cast=True)
            b.load(upw[64:128, 0:DHE], t_uc, cast=True)
            b.load(upw[0:64, DHE:2 * DHE], t_ue, cast=True)
            b.load(upw[64:128, DHE:2 * DHE], t_ue, cast=True)
            b.barrier()
            for h in range(H):
                jh, sh = h // 2, h % 2
                base = 64 * sh
                lath = lat[base:base + 64, jh, :]
                pe0 = b.pe.n
                dve0 = b.dve.n

                def pec(t, lath=lath, base=base, pe0=pe0, dve0=dve0):
                    for f in range(NF):
                        if f >= 1:
                            b.dve.wait(t, dve0 + f)
                        b.pe.inc(t.matmul(
                            pup[:], upw[base:base + 64, 0:DHE],
                            lath[:, f * 512:(f + 1) * 512],
                            start=True, stop=True))
                b.blk.tensor(pec)

                def dvc(v, dst=dst, base=base, jh=jh, pe0=pe0):
                    for f in range(NF):
                        sl = slice(f * 512, (f + 1) * 512)
                        b.pe.wait(v, pe0 + f + 1)
                        b.dve.inc(v.tensor_copy(dst[base:base + 32, jh, sl],
                                                pup[:]))
                b.blk.vector(dvc)
                b.barrier()
                pe0 = b.pe.n
                dve0 = b.dve.n

                def pee(t, lath=lath, base=base, pe0=pe0, dve0=dve0):
                    for f in range(NF):
                        sl = slice(f * 512, (f + 1) * 512)
                        if f >= 1:
                            b.dve.wait(t, dve0 + 4 * f - 3)
                        b.pe.inc(t.matmul(pup[:],
                                          upw[base:base + 64, DHE:2 * DHE],
                                          lath[:, sl], start=True, stop=True))
                        b.dve.wait(t, dve0 + 4 * f + 1)
                        b.pe.inc(t.matmul(prt[:], rotm[:], esb[:, sl],
                                          start=True, stop=True))

                def dve_e(v, pe0=pe0):
                    for f in range(NF):
                        sl = slice(f * 512, (f + 1) * 512)
                        b.pe.wait(v, pe0 + 2 * f + 1)
                        b.dve.inc(v.tensor_copy(esb[:, sl], pup[:]))
                        b.pe.wait(v, pe0 + 2 * f + 2)
                        v.tensor_mul(rsb[:, sl], esb[:, sl], cosb[:, sl])
                        v.tensor_mul(rop[:, sl], prt[:], sinb[:, sl])
                        b.dve.inc(v.tensor_add(rsb[:, sl], rsb[:, sl],
                                               rop[:, sl]), 3)
                b.blk.tensor(pee)
                b.blk.vector(dve_e)
                b.barrier()
                if base + 32 == 32:
                    def dvp(v, dst=dst, jh=jh):
                        b.dve.inc(v.tensor_copy(dst[32:64, jh, :], rsb[:]))
                    b.blk.vector(dvp)
                else:
                    b.store(dst[96:128, jh, :], rsb[:])
                b.barrier()
        phA.close()

    with ExitStack() as phB:
        vlat = p.sbuf("vlat", (128, 8, T), BF16, stack=phB)
        lat_matmul(t_wvd, vlat)
        vupw = p.sbuf("vupw", (128, DH), BF16, stack=phB)
        pv = p.psum("pv", (128, 64), F32, stack=phB)
        b.load(vupw[0:64, :], t_wvu, cast=True)
        b.load(vupw[64:128, :], t_wvu, cast=True)
        b.barrier()
        for h in range(H):
            jh, sh = h // 2, h % 2
            base = 64 * sh
            pe0 = b.pe.n
            dve0 = b.dve.n

            def pev(t, h=h, jh=jh, base=base, pe0=pe0, dve0=dve0):
                for tt in range(16):
                    if tt >= 1:
                        b.dve.wait(t, dve0 + tt)
                    b.pe.inc(t.matmul(
                        pv[:], vlat[base:base + 64, jh,
                                    tt * 128:(tt + 1) * 128],
                        vupw[base:base + 64, :], start=True, stop=True))
            b.blk.tensor(pev)

            def dvv(v, h=h, pe0=pe0):
                for tt in range(16):
                    b.pe.wait(v, pe0 + tt + 1)
                    b.dve.inc(v.tensor_copy(vt_h[:, tt, h, 0:64], pv[:]))
            b.blk.vector(dvv)
            b.barrier()

    yn = p.sbuf("yn", (128, 8, T), BF16)
    with ExitStack() as phC:
        pS = [p.psum("l3_ps", (128, 512), F32, stack=phC) for _ in range(2)]
        pY = p.psum("l3_py", (65, T), F32, stack=phC)
        es = [p.sbuf("l3_es", (128, 512), BF16, stack=phC) for _ in range(2)]
        rrow = p.sbuf("l3_rrow", (1, T), F32, stack=phC)
        rbch = p.sbuf("l3_rbc", (64, 512), F32, stack=phC)
        pR = p.psum("l3_pr", (64, 512), F32, stack=phC)

        for h in range(H):
            jh, sh = h // 2, h % 2
            base = 64 * sh
            it = [(tk, ff) for ff in range(4) for tk in range(4 * (ff + 1))]
            pe0, act0, dve0 = b.pe.n, b.act.n, b.dve.n

            def pe_a(t, jh=jh, base=base, it=it, pe0=pe0, act0=act0,
                     dve0=dve0, h=h):
                def y_mm(pi):
                    ptk, pff = it[pi]
                    diag = ptk >= 4 * pff
                    if diag:
                        b.dve.wait(t, dve_counts[pi])
                    else:
                        b.act.wait(t, act_counts[pi])
                    b.pe.inc(t.matmul(
                        pY[:, pff * 512:(pff + 1) * 512],
                        vt_h[:, ptk, h, :], es[pi % 2][:],
                        start=(ptk == 0), stop=(ptk == 4 * (pff + 1) - 1)))
                for i, (tk, ff) in enumerate(it):
                    ps = pS[i % 2]
                    if i >= 2:
                        pi2 = i - 2
                        if it[pi2][0] >= 4 * it[pi2][1]:
                            b.dve.wait(t, dve_counts[pi2])
                        else:
                            b.act.wait(t, act_counts[pi2])
                    b.pe.inc(t.matmul(ps[:], kh[base:base + 64, jh,
                                               tk * 128:(tk + 1) * 128],
                                      qh[base:base + 64, jh,
                                         ff * 512:(ff + 1) * 512],
                                      start=True, stop=True))
                    if i >= 1:
                        y_mm(i - 1)
                y_mm(len(it) - 1)

            act_counts = {}
            dve_counts = {}
            na, nd = act0, dve0
            for i, (tk, ff) in enumerate(it):
                na += 1
                act_counts[i] = na
                if tk >= 4 * ff:
                    nd += 1
                    dve_counts[i] = nd

            def act_a(a, it=it, pe0=pe0):
                for i, (tk, ff) in enumerate(it):
                    cnt = pe0 + (2 * i if i >= 1 else 1)
                    b.pe.wait(a, cnt)
                    b.act.inc(a.activation(es[i % 2][:], pS[i % 2][:], AF.Exp,
                                           scale=float(SCALE)))

            def dve_a(v, it=it):
                for i, (tk, ff) in enumerate(it):
                    if tk >= 4 * ff:
                        dlt = tk - 4 * ff
                        b.act.wait(v, act_counts[i])
                        b.dve.inc(v.tensor_mul(es[i % 2][:], es[i % 2][:],
                                               cmask[dlt][:]))
            b.blk.tensor(pe_a)
            b.blk.scalar(act_a)
            b.blk.vector(dve_a)
            b.barrier()

            def dvr(v):
                b.dve.inc(v.reciprocal(rrow[:], pY[64:65, :]))
            b.blk.vector(dvr)
            pe1_0 = b.pe.n
            dve1_0 = b.dve.n

            def pe_n(t, pe1_0=pe1_0, dve1_0=dve1_0):
                b.dve.wait(t, dve1_0)
                for f in range(NF):
                    if f >= 1:
                        b.dve.wait(t, dve1_0 + 2 * f)
                    b.pe.inc(t.matmul(pR[:], b.ones_row_f[0:1, 0:64],
                                      rrow[0:1, f * 512:(f + 1) * 512],
                                      start=True, stop=True))

            def dve_n(v, jh=jh, base=base, pe1_0=pe1_0):
                for f in range(NF):
                    sl = slice(f * 512, (f + 1) * 512)
                    b.pe.wait(v, pe1_0 + f + 1)
                    v.tensor_copy(rbch[:], pR[:])
                    b.dve.inc(v.tensor_mul(yn[base:base + 64, jh, sl],
                                           pY[0:64, sl], rbch[:]), 2)
            b.blk.tensor(pe_n)
            b.blk.vector(dve_n)
            b.barrier()

    with ExitStack() as phD:
        wc = p.sbuf("wc_s", (128, 8, D), BF16, stack=phD)
        b.load(wc[:], fm_ap(t_wc), cast=True)
        b.barrier()
        sfm = xn

        def pev(eng, rt, ft, ps):
            return eng.tensor_copy(sfm[:, rt, ft * 512:(ft + 1) * 512], ps[:])
        emit_mm_tiled(b, None, pev,
                      lambda kt, rt: wc[:, kt, rt * 128:(rt + 1) * 128],
                      lambda kt, ft: yn[:, kt, ft * 512:(ft + 1) * 512],
                      8, D, T)
        b.store(fm_ap(t_s), sfm[:])
    b.finish()
    return b


def build_l3_ffn():
    NP = 2048
    b = B("l3ffn")
    p = b.p

    t_x = p.din("xT", (D, T))
    t_s = p.din("sT", (D, T), BF16)
    t_phi = p.din("phi", (D, NP))
    t_w = p.din("We", (64, D, D))
    t_eb = p.din("eb", (64, D))
    t_out = p.dout("xoutT", (D, T))

    cc_in = p.dtmp("cc_in", (T, NP))
    cc_out = p.dtmp("cc_out", (T, NP))
    t_yb = p.dtmp("l3f_y", (NP, D), BF16)
    t_ce = p.dtmp("l3f_ce", (NP, T), BF16)

    b.init_consts()

    ident = b.const("c_ident3", np.eye(128, dtype=np.float32), sbuf_dt=BF16)
    xnh = p.sbuf("xnh", (128, 8, T), BF16)
    onescol = p.sbuf("onescol", (128, 8), F32)
    rrow = p.sbuf("l3f_rrow", (1, T), F32)
    rbc = p.sbuf("l3f_rbc", (128, T), F32)
    phEarly = ExitStack()
    xnl = p.sbuf("xnl", (128, 8, T), BF16, stack=phEarly)

    def vinit(v):
        b.dve.inc(v.memset(onescol[:], 1.0))
    b.blk.vector(vinit)

    ln_fm_stream(b, fm_ap(t_x), None, hilo_gamma=(xnh, xnl, onescol))

    phih = p.sbuf("phih", (128, 8, NP), BF16, stack=phEarly)
    phil = p.sbuf("phil", (128, 8, NP), BF16, stack=phEarly)
    with ExitStack() as ph:
        phf = p.sbuf("phf", (128, NP), F32, stack=ph)
        for j in range(8):
            b.load(phf[:], fm_ap(t_phi)[:, j, :])
            b.barrier()

            def dv(v, j=j):
                v.tensor_copy(phih[:, j, :], phf[:])
                b.dve.inc(v.tensor_sub(phil[:, j, :], phf[:], phih[:, j, :]),
                          2)
            b.blk.vector(dv)
            b.barrier()

    def bcast_row(src_row, dst_bc, ncols=T):
        with ExitStack() as ph:
            pbc = p.psum("bc_ps", (128, 512), F32, stack=ph)
            pe0 = b.pe.n
            dve0 = b.dve.n

            def pe(t):
                for f in range(ncols // 512):
                    if f >= 1:
                        b.dve.wait(t, dve0 + f)
                    b.pe.inc(t.matmul(pbc[:], b.ones_row_f[:],
                                      src_row[:, f * 512:(f + 1) * 512],
                                      start=True, stop=True))

            def dv(v):
                for f in range(ncols // 512):
                    b.pe.wait(v, pe0 + f + 1)
                    b.dve.inc(v.tensor_copy(
                        dst_bc[:, f * 512:(f + 1) * 512], pbc[:]))
            b.blk.tensor(pe)
            b.blk.vector(dv)
            b.barrier()

    with ExitStack() as ph:
        etile = p.sbuf("etile", (128, NP), F32, stack=ph)
        plt = p.psum("plt", (128, 512), F32, stack=ph)
        for mt in range(16):
            pe0 = b.pe.n
            act0 = b.act.n

            def pe1(t, mt=mt, pe0=pe0, act0=act0):
                for f in range(NP // 512):
                    sl = slice(f * 512, (f + 1) * 512)
                    if f >= 1:
                        b.act.wait(t, act0 + f)
                    mm = None
                    for kt in range(8):
                        t.matmul(plt[:], xnh[:, kt, mt * 128:(mt + 1) * 128],
                                 phih[:, kt, sl], start=(kt == 0), stop=False)
                        t.matmul(plt[:], xnh[:, kt, mt * 128:(mt + 1) * 128],
                                 phil[:, kt, sl], start=False, stop=False)
                        mm = t.matmul(plt[:],
                                      xnl[:, kt, mt * 128:(mt + 1) * 128],
                                      phih[:, kt, sl], start=False,
                                      stop=(kt == 7))
                    b.pe.inc(mm)
            b.blk.tensor(pe1)

            def act1(a, mt=mt, pe0=pe0):
                for f in range(NP // 512):
                    sl = slice(f * 512, (f + 1) * 512)
                    b.pe.wait(a, pe0 + f + 1)
                    b.act.inc(a.activation(etile[:, sl], plt[:], AF.Exp))
            b.blk.scalar(act1)
            b.barrier()
            b.store(cc_in.rearrange("(a p) c -> p a c", p=128)[:, mt, :],
                    etile[:])
            b.barrier()

    ccs = p.sem("cc")

    def gcc(g):
        g.collective_compute(
            "AllReduce", ALU.add,
            replica_groups=[list(range(NCORES))],
            ins=[cc_in.opt()], outs=[cc_out.opt()],
        ).then_inc(ccs.h, 1)
        ccs.n += 1
    b.blk.gpsimd(gcc)

    def wcc(eng):
        ccs.wait(eng)
    for sec in (b.blk.sync, b.blk.tensor, b.blk.vector, b.blk.scalar,
                b.blk.gpsimd):
        sec(wcc)

    with ExitStack() as ph:
        ce = p.sbuf("ce", (128, T), BF16, stack=ph)
        cf = p.sbuf("cf", (128, T), F32, stack=ph)
        onesb = p.sbuf("onesb3", (128, 1), BF16, stack=ph)
        plt = p.psum("plt2", (128, 512), F32, stack=ph)
        pcs = [p.psum(f"pcs{i}", (1, 512), F32, stack=ph) for i in range(4)]

        def vinit2(v):
            b.dve.inc(v.memset(onesb[:], 1.0))
        b.blk.vector(vinit2)
        b.barrier()
        ce_hbm = t_ce.rearrange("(a p) c -> p a c", p=128)
        for nt_ in range(16):
            pe0 = b.pe.n
            dve0 = b.dve.n

            def pe1(t, nt_=nt_, pe0=pe0, dve0=dve0):
                for f in range(NF):
                    sl = slice(f * 512, (f + 1) * 512)
                    if f >= 1:
                        b.dve.wait(t, dve0 + f)
                    mm = None
                    for kt in range(8):
                        t.matmul(plt[:], phih[:, kt, nt_ * 128:(nt_ + 1) * 128],
                                 xnh[:, kt, sl], start=(kt == 0), stop=False)
                        t.matmul(plt[:], phih[:, kt, nt_ * 128:(nt_ + 1) * 128],
                                 xnl[:, kt, sl], start=False, stop=False)
                        mm = t.matmul(plt[:],
                                      phil[:, kt, nt_ * 128:(nt_ + 1) * 128],
                                      xnh[:, kt, sl], start=False,
                                      stop=(kt == 7))
                    b.pe.inc(mm)
            b.blk.tensor(pe1)

            def dv1(v, pe0=pe0):
                for f in range(NF):
                    sl = slice(f * 512, (f + 1) * 512)
                    b.pe.wait(v, pe0 + f + 1)
                    b.dve.inc(v.tensor_scalar_add(cf[:, sl], plt[:], -45.0))
            b.blk.vector(dv1)

            def act1(a):
                b.dve.wait(a)
                b.act.inc(a.activation(ce[:], cf[:], AF.Exp))
            b.blk.scalar(act1)
            b.barrier()

            def pe2(t, nt_=nt_):
                mm = None
                for f in range(NF):
                    sl = slice(f * 512, (f + 1) * 512)
                    mm = t.matmul(pcs[f][:], onesb[:], ce[:, sl],
                                  start=(nt_ == 0), stop=(nt_ == 15))
                b.pe.inc(mm)
            b.blk.tensor(pe2)
            b.store(ce_hbm[:, nt_, :], ce[:])
            b.barrier()

        def dvc(v):
            for f in range(NF):
                b.dve.inc(v.tensor_copy(rrow[:, f * 512:(f + 1) * 512],
                                        pcs[f][:]))
            b.dve.wait(v)
            b.dve.inc(v.reciprocal(rrow[:], rrow[:]))
        b.blk.vector(dvc)
        b.barrier()
        bcast_row(rrow, rbc)

    phEarly.close()

    with ExitStack() as ph:
        ech = p.sbuf("ech", (128, 16, 512), F32, stack=ph)
        dch = p.sbuf("dch", (128, 16, 512), BF16, stack=ph)
        xtm = p.sbuf("xtm", (128, 16, D), BF16, stack=ph)
        ptile = p.psum("ptile3", (128, 128), BF16, stack=ph)
        b.barrier()
        emit_transpose(b, lambda i, j: xnh[:, i, j * 128:(j + 1) * 128],
                       lambda j, i: xtm[:, j, i * 128:(i + 1) * 128],
                       8, 16, ident, ptile)

        slT = p.sbuf("slT", (128, 8, 512), BF16, stack=ph)
        wexp = p.sbuf("wexp", (128, 2, 8, D), BF16, stack=ph)
        ebrow = p.sbuf("ebrow", (1, D), BF16, stack=ph)
        ysb = p.sbuf("ysb", (32, D), BF16, stack=ph)
        psl = [p.psum("psl", (128, 512), F32, stack=ph) for _ in range(2)]
        pye = p.psum("pye", (32, 512), F32, stack=ph)

        cc_out_r = cc_out.rearrange("(a p) c -> p a c", p=128)
        cc_in_r = cc_in.rearrange("(a p) c -> p a c", p=128)
        for ec in range(4):
            esl = slice(ec * 512, (ec + 1) * 512)
            for mt in range(16):
                b.load(ech[:, mt, :], cc_out_r[:, mt, esl])
            b.barrier()

            def dvr(v):
                for mt in range(16):
                    b.dve.inc(v.nop())
            def actr(a):
                for mt in range(16):
                    a.activation(ech[:, mt, :], ech[:, mt, :], AF.Ln)
                    b.act.inc(a.activation(ech[:, mt, :], ech[:, mt, :],
                                           AF.Exp, scale=-1.0))
            b.blk.scalar(actr)
            b.barrier()
            def lds(g):
                for mt in range(16):
                    if KDEBUG_SERIAL_DMA:
                        b.gld.wait(g)
                    b.gld.dma(g.dma_start(out=dch[:, mt, :],
                                          in_=cc_in_r[:, mt, esl]))
            b.blk.gpsimd(lds)
            b.barrier()

            def dvd(v):
                for mt in range(16):
                    b.dve.inc(v.tensor_mul(dch[:, mt, :], dch[:, mt, :],
                                           ech[:, mt, :]))
            b.blk.vector(dvd)
            b.barrier()

            pe0 = b.pe.n
            dve0 = b.dve.n

            def pesl(t, pe0=pe0, dve0=dve0):
                for dt_ in range(8):
                    ps = psl[dt_ % 2]
                    if dt_ >= 2:
                        b.dve.wait(t, dve0 + dt_ - 1)
                    mm = None
                    for mt in range(16):
                        mm = t.matmul(ps[:], xtm[:, mt, dt_ * 128:(dt_ + 1) * 128],
                                      dch[:, mt, :],
                                      start=(mt == 0), stop=(mt == 15))
                    b.pe.inc(mm)
            b.blk.tensor(pesl)

            def dvsl(v, pe0=pe0):
                for dt_ in range(8):
                    b.pe.wait(v, pe0 + dt_ + 1)
                    b.dve.inc(v.tensor_copy(slT[:, dt_, :], psl[dt_ % 2][:]))
            b.blk.vector(dvsl)
            b.barrier()

            for ei in range(16):
                n = ec * 16 + ei
                buf = ei % 2
                b.load(wexp[:, buf, :, :], fm_ap(t_w[n]), cast=True)
                b.load(ebrow[:], t_eb[n:n + 1, :], cast=True)
                b.barrier()
                pe0 = b.pe.n
                dve0 = b.dve.n

                def pey(t, n=n, ei=ei, buf=buf, pe0=pe0, dve0=dve0):
                    for f in range(2):
                        sl = slice(f * 512, (f + 1) * 512)
                        if f >= 1:
                            b.dve.wait(t, dve0 + f)
                        mm = None
                        for kt in range(8):
                            mm = t.matmul(pye[:],
                                          slT[:, kt, ei * 32:(ei + 1) * 32],
                                          wexp[:, buf, kt, sl],
                                          start=(kt == 0), stop=False)
                        mm = t.matmul(pye[:], b.ones_row_b[0:1, 0:32],
                                      ebrow[:, sl], start=False, stop=True)
                        b.pe.inc(mm)
                b.blk.tensor(pey)

                def dvy(v, pe0=pe0):
                    for f in range(2):
                        sl = slice(f * 512, (f + 1) * 512)
                        b.pe.wait(v, pe0 + f + 1)
                        b.dve.inc(v.tensor_copy(ysb[:, sl], pye[:]))
                b.blk.vector(dvy)
                b.barrier()
                b.store(t_yb[n * 32:(n + 1) * 32, :], ysb[:])
                b.barrier()

    with ExitStack() as ph:
        ebt = p.sbuf("ebt3", (128, 2, T), BF16, stack=ph)
        pct = p.sbuf("pct3", (128, 2, D), BF16, stack=ph)
        xch = p.sbuf("xch4", (128, 2, T), F32, stack=ph)
        sch = p.sbuf("sch4", (128, 2, T), BF16, stack=ph)
        och = p.sbuf("och4", (128, 2, T), F32, stack=ph)
        gch = p.sbuf("gch4", (128, T), F32, stack=ph)
        pout = p.psum("pout3", (128, 2, T), F32, stack=ph)
        yb_hbm = t_yb.rearrange("(a p) c -> p a c", p=128)
        for rd in range(4):
            ld0 = b.ld.n
            pe0 = b.pe.n

            def loads(sy, rd=rd, ld0=ld0, pe0=pe0):
                for et in range(16):
                    if et >= 1:
                        b.pe.wait(sy, pe0 + et)
                    if KDEBUG_SERIAL_DMA:
                        b.ld.wait(sy)
                    b.ld.dma(sy.dma_start(out=ebt[:, et % 2, :],
                                          in_=ce_hbm[:, et, :]))
                    if KDEBUG_SERIAL_DMA:
                        b.ld.wait(sy)
                    b.ld.dma(sy.dma_start(out=pct[:, et % 2, :],
                                          in_=yb_hbm[:, et, :]))
            b.blk.sync(loads)

            def pe1(t, rd=rd, ld0=ld0, pe0=pe0):
                for et in range(16):
                    b.ld.wait(t, ld0 + 32 * (et + 1))
                    mm = None
                    for dd in range(2):
                        for f in range(NF):
                            sl = slice(f * 512, (f + 1) * 512)
                            mm = t.matmul(
                                pout[:, dd, sl],
                                pct[:, et % 2,
                                    (2 * rd + dd) * 128:(2 * rd + dd + 1) * 128],
                                ebt[:, et % 2, sl],
                                start=(et == 0), stop=(et == 15))
                    b.pe.inc(mm)
            b.blk.tensor(pe1)
            b.barrier()

            b.load(xch[:], fm_ap(t_x)[:, 2 * rd:2 * rd + 2, :])
            b.load(sch[:], fm_ap(t_s)[:, 2 * rd:2 * rd + 2, :])
            b.barrier()
            for dd in range(2):
                def dvc2(v, dd=dd):
                    b.dve.inc(v.tensor_mul(gch[:], pout[:, dd, :], rbc[:]))
                b.blk.vector(dvc2)

                def acc(a):
                    b.dve.wait(a)
                    b.act.inc(a.activation(gch[:], gch[:], AF.Silu))
                b.blk.scalar(acc)

                def dvc3(v, dd=dd):
                    b.act.wait(v)
                    v.tensor_mul(och[:, dd, :], sch[:, dd, :], gch[:])
                    b.dve.inc(v.tensor_add(och[:, dd, :], och[:, dd, :],
                                           xch[:, dd, :]), 2)
                b.blk.vector(dvc3)
                b.barrier()
            b.store(fm_ap(t_out)[:, 2 * rd:2 * rd + 2, :], och[:])
            b.barrier()
    b.finish()
    return b


F16 = mybir.dt.float16


def build_l0_mixer(debug=False):
    DI = 2048
    NS = 16
    DS = 64
    b = B("l0mix")
    p = b.p

    t_x = p.din("xT", (D, T))
    t_win = p.din("w_in", (D, 2 * DI))
    t_cw = p.din("conv_w", (128, 64))
    t_cb = p.din("conv_b", (128, NS))
    t_wx = p.din("w_x", (DI, 132))
    t_alog = p.din("a_log", (DI, DS))
    t_dtb = p.din("dt_bias", (128, NS))
    t_dco = p.din("dcoef", (128, NS))
    t_wout = p.din("w_out", (DI, D))
    t_xn = p.dout("xnT", (D, T), BF16)
    t_s = p.dout("sT", (D, T), BF16)
    t_y = (p.dout if debug else p.dtmp)("l0_y", (DI, T), BF16)
    t_dbg = p.dout("l0_dbg", (130, T)) if debug else None

    b.init_consts()
    ident = b.const("c_ident0", np.eye(128, dtype=np.float32), sbuf_dt=BF16)
    identf = b.const("c_identf", np.eye(128, dtype=np.float32))
    np_seln = np.zeros((64, 64, 128), np.float32)
    for _n in range(64):
        np_seln[_n, _n, :] = 1.0
    seln = b.const("c_seln", np_seln, sbuf_dt=BF16)

    xn = p.sbuf("xn", (128, 8, T), BF16)
    ln_fm_stream(b, fm_ap(t_x), xn)
    b.store(fm_ap(t_xn), xn[:])


    TP = T + 3
    cwcol = p.sbuf("cwcol", (128, 64), F32)
    cbcol = p.sbuf("cbcol", (128, NS), F32)
    dtbcol = p.sbuf("dtbcol", (128, NS), F32)
    dcocol = p.sbuf("dcocol", (128, NS), F32)
    scanScope = ExitStack()
    xc = p.sbuf("xc", (128, NS, TP), BF16, stack=scanScope)
    acol = p.sbuf("acol", (128, NS, DS), F32, stack=scanScope)
    srow = p.sbuf("srow", (1, T), F32, stack=scanScope)
    btr = p.sbuf("btr", (64, T), BF16, stack=scanScope)
    ctr = p.sbuf("ctr", (64, T), BF16, stack=scanScope)

    b.load(acol[:], t_alog.rearrange("(a p) c -> p a c", p=128))
    b.load(cwcol[:], t_cw)
    b.load(cbcol[:], t_cb)
    b.load(dtbcol[:], t_dtb)
    b.load(dcocol[:], t_dco)
    b.barrier()

    def acta(a):
        b.act.inc(a.activation(acol[:].rearrange("p a c -> p (a c)"),
                               acol[:].rearrange("p a c -> p (a c)"), AF.Exp))
    b.blk.scalar(acta)

    def dvea(v):
        b.act.wait(v)
        b.dve.inc(v.tensor_scalar_mul(acol[:].rearrange("p a c -> p (a c)"),
                                      acol[:].rearrange("p a c -> p (a c)"),
                                      -1.0))
    b.blk.vector(dvea)
    b.barrier()

    with ExitStack() as ph:
        win = p.sbuf("win", (128, 8, DI), BF16, stack=ph)
        xs = p.sbuf("xs", (128, 8, TP), BF16, stack=ph)
        b.load(win[:], fm_ap(t_win)[:, :, 0:DI], cast=True)
        b.barrier()

        def zinit(v):
            v.memset(xs[:, :, 0:3], 0.0)
            b.dve.inc(v.memset(xc[:, :, 0:3], 0.0))
        b.blk.vector(zinit)

        cacc = p.sbuf("cacc", (128, T), F32, stack=ph)
        for hg in range(2):
            def xsev(eng, rt, ft, ps, hg=hg):
                return eng.tensor_copy(
                    xs[:, rt, 3 + ft * 512:3 + (ft + 1) * 512], ps[:])
            emit_mm_tiled(b, None, xsev,
                          lambda kt, rt, hg=hg: win[:, kt, hg * 1024 +
                                                    rt * 128:(hg * 1024 +
                                                              (rt + 1) * 128)],
                          lambda kt, ft: xn[:, kt, ft * 512:(ft + 1) * 512],
                          8, 1024, T)

            for st8 in range(8):
                st = hg * 8 + st8
                def dvc(v, st=st, st8=st8):
                    v.tensor_scalar(cacc[:], xs[:, st8, 0:T],
                                    cwcol[:, 4 * st:4 * st + 1], None,
                                    ALU.mult)
                    ins = None
                    for j in range(1, 4):
                        ins = v.scalar_tensor_tensor(
                            cacc[:], xs[:, st8, j:j + T],
                            cwcol[:, 4 * st + j:4 * st + j + 1], cacc[:],
                            ALU.mult, ALU.add)
                    b.dve.inc(ins)
                b.blk.vector(dvc)

                def acc_(a, st=st):
                    b.dve.wait(a)
                    b.act.inc(a.activation(xc[:, st, 3:TP], cacc[:], AF.Silu,
                                           bias=cbcol[:, st:st + 1]))
                b.blk.scalar(acc_)
                b.barrier()

    with ExitStack() as ph:
        wx = p.sbuf("wx", (128, NS, 132), BF16, stack=ph)
        sst = p.sbuf("sst", (128, 132), F32, stack=ph)
        pss = p.psum("pssm", (128, 132), F32, stack=ph)
        ptile = p.psum("ptile0", (128, 128), BF16, stack=ph)
        psr = p.psum("psr", (128, 128), BF16, stack=ph)
        b.load(wx[:], t_wx.rearrange("(a p) c -> p a c", p=128), cast=True)
        b.barrier()
        sstb = p.sbuf("sstb", (128, 132), BF16, stack=ph)
        psf = p.psum("psf", (1, 128), F32, stack=ph)
        for tt in range(16):
            pe0 = b.pe.n

            def pe1(t, tt=tt, pe0=pe0):
                mm = None
                for kt in range(NS):
                    mm = t.matmul(pss[:], xc[:, kt, 3 + tt * 128:
                                             3 + (tt + 1) * 128],
                                  wx[:, kt, :], start=(kt == 0),
                                  stop=(kt == NS - 1))
                b.pe.inc(mm)
            b.blk.tensor(pe1)

            def dv1(v, pe0=pe0):
                b.pe.wait(v, pe0 + 1)
                v.tensor_copy(sst[:], pss[:])
                b.dve.inc(v.tensor_copy(sstb[:], sst[:]), 2)
            b.blk.vector(dv1)
            b.barrier()

            pe1_0 = b.pe.n

            def pe2(t, tt=tt, pe1_0=pe1_0):
                t.transpose(psr[0:128, 0:128], sstb[:, 0:128], ident[:])
                b.pe.inc(t.transpose(psf[:], sst[:, 128:129], identf[:]), 2)
            b.blk.tensor(pe2)

            def dv3(v, tt=tt, pe1_0=pe1_0):
                b.pe.wait(v, pe1_0 + 2)
                v.tensor_copy(btr[:, tt * 128:(tt + 1) * 128],
                              psr[0:64, 0:128])
                v.tensor_copy(ctr[:, tt * 128:(tt + 1) * 128],
                              psr[64:128, 0:128])
                b.dve.inc(v.tensor_copy(srow[:, tt * 128:(tt + 1) * 128],
                                        psf[:]), 3)
            b.blk.vector(dv3)
            b.barrier()

    sbc = p.sbuf("sbc", (128, T), F32, stack=scanScope)
    with ExitStack() as ph:
        pbc = p.psum("pbc0", (128, 512), F32, stack=ph)
        pe0 = b.pe.n
        dve0 = b.dve.n

        def peb(t):
            for f in range(NF):
                if f >= 1:
                    b.dve.wait(t, dve0 + f)
                b.pe.inc(t.matmul(pbc[:], b.ones_row_f[:],
                                  srow[:, f * 512:(f + 1) * 512],
                                  start=True, stop=True))

        def dvb(v):
            for f in range(NF):
                b.pe.wait(v, pe0 + f + 1)
                b.dve.inc(v.tensor_copy(sbc[:, f * 512:(f + 1) * 512],
                                        pbc[:]))
        b.blk.tensor(peb)
        b.blk.vector(dvb)
        b.barrier()

    with ExitStack() as ph:
        lam = p.sbuf("lam", (128, T), F32, stack=ph)
        bn = p.sbuf("bn", (128, T), F32, stack=ph)
        hh = p.sbuf("hh", (128, T), F32, stack=ph)
        yac = p.sbuf("yac", (128, T), F32, stack=ph)
        ybf = p.sbuf("ybf", (128, T), BF16, stack=ph)
        pbb = p.psum("pbb", (128, 4, 512), F32, stack=ph)
        pcc = p.psum("pcc", (128, 4, 512), F32, stack=ph)
        y_hbm = t_y.rearrange("(a p) c -> p a c", p=128)

        dtxs = p.sbuf("dtxs", (128, T), BF16, stack=ph)
        dts = p.sbuf("dts0", (128, T), F32, stack=ph)
        u2s = p.sbuf("u2s", (128, T), F32, stack=ph)
        for st in range(NS):
            def dva(v, st=st):
                b.dve.inc(v.tensor_scalar(dts[:], sbc[:],
                                          dtbcol[:, st:st + 1], None,
                                          ALU.add))
            b.blk.vector(dva)

            def aca(a):
                b.dve.wait(a)
                a.activation(u2s[:], dts[:], AF.Abs)
                b.act.inc(a.activation(u2s[:], u2s[:], AF.Exp, scale=-1.0))
            b.blk.scalar(aca)

            def dvb2(v):
                b.act.wait(v)
                b.dve.inc(v.tensor_scalar_add(u2s[:], u2s[:], 1.0))
            b.blk.vector(dvb2)

            def acb(a):
                b.dve.wait(a)
                b.act.inc(a.activation(u2s[:], u2s[:], AF.Ln))
            b.blk.scalar(acb)

            def dvc2(v, st=st):
                b.act.wait(v)
                v.tensor_scalar_max(dts[:], dts[:], 0.0)
                v.tensor_add(dts[:], dts[:], u2s[:])
                b.dve.inc(v.tensor_mul(dtxs[:], dts[:], xc[:, st, 3:TP]), 3)
            b.blk.vector(dvc2)
            b.barrier()
            for n in range(DS):
                pe0 = b.pe.n
                dve0 = b.dve.n
                act0 = b.act.n

                def peb2(t, st=st, n=n, pe0=pe0, dve0=dve0):
                    b.dve.wait(t, dve0)
                    for f in range(NF):
                        sl = slice(f * 512, (f + 1) * 512)
                        t.matmul(pbb[:, f, :], seln[:, n, :],
                                 btr[:, sl], start=True, stop=True)
                        mm = t.matmul(pcc[:, f, :], seln[:, n, :],
                                      ctr[:, sl], start=True, stop=True)
                    b.pe.inc(mm)

                def acl(a, st=st, n=n, pe0=pe0, act0=act0, dve0=dve0):
                    b.dve.wait(a, dve0)
                    b.act.inc(a.activation(lam[:], dts[:], AF.Exp,
                                           scale=acol[:, st, n:n + 1]))

                def dvs(v, st=st, n=n, pe0=pe0, act0=act0, dve0=dve0):
                    b.pe.wait(v, pe0 + 1)
                    b.act.wait(v, act0 + 1)
                    for f in range(NF):
                        sl = slice(f * 512, (f + 1) * 512)
                        v.tensor_mul(bn[:, sl], dtxs[:, sl], pbb[:, f, :])
                    v.tensor_tensor_scan(hh[:], lam[:], bn[:], 0.0,
                                         ALU.mult, ALU.add)
                    for f in range(NF):
                        sl = slice(f * 512, (f + 1) * 512)
                        v.tensor_mul(hh[:, sl], hh[:, sl], pcc[:, f, :])
                    if n == 0:
                        b.dve.inc(v.tensor_copy(yac[:], hh[:]))
                    else:
                        b.dve.inc(v.tensor_add(yac[:], yac[:], hh[:]))
                b.blk.tensor(peb2)
                b.blk.scalar(acl)
                b.blk.vector(dvs)
            def dvy(v, st=st):
                v.scalar_tensor_tensor(yac[:], xc[:, st, 3:TP],
                                       dcocol[:, st:st + 1], yac[:],
                                       ALU.mult, ALU.add)
                b.dve.inc(v.tensor_copy(ybf[:], yac[:]), 2)
            b.blk.vector(dvy)
            b.barrier()
            b.store(y_hbm[:, st, :], ybf[:])
            b.barrier()

    if debug:
        b.store(t_dbg[0:64, :], btr[:], cast=True)
        b.store(t_dbg[64:128, :], ctr[:], cast=True)
        b.store(t_dbg[128:129, :], srow[:])
        b.store(t_dbg[129:130, :], sbc[0:1, :])
        b.barrier()
    scanScope.close()

    with ExitStack() as ph:
        winz = p.sbuf("winz", (128, 8, DI), BF16, stack=ph)
        wout = p.sbuf("wout", (128, NS, D), BF16, stack=ph)
        ych = p.sbuf("ych0", (128, NS, 512), BF16, stack=ph)
        zch = p.sbuf("zch0", (128, NS, 512), BF16, stack=ph)
        sch = p.sbuf("sch0", (128, 8, 512), BF16, stack=ph)
        pz = [p.psum("pz", (128, 512), F32, stack=ph) for _ in range(2)]
        b.load(winz[:], fm_ap(t_win)[:, :, DI:2 * DI], cast=True)
        b.load(wout[:], t_wout.rearrange("(a p) c -> p a c", p=128),
               cast=True)
        b.barrier()
        for tc in range(T // 512):
            tsl = slice(tc * 512, (tc + 1) * 512)
            b.load(ych[:], y_hbm[:, :, tsl])
            b.barrier()
            pe0 = b.pe.n
            act0 = b.act.n

            def pez(t, tc=tc, pe0=pe0, act0=act0):
                for ct in range(NS):
                    ps = pz[ct % 2]
                    if ct >= 2:
                        b.act.wait(t, act0 + ct - 1)
                    mm = None
                    for kt in range(8):
                        mm = t.matmul(ps[:],
                                      winz[:, kt, ct * 128:(ct + 1) * 128],
                                      xn[:, kt, tsl],
                                      start=(kt == 0), stop=(kt == 7))
                    b.pe.inc(mm)
            b.blk.tensor(pez)

            def acz(a, pe0=pe0):
                for ct in range(NS):
                    b.pe.wait(a, pe0 + ct + 1)
                    b.act.inc(a.activation(zch[:, ct, :], pz[ct % 2][:],
                                           AF.Silu))
            b.blk.scalar(acz)
            b.barrier()

            def dvg(v):
                for ct in range(NS):
                    b.dve.inc(v.tensor_mul(ych[:, ct, :], ych[:, ct, :],
                                           zch[:, ct, :]))
            b.blk.vector(dvg)
            b.barrier()

            pe0 = b.pe.n
            dve0 = b.dve.n

            def peo(t, pe0=pe0, dve0=dve0):
                for dt_ in range(8):
                    ps = pz[dt_ % 2]
                    if dt_ >= 2:
                        b.dve.wait(t, dve0 + dt_ - 1)
                    mm = None
                    for kt in range(NS):
                        mm = t.matmul(ps[:],
                                      wout[:, kt, dt_ * 128:(dt_ + 1) * 128],
                                      ych[:, kt, :],
                                      start=(kt == 0), stop=(kt == NS - 1))
                    b.pe.inc(mm)
            b.blk.tensor(peo)

            def dvo(v, pe0=pe0):
                for dt_ in range(8):
                    b.pe.wait(v, pe0 + dt_ + 1)
                    b.dve.inc(v.tensor_copy(sch[:, dt_, :], pz[dt_ % 2][:]))
            b.blk.vector(dvo)
            b.barrier()
            b.store(fm_ap(t_s)[:, :, tsl], sch[:])
            b.barrier()
    b.finish()
    return b



def build_l2_mixer(debug=False, upto=None):
    Q = 32
    NCH = T // Q
    b = B("l2mix")
    p = b.p

    t_x = p.din("xT", (D, T))
    t_maa = p.din("maa_x", (128, 8))
    t_wkvrg = p.din("wkvrg", (128, 40))
    t_tmw1 = p.din("tm_w1", (D, 160))
    t_tmw2 = p.din("tm_w2f", (5, 32, D))
    t_td = p.din("td", (128, 8))
    t_tdw1 = p.din("td_w1", (D, 64))
    t_tdw2 = p.din("td_w2", (64, D))
    t_u = p.din("u_col", (128, 8))
    t_wr = p.din("wr", (D, D))
    t_wk = p.din("wk", (D, D))
    t_wv = p.din("wv", (D, D))
    t_wg = p.din("wg", (D, D))
    t_wo = p.din("wo", (D, D))
    t_lnw = p.din("lnw", (1, D))
    t_lnb = p.din("lnb", (1, D))
    t_xn = p.dout("xnT", (D, T), BF16)
    t_s = p.dout("sT", (D, T), BF16)
    t_rf = p.dtmp("l2_rf", (D, T), BF16)
    t_kf = p.dtmp("l2_kf", (D, T), BF16)
    t_ktm = p.dtmp("l2_ktm", (T, D), BF16)
    t_vtm = p.dtmp("l2_vtm", (T, D), BF16)
    t_gtm = p.dtmp("l2_gtm", (T, D), BF16)
    t_ew = p.dtmp("l2_ew", (D, T), F16)
    t_y = p.dtmp("l2_y", (T, D), BF16)
    if debug:
        t_dbg = p.dout("l2_dbg", (D, T), F32)
        t_y = p.dout("l2_yd", (T, D), BF16)
        t_ew = p.dout("l2_ewd", (D, T), F16)
        t_gtm = p.dout("l2_gtmd", (T, D), BF16)
        t_rf = p.dout("l2_rfd", (D, T), BF16)
        t_kf = p.dout("l2_kfd", (D, T), BF16)
        t_vtm = p.dout("l2_vtmd", (T, D), BF16)

    b.init_consts()
    ident = b.const("c_ident2", np.eye(128, dtype=np.float32), sbuf_dt=BF16)
    smask_np = (np.arange(32)[:, None] < np.arange(32)[None, :])
    smask = b.const("c_smask", smask_np.astype(np.float32), sbuf_dt=BF16)

    maacol = p.sbuf("maacol", (128, 8), F32)
    wkvcol = p.sbuf("wkvcol", (128, 40), F32)
    tdcol = p.sbuf("tdcol", (128, 8), F32)
    ucol = p.sbuf("ucol", (128, 8), F32)
    b.load(maacol[:], t_maa)
    b.load(wkvcol[:], t_wkvrg)
    b.load(tdcol[:], t_td)
    b.load(ucol[:], t_u)

    xn = p.sbuf("xn", (128, 8, T), BF16)
    ln_fm_stream(b, fm_ap(t_x), xn)
    b.store(fm_ap(t_xn), xn[:])

    if upto == "ln":
        b.finish()
        return b

    phX = ExitStack()
    sx = p.sbuf("sx", (128, 8, T), BF16, stack=phX)
    mfb = [p.sbuf(f"mf{f}", (32, T), BF16, stack=phX) for f in range(5)]

    def dvsx(v):
        for j in range(8):
            v.tensor_scalar_mul(sx[:, j, 0:1], xn[:, j, 0:1], -1.0)
            b.dve.inc(v.tensor_sub(sx[:, j, 1:T], xn[:, j, 0:T - 1],
                                   xn[:, j, 1:T]), 2)
    b.blk.vector(dvsx)
    b.barrier()

    with ExitStack() as ph:
        xxx = p.sbuf("xxx", (128, 8, T), BF16, stack=ph)
        w1s = p.sbuf("tm1", (128, 8, 160), BF16, stack=ph)
        pm = p.psum("pm", (32, 512), F32, stack=ph)
        b.load(w1s[:], t_tmw1.rearrange("(a p) c -> p a c", p=128), cast=True)
        b.barrier()

        def dvx(v):
            for j in range(8):
                b.dve.inc(v.scalar_tensor_tensor(
                    xxx[:, j, :], sx[:, j, :], maacol[:, j:j + 1], xn[:, j, :],
                    ALU.mult, ALU.add))
        b.blk.vector(dvx)
        b.barrier()

        for f5 in range(5):
            pe0, act0 = b.pe.n, b.act.n

            def pem(t, f5=f5, act0=act0):
                for ft in range(NF):
                    sl = slice(ft * 512, (ft + 1) * 512)
                    if ft >= 1:
                        b.act.wait(t, act0 + ft)
                    mm = None
                    for kt in range(8):
                        mm = t.matmul(pm[:], w1s[:, kt, f5 * 32:(f5 + 1) * 32],
                                      xxx[:, kt, sl], start=(kt == 0),
                                      stop=(kt == 7))
                    b.pe.inc(mm)

            def acm(a, f5=f5, pe0=pe0):
                for ft in range(NF):
                    sl = slice(ft * 512, (ft + 1) * 512)
                    b.pe.wait(a, pe0 + ft + 1)
                    b.act.inc(a.activation(mfb[f5][:, sl], pm[:], AF.Tanh))
            b.blk.tensor(pem)
            b.blk.scalar(acm)
            b.barrier()


    if upto == "mblocks":
        b.finish()
        return b

    with ExitStack() as ph:
        tmw2s = p.sbuf("tm2", (32, 5, D), BF16, stack=ph)
        xf = p.sbuf("xf2", (128, 8, T), BF16, stack=ph)
        wmat = p.sbuf("wmat", (128, 8, D), BF16, stack=ph)
        stg = p.sbuf("stg", (128, D), BF16, stack=ph)
        stg3 = p.sbuf("stg3", (128, T), BF16, stack=ph)
        pdx = [p.psum("pdx", (128, 512), F32, stack=ph) for _ in range(2)]
        b.load(tmw2s[:], t_tmw2.rearrange("f a c -> a f c"), cast=True)
        b.barrier()

        def make_xf(f5):
            pe0, dve0 = b.pe.n, b.dve.n

            def ped(t, f5=f5, dve0=dve0):
                for i in range(8 * NF):
                    dt_, ft = i // NF, i % NF
                    sl = slice(ft * 512, (ft + 1) * 512)
                    if i >= 2:
                        b.dve.wait(t, dve0 + 2 * (i - 1))
                    b.pe.inc(t.matmul(pdx[i % 2][:],
                                      tmw2s[:, f5, dt_ * 128:(dt_ + 1) * 128],
                                      mfb[f5][:, sl], start=True, stop=True))

            def dvd(v, f5=f5, pe0=pe0):
                for i in range(8 * NF):
                    dt_, ft = i // NF, i % NF
                    sl = slice(ft * 512, (ft + 1) * 512)
                    b.pe.wait(v, pe0 + i + 1)
                    v.scalar_tensor_tensor(
                        xf[:, dt_, sl], pdx[i % 2][:],
                        wkvcol[:, f5 * 8 + dt_:f5 * 8 + dt_ + 1],
                        sx[:, dt_, sl], ALU.add, ALU.mult)
                    b.dve.inc(v.tensor_add(xf[:, dt_, sl], xf[:, dt_, sl],
                                           xn[:, dt_, sl]), 2)
            b.blk.tensor(ped)
            b.blk.vector(dvd)
            b.barrier()

        def proj_tm(dst_hbm, t_w, act=None):
            b.load(wmat[:], fm_ap(t_w), cast=True)
            b.barrier()
            dst_r = dst_hbm.rearrange("(a p) c -> p a c", p=128)
            for tt in range(16):
                pe0, ev0 = b.pe.n, (b.act.n if act else b.dve.n)

                def pep(t, tt=tt, ev0=ev0):
                    evs = b.act if act else b.dve
                    for ft in range(2):
                        sl = slice(ft * 512, (ft + 1) * 512)
                        if ft >= 1:
                            evs.wait(t, ev0 + ft)
                        mm = None
                        for kt in range(8):
                            mm = t.matmul(pdx[0][:],
                                          xf[:, kt, tt * 128:(tt + 1) * 128],
                                          wmat[:, kt, sl], start=(kt == 0),
                                          stop=(kt == 7))
                        b.pe.inc(mm)
                b.blk.tensor(pep)

                if act:
                    def acp(a, tt=tt, pe0=pe0):
                        for ft in range(2):
                            sl = slice(ft * 512, (ft + 1) * 512)
                            b.pe.wait(a, pe0 + ft + 1)
                            b.act.inc(a.activation(stg[:, sl], pdx[0][:], act))
                    b.blk.scalar(acp)
                else:
                    def dvp(v, tt=tt, pe0=pe0):
                        for ft in range(2):
                            sl = slice(ft * 512, (ft + 1) * 512)
                            b.pe.wait(v, pe0 + ft + 1)
                            b.dve.inc(v.tensor_copy(stg[:, sl], pdx[0][:]))
                    b.blk.vector(dvp)
                b.barrier()
                b.store(dst_r[:, tt, :], stg[:])
                b.barrier()

        def proj_fm(dst_hbm, t_w):
            b.load(wmat[:], fm_ap(t_w), cast=True)
            b.barrier()
            dst_r = fm_ap(dst_hbm)
            for rt in range(8):
                pe0, dve0 = b.pe.n, b.dve.n

                def pep(t, rt=rt, dve0=dve0):
                    for ft in range(NF):
                        sl = slice(ft * 512, (ft + 1) * 512)
                        if ft >= 2:
                            b.dve.wait(t, dve0 + ft - 1)
                        mm = None
                        for kt in range(8):
                            mm = t.matmul(pdx[ft % 2][:],
                                          wmat[:, kt, rt * 128:(rt + 1) * 128],
                                          xf[:, kt, sl], start=(kt == 0),
                                          stop=(kt == 7))
                        b.pe.inc(mm)

                def dvp(v, rt=rt, pe0=pe0):
                    for ft in range(NF):
                        sl = slice(ft * 512, (ft + 1) * 512)
                        b.pe.wait(v, pe0 + ft + 1)
                        b.dve.inc(v.tensor_copy(stg3[:, sl], pdx[ft % 2][:]))
                b.blk.tensor(pep)
                b.blk.vector(dvp)
                b.barrier()
                b.store(dst_r[:, rt, :], stg3[:])
                b.barrier()

        make_xf(4)
        proj_tm(t_gtm, t_wg, act=AF.Silu)

        if upto == "g":
            b.finish()
            return b

        make_xf(0)
        with ExitStack() as ph2:
            tdw1s = p.sbuf("tdw1", (128, 8, 64), BF16, stack=ph2)
            tdw2s = p.sbuf("tdw2", (64, D), BF16, stack=ph2)
            th = p.sbuf("th", (64, T), BF16, stack=ph2)
            ewst = p.sbuf("ewst", (128, 8, T), F16, stack=ph2)
            pt4 = p.psum("pt4", (64, T), F32, stack=ph2)
            b.load(tdw1s[:], t_tdw1.rearrange("(a p) c -> p a c", p=128),
                   cast=True)
            b.load(tdw2s[:], t_tdw2, cast=True)
            b.barrier()
            pe0 = b.pe.n

            def pet(t):
                mm = None
                for ft in range(NF):
                    sl = slice(ft * 512, (ft + 1) * 512)
                    for kt in range(8):
                        mm = t.matmul(pt4[:, sl], tdw1s[:, kt, :],
                                      xf[:, kt, sl], start=(kt == 0),
                                      stop=(kt == 7))
                b.pe.inc(mm)

            def act_t(a, pe0=pe0):
                b.pe.wait(a, pe0 + 1)
                b.act.inc(a.activation(th[:], pt4[:], AF.Tanh))
            b.blk.tensor(pet)
            b.blk.scalar(act_t)
            b.barrier()

            pe0, act0 = b.pe.n, b.act.n

            def pew(t, act0=act0):
                for i in range(8 * NF):
                    dt_, ft = i // NF, i % NF
                    sl = slice(ft * 512, (ft + 1) * 512)
                    if i >= 2:
                        b.act.wait(t, act0 + i - 1)
                    b.pe.inc(t.matmul(pdx[i % 2][:],
                                      tdw2s[:, dt_ * 128:(dt_ + 1) * 128],
                                      th[:, sl], start=True, stop=True))

            def acw(a, pe0=pe0):
                for i in range(8 * NF):
                    dt_, ft = i // NF, i % NF
                    sl = slice(ft * 512, (ft + 1) * 512)
                    b.pe.wait(a, pe0 + i + 1)
                    b.act.inc(a.activation(ewst[:, dt_, sl], pdx[i % 2][:],
                                           AF.Exp,
                                           bias=tdcol[:, dt_:dt_ + 1]))
            b.blk.tensor(pew)
            b.blk.scalar(acw)
            b.barrier()
            b.store(fm_ap(t_ew), ewst[:])
            b.barrier()

        if upto == "ew":
            b.finish()
            return b

        make_xf(1)
        proj_fm(t_kf, t_wk)
        proj_tm(t_ktm, t_wk)
        make_xf(2)
        proj_tm(t_vtm, t_wv)
        make_xf(3)
        proj_fm(t_rf, t_wr)

    phX.close()

    if upto == "kvr":
        b.finish()
        return b

    with ExitStack() as ph:
        S = p.sbuf("S", (128, 8, 64), F32, stack=ph)
        Sb = p.sbuf("Sb", (128, 8, 64), BF16, stack=ph)
        rch = p.sbuf("rch", (128, 8, Q), BF16, stack=ph)
        kch = p.sbuf("kch", (128, 8, Q), BF16, stack=ph)
        ewch = p.sbuf("ewch", (128, 8, Q), F32, stack=ph)
        ktmc = p.sbuf("ktmc", (32, D), BF16, stack=ph)
        vtmc = p.sbuf("vtmc", (32, D), BF16, stack=ph)
        zro = p.sbuf("zro", (128, 8, Q), F32, stack=ph)
        cumI = p.sbuf("cumI", (128, 8, Q), F32, stack=ph)
        zqc = p.sbuf("zqc", (128, 8, Q), F32, stack=ph)
        e2f32 = p.sbuf("e2f32", (128, 8, Q), F32, stack=ph)
        e3f32 = p.sbuf("e3f32", (128, 8, Q), F32, stack=ph)
        e4f32 = p.sbuf("e4f32", (128, 8, Q), F32, stack=ph)
        e1 = p.sbuf("e1", (128, 8, Q), BF16, stack=ph)
        e2 = p.sbuf("e2", (128, 8, Q), BF16, stack=ph)
        e3 = p.sbuf("e3", (128, 8, Q), BF16, stack=ph)
        e4f_ = p.sbuf("e4f", (128, 8, Q), BF16, stack=ph)
        deccol = p.sbuf("deccol", (128, 8), F32, stack=ph)
        rt1 = p.sbuf("rt1", (128, 8, Q), BF16, stack=ph)
        rt2 = p.sbuf("rt2", (128, 8, Q), BF16, stack=ph)
        kt1 = p.sbuf("kt1", (128, 8, Q), BF16, stack=ph)
        q1 = p.sbuf("q1", (128, 8, Q), BF16, stack=ph)
        e4t = p.sbuf("e4t", (32, D), BF16, stack=ph)
        q1t = p.sbuf("q1t", (32, D), BF16, stack=ph)
        kth = p.sbuf("kth", (32, D), BF16, stack=ph)
        dtm = p.sbuf("dtm", (32, 16), F32, stack=ph)
        msbt = p.sbuf("msbt", (32, 32), BF16, stack=ph)
        ystg = p.sbuf("ystg", (32, D), BF16, stack=ph)
        pT = [p.psum("pT2", (32, 128), BF16, stack=ph) for _ in range(2)]
        pM = p.psum("pM", (32, 32), F32, stack=ph)
        pYr = p.psum("pYr", (32, 64), F32, stack=ph)
        pKV = p.psum("pKV", (64, 64), F32, stack=ph)

        def vz(v):
            v.memset(S[:], 0.0)
            v.memset(Sb[:], 0.0)
            b.dve.inc(v.memset(zro[:], 0.0))
        b.blk.vector(vz)
        b.barrier()

        ew_r = fm_ap(t_ew)
        rf_r = fm_ap(t_rf)
        kf_r = fm_ap(t_kf)
        for c in range(NCH):
            csl = slice(c * Q, (c + 1) * Q)
            b.load(rch[:], rf_r[:, :, csl])
            b.load(kch[:], kf_r[:, :, csl])
            b.load(ewch[:], ew_r[:, :, csl], cast=True)
            b.load(ktmc[:], t_ktm[c * Q:(c + 1) * Q, :])
            b.load(vtmc[:], t_vtm[c * Q:(c + 1) * Q, :])
            b.barrier()

            def dvB(v, c=c):
                for j in range(8):
                    v.tensor_tensor_scan(cumI[:, j, :], zro[:, j, :],
                                         ewch[:, j, :], 0.0, ALU.add, ALU.add)
                for j in range(8):
                    v.tensor_sub(zqc[:, j, :], cumI[:, j, :], ewch[:, j, :])
                for j in range(8):
                    mid = cumI[:, j, 15:16]
                    tot = cumI[:, j, Q - 1:Q]
                    v.tensor_scalar(e2f32[:, j, :], cumI[:, j, :], mid, 38.0,
                                    ALU.subtract, ALU.min)
                    v.tensor_scalar(e3f32[:, j, :], zqc[:, j, :], mid, -38.0,
                                    ALU.subtract, ALU.max)
                    v.tensor_scalar(e4f32[:, j, :], cumI[:, j, :], tot, None,
                                    ALU.subtract)
                b.dve.inc(v.nop())
            b.blk.vector(dvB)
            b.barrier()

            def acC(a, c=c):
                fl = lambda x: x[:].rearrange("p a c -> p (a c)")
                a.activation(fl(e2), fl(e2f32), AF.Exp)
                a.activation(fl(e3), fl(e3f32), AF.Exp, scale=-1.0)
                a.activation(fl(e4f_), fl(e4f32), AF.Exp)
                a.activation(fl(e1), fl(zqc), AF.Exp, scale=-1.0)
                b.act.inc(a.activation(
                    deccol[:], cumI[:, :, Q - 1:Q].rearrange("p a c -> p (a c)"),
                    AF.Exp, scale=-1.0))
            b.blk.scalar(acC)
            b.barrier()

            def dvD(v, c=c):
                fl = lambda x: x[:].rearrange("p a c -> p (a c)")
                v.tensor_mul(fl(rt1), fl(rch), fl(e1))
                v.tensor_mul(fl(rt2), fl(rch), fl(e3))
                v.tensor_mul(fl(kt1), fl(kch), fl(e2))
                for j in range(8):
                    v.scalar_tensor_tensor(q1[:, j, :], rch[:, j, :],
                                           ucol[:, j:j + 1], kch[:, j, :],
                                           ALU.mult, ALU.mult)
                b.dve.inc(v.nop())
            b.blk.vector(dvD)
            b.barrier()

            pe0, dve0 = b.pe.n, b.dve.n

            def peE(t, c=c, dve0=dve0):
                for i in range(16):
                    if i >= 2:
                        b.dve.wait(t, dve0 + i - 1)
                    src = e4f_[:, i, :] if i < 8 else q1[:, i - 8, :]
                    b.pe.inc(t.transpose(pT[i % 2][:], src, ident[:]))

            def dvE(v, c=c, pe0=pe0):
                for i in range(16):
                    b.pe.wait(v, pe0 + i + 1)
                    dst = e4t if i < 8 else q1t
                    j = i % 8
                    b.dve.inc(v.tensor_copy(dst[:, j * 128:(j + 1) * 128],
                                            pT[i % 2][:]))
                v.tensor_reduce(dtm[:], q1t[:].rearrange(
                    "p (h k) -> p h k", k=64), AX.X, ALU.add)
                b.dve.inc(v.tensor_mul(kth[:], ktmc[:], e4t[:]))
            b.blk.tensor(peE)
            b.blk.vector(dvE)
            b.barrier()

            peG, dveG = b.pe.n, b.dve.n

            def peH(t, c=c, peG=peG, dveG=dveG):
                for h in range(16):
                    pb = 64 * (h % 2)
                    j = h // 2
                    hs = slice(64 * h, 64 * (h + 1))
                    if h >= 1:
                        b.dve.wait(t, dveG + 4 * h - 3)
                    b.pe.inc(t.matmul(pM[:], kt1[pb:pb + 64, j, :],
                                      rt2[pb:pb + 64, j, :],
                                      start=True, stop=True))
                    if h >= 1:
                        b.dve.wait(t, dveG + 4 * h - 2)
                    b.pe.inc(t.matmul(pYr[:], rt1[pb:pb + 64, j, :],
                                      Sb[pb:pb + 64, j, :],
                                      start=True, stop=False))
                    b.dve.wait(t, dveG + 4 * h + 1)
                    b.pe.inc(t.matmul(pYr[:], msbt[:], vtmc[:, hs],
                                      start=False, stop=True))
                    b.pe.inc(t.matmul(pKV[:], kth[:, hs], vtmc[:, hs],
                                      start=True, stop=True))

            def dvH(v, c=c, peG=peG):
                for h in range(16):
                    pb = 64 * (h % 2)
                    j = h // 2
                    hs = slice(64 * h, 64 * (h + 1))
                    b.pe.wait(v, peG + 4 * h + 1)
                    b.dve.inc(v.tensor_mul(msbt[:], pM[:], smask[:, 0:32]))
                    b.pe.wait(v, peG + 4 * h + 3)
                    b.dve.inc(v.scalar_tensor_tensor(
                        ystg[:, hs], vtmc[:, hs], dtm[:, h:h + 1], pYr[:],
                        ALU.mult, ALU.add))
                    b.pe.wait(v, peG + 4 * h + 4)
                    b.dve.inc(v.scalar_tensor_tensor(
                        S[pb:pb + 64, j, :], S[pb:pb + 64, j, :],
                        deccol[pb:pb + 64, j:j + 1], pKV[:],
                        ALU.mult, ALU.add))
                    b.dve.inc(v.tensor_copy(Sb[pb:pb + 64, j, :],
                                            S[pb:pb + 64, j, :]))
            b.blk.tensor(peH)
            b.blk.vector(dvH)
            b.barrier()
            b.store(t_y[c * Q:(c + 1) * Q, :], ystg[:])


    if upto == "scan":
        b.finish()
        return b

    with ExitStack() as ph:
        yt = p.sbuf("yt", (128, D), BF16, stack=ph)
        gt = p.sbuf("gt", (128, D), BF16, stack=ph)
        zt = p.sbuf("zt", (128, D), BF16, stack=ph)
        sq = p.sbuf("sq", (128, D), F32, stack=ph)
        musum = p.sbuf("musum", (128, 16), F32, stack=ph)
        sqsum = p.sbuf("sqsum", (128, 16), F32, stack=ph)
        mu = p.sbuf("mu", (128, 16), F32, stack=ph)
        va = p.sbuf("va", (128, 16), F32, stack=ph)
        rstd = p.sbuf("rstd", (128, 16), F32, stack=ph)
        lnwbc = p.sbuf("lnwbc", (128, D), BF16, stack=ph)
        lnbbc = p.sbuf("lnbbc", (128, D), BF16, stack=ph)
        lnrow = p.sbuf("lnrow", (1, D), F32, stack=ph)
        lnrow2 = p.sbuf("lnrow2", (1, D), F32, stack=ph)
        zfm = p.sbuf("zfm", (128, 8, T), BF16, stack=ph)
        pbc = p.psum("pbc", (128, 512), F32, stack=ph)
        pT2 = [p.psum("pT3", (128, 128), BF16, stack=ph) for _ in range(2)]
        b.load(lnrow[:], t_lnw)
        b.load(lnrow2[:], t_lnb)
        b.barrier()
        pe0, dve0 = b.pe.n, b.dve.n

        def pebc(t, dve0=dve0):
            for i in range(4):
                row = lnrow if i < 2 else lnrow2
                if i >= 1:
                    b.dve.wait(t, dve0 + i)
                b.pe.inc(t.matmul(pbc[:], b.ones_row_f[:, 0:128],
                                  row[:, (i % 2) * 512:(i % 2 + 1) * 512],
                                  start=True, stop=True))

        def dvbc(v, pe0=pe0):
            for i in range(4):
                dst = lnwbc if i < 2 else lnbbc
                b.pe.wait(v, pe0 + i + 1)
                b.dve.inc(v.tensor_copy(
                    dst[:, (i % 2) * 512:(i % 2 + 1) * 512], pbc[:]))
        b.blk.tensor(pebc)
        b.blk.vector(dvbc)
        b.barrier()

        y_r = t_y.rearrange("(a p) c -> p a c", p=128)
        g_r = t_gtm.rearrange("(a p) c -> p a c", p=128)
        for tt in range(16):
            b.load(yt[:], y_r[:, tt, :])
            b.load(gt[:], g_r[:, tt, :])
            b.barrier()

            def dv1(v, tt=tt):
                v.tensor_mul(sq[:], yt[:], yt[:])
                v.tensor_reduce(musum[:], yt[:].rearrange(
                    "p (h k) -> p h k", k=64), AX.X, ALU.add)
                v.tensor_reduce(sqsum[:], sq[:].rearrange(
                    "p (h k) -> p h k", k=64), AX.X, ALU.add)
                v.tensor_scalar_mul(mu[:], musum[:], 1.0 / 64)
                v.tensor_scalar_mul(va[:], sqsum[:], 1.0 / 64)
                v.tensor_mul(sq[:, 0:16], mu[:], mu[:])
                v.tensor_sub(va[:], va[:], sq[:, 0:16])
                b.dve.inc(v.tensor_scalar_add(va[:], va[:], LN_EPS))
            b.blk.vector(dv1)
            b.barrier()

            def ac1(a, tt=tt):
                a.activation(rstd[:], va[:], AF.Ln)
                b.act.inc(a.activation(rstd[:], rstd[:], AF.Exp, scale=-0.5))
            b.blk.scalar(ac1)
            b.barrier()

            def dv2(v, tt=tt):
                for h in range(16):
                    hs = slice(64 * h, 64 * (h + 1))
                    v.tensor_scalar(zt[:, hs], yt[:, hs], mu[:, h:h + 1],
                                    rstd[:, h:h + 1], ALU.subtract, ALU.mult)
                v.tensor_mul(zt[:], zt[:], lnwbc[:])
                v.tensor_add(zt[:], zt[:], lnbbc[:])
                b.dve.inc(v.tensor_mul(zt[:], zt[:], gt[:]))
            b.blk.vector(dv2)
            b.barrier()

            pe0, dve0 = b.pe.n, b.dve.n

            def pez(t, tt=tt, dve0=dve0):
                for j in range(8):
                    if j >= 2:
                        b.dve.wait(t, dve0 + j - 1)
                    b.pe.inc(t.transpose(pT2[j % 2][:],
                                         zt[:, j * 128:(j + 1) * 128],
                                         ident[:]))

            def dvz(v, tt=tt, pe0=pe0):
                for j in range(8):
                    b.pe.wait(v, pe0 + j + 1)
                    b.dve.inc(v.tensor_copy(
                        zfm[:, j, tt * 128:(tt + 1) * 128], pT2[j % 2][:]))
            b.blk.tensor(pez)
            b.blk.vector(dvz)
            b.barrier()

        if debug:
            b.store(fm_ap(t_dbg), zfm[:], cast=True)

        wo_s = p.sbuf("wo_s", (128, 8, D), BF16, stack=ph)
        sfm = p.sbuf("sfm", (128, 8, T), BF16, stack=ph)
        b.load(wo_s[:], fm_ap(t_wo), cast=True)
        b.barrier()
        emit_mm_tiled(
            b, None,
            lambda eng, rt, ft, ps: eng.tensor_copy(
                sfm[:, rt, ft * 512:(ft + 1) * 512], ps[:]),
            lambda kt, rt: wo_s[:, kt, rt * 128:(rt + 1) * 128],
            lambda kt, ft: zfm[:, kt, ft * 512:(ft + 1) * 512],
            8, D, T)
        b.store(fm_ap(t_s), sfm[:])
    b.finish()
    return b


_BUILDERS = {}


def _get_builders():
    if not _BUILDERS:
        _BUILDERS["l0m"] = build_l0_mixer()
        _BUILDERS["l0f"] = build_l0_ffn()
        _BUILDERS["l1m"] = build_l1_mixer()
        _BUILDERS["l1f"] = build_l1_ffn()
        _BUILDERS["l2m"] = build_l2_mixer()
        _BUILDERS["l2f"] = build_l2_ffn()
        _BUILDERS["l3m"] = build_l3_mixer()
        _BUILDERS["l3f"] = build_l3_ffn()
    return _BUILDERS


def kernel(**ins):
    from concourse.bass_utils import run_bass_kernel_spmd
    bl = _get_builders()
    f32 = np.float32
    a32 = lambda k: np.asarray(ins[k], f32)
    col = lambda a, nt=8: np.ascontiguousarray(
        np.asarray(a, f32).reshape(nt, 128).T)

    x = a32("L0_attn_x")
    xT = [np.ascontiguousarray(x[c].T) for c in range(8)]

    wx = np.zeros((2048, 132), f32)
    wx[:, 0:129] = a32("L0_attn_W_x")
    shared = {
        "l0m": {
            "w_in": a32("L0_attn_W_in"),
            "conv_w": np.ascontiguousarray(
                a32("L0_attn_conv_w").reshape(16, 128, 4)
                .transpose(1, 0, 2).reshape(128, 64)),
            "conv_b": col(ins["L0_attn_conv_b"], 16),
            "w_x": wx,
            "a_log": a32("L0_attn_A_log"),
            "dt_bias": col(ins["L0_attn_dt_bias"], 16),
            "dcoef": col(ins["L0_attn_D"], 16),
            "w_out": a32("L0_attn_W_out"),
        },
        "l0f": {
            "gamma_x": col(ins["L0_ffn_gamma_x"]),
            "gamma_slot": a32("L0_ffn_gamma_slot").reshape(1, 1024),
            "slot_embeds": a32("L0_ffn_slot_embeds").reshape(4096, 1024),
            "w1": a32("L0_ffn_w1"),
            "w2": a32("L0_ffn_w2"),
        },
        "l1m": {
            "qkv_wT": np.ascontiguousarray(a32("L1_attn_qkv_w").T),
            "q_nw": a32("L1_attn_q_norm_w").reshape(64, 1),
            "q_nb": a32("L1_attn_q_norm_b").reshape(64, 1),
            "k_nw": a32("L1_attn_k_norm_w").reshape(64, 1),
            "k_nb": a32("L1_attn_k_norm_b").reshape(64, 1),
            "proj_wT": np.ascontiguousarray(a32("L1_attn_proj_w").T),
            "proj_b": col(ins["L1_attn_proj_b"]),
        },
        "l1f": {
            "w_gate": a32("L1_ffn_w_gate"),
            "w_up": a32("L1_ffn_w_up"),
            "w_down": a32("L1_ffn_w_down"),
        },
        "l2m": {
            "maa_x": col(ins["L2_attn_time_maa_x"]),
            "wkvrg": np.ascontiguousarray(
                a32("L2_attn_time_maa_wkvrg").reshape(5, 8, 128)
                .transpose(2, 0, 1).reshape(128, 40)),
            "tm_w1": a32("L2_attn_tm_w1"),
            "tm_w2f": a32("L2_attn_tm_w2"),
            "td": col(ins["L2_attn_time_decay"]),
            "td_w1": a32("L2_attn_td_w1"),
            "td_w2": a32("L2_attn_td_w2"),
            "u_col": col(a32("L2_attn_time_faaaa").reshape(1024)),
            "wr": a32("L2_attn_Wr"),
            "wk": a32("L2_attn_Wk"),
            "wv": a32("L2_attn_Wv"),
            "wg": a32("L2_attn_Wg"),
            "wo": a32("L2_attn_Wo"),
            "lnw": a32("L2_attn_ln_w").reshape(1, 1024),
            "lnb": a32("L2_attn_ln_b").reshape(1, 1024),
        },
        "l2f": {
            "fc1_q": np.asarray(ins["L2_ffn_fc1_q"], np.int32),
            "fc1_scales": a32("L2_ffn_fc1_scales"),
            "fc1_bias": col(ins["L2_ffn_fc1_bias"], 32),
            "fc2_q": np.asarray(ins["L2_ffn_fc2_q"], np.int32),
            "fc2_scales": a32("L2_ffn_fc2_scales"),
            "fc2_bias": col(ins["L2_ffn_fc2_bias"]),
            "ln_g": col(ins["L2_ffn_ln_gamma"]),
            "ln_b": col(ins["L2_ffn_ln_beta"]),
        },
        "l3m": {
            "wq_down": a32("L3_attn_Wq_down"),
            "wk_down": a32("L3_attn_Wk_down"),
            "wv_down": a32("L3_attn_Wv_down"),
            "wq_up_c": a32("L3_attn_Wq_up_c"),
            "wq_up_e": a32("L3_attn_Wq_up_e"),
            "wk_up_c": a32("L3_attn_Wk_up_c"),
            "wk_up_e": a32("L3_attn_Wk_up_e"),
            "wv_up": a32("L3_attn_Wv_up"),
            "wc": a32("L3_attn_Wc"),
        },
        "l3f": {
            "phi": a32("L3_ffn_phi").reshape(1024, 2048),
            "We": a32("L3_ffn_W"),
            "eb": a32("L3_ffn_b"),
        },
    }

    def launch(key, per_core, outs):
        b = bl[key]
        im = []
        for c in range(8):
            d = dict(shared[key])
            d.update(per_core[c])
            d.update(b.const_inputs)
            im.append(d)
        res = run_bass_kernel_spmd(b.nc, im, core_ids=list(range(8)))
        return [{k: np.asarray(res.results[c][k]) for k in outs}
                for c in range(8)]

    cur = xT
    for lay in ("l0", "l1", "l2", "l3"):
        mix = launch(lay + "m", [{"xT": cur[c]} for c in range(8)],
                     ["xnT", "sT"])
        if lay == "l3":
            per = [{"xT": cur[c], "sT": mix[c]["sT"]} for c in range(8)]
        else:
            per = [{"xT": cur[c], "xnT": mix[c]["xnT"], "sT": mix[c]["sT"]}
                   for c in range(8)]
        ffo = launch(lay + "f", per, ["xoutT"])
        cur = [ffo[c]["xoutT"] for c in range(8)]

    out = np.stack([np.ascontiguousarray(cur[c].T) for c in range(8)])
    return out.astype(np.float32)


# version 03
# speedup vs baseline: 1.1649x; 1.1649x over previous
import math
import numpy as np
from contextlib import ExitStack

import concourse.bass as bass
import concourse.mybir as mybir
from concourse.bass_utils import run_bass_kernel_spmd

F32 = mybir.dt.float32
BF16 = mybir.dt.bfloat16
I32 = mybir.dt.int32
AF = mybir.ActivationFunctionType
ALU = mybir.AluOpType
AX = mybir.AxisListType

T = 2048
D = 1024
KDEBUG_SERIAL_DMA = False
NCORES = 8
LN_EPS = 1e-5
NF = T // 512



class Sem:
    def __init__(self, h):
        self.h = h
        self.n = 0

    def inc(self, inst, k=1):
        inst.then_inc(self.h, k)
        self.n += k
        return self.n

    def dma(self, inst):
        return self.inc(inst, 16)

    def wait(self, eng, v=None):
        eng.wait_ge(self.h, self.n if v is None else v)


class Prog:
    def __init__(self):
        self.nc = bass.Bass("TRN2", target_bir_lowering=False)
        self.ctx = ExitStack()
        self._names = {}
        self.pfx = ""
        self.bound = {}
        self.douts = {}
        self.default_stack = None
        self.out_as_tmp = False

    def _nm(self, base):
        i = self._names.get(base, 0)
        self._names[base] = i + 1
        return f"{base}_{i}" if i else base

    def sem(self, name="s"):
        return Sem(self.ctx.enter_context(self.nc.semaphore(self._nm(name))))

    def sbuf(self, name, shape, dt=F32, stack=None):
        return (stack or self.default_stack or self.ctx).enter_context(
            self.nc.sbuf_tensor(self._nm(name), list(shape), dt))

    def psum(self, name, shape, dt=F32, stack=None):
        return (stack or self.default_stack or self.ctx).enter_context(
            self.nc.psum_tensor(self._nm(name), list(shape), dt))

    def din(self, name, shape, dt=F32):
        if name in self.bound:
            return self.bound[name]
        return self.nc.dram_tensor(self.pfx + name, list(shape), dt,
                                   kind="ExternalInput").ap()

    def dout(self, name, shape, dt=F32):
        if self.out_as_tmp:
            ap = self.dtmp(self.pfx + name, shape, dt)
        else:
            ap = self.nc.dram_tensor(self.pfx + name, list(shape), dt,
                                     kind="ExternalOutput").ap()
        self.douts[self.pfx + name] = ap
        return ap

    def dtmp(self, name, shape, dt=F32):
        return self.nc.dram_tensor(self._nm(name), list(shape), dt,
                                   kind="Internal").ap()

    def block(self):
        return self.ctx.enter_context(self.nc.Block())


def fm_ap(t_hbm, p=128):
    return t_hbm.rearrange("(j p) t -> p j t", p=p)


class B:

    def __init__(self, tag=""):
        self.p = Prog()
        self.nc = self.p.nc
        self.tag = tag
        p = self.p
        self.ld = p.sem("ld")
        self.gld = p.sem("gld")
        self.st = p.sem("st")
        self.gst = p.sem("gst")
        self.pe = p.sem("pe")
        self.dve = p.sem("dve")
        self.act = p.sem("act")
        self.pool = p.sem("pool")
        self.blk = p.block()
        self.const_inputs = {}
        self.ones_row_f = p.sbuf("c_ones_row_f", (1, 128), F32)
        self.ones_row_b = p.sbuf("c_ones_row_b", (1, 512), BF16)
        self.onesf_col = p.sbuf("c_onesf_col", (128, 2), F32)

    def init_consts(self):
        if getattr(self, "_consts_done", False):
            return
        self._consts_done = True

        def f(v):
            v.memset(self.ones_row_f[:], 1.0)
            v.memset(self.ones_row_b[:], 1.0)
            self.dve.inc(v.memset(self.onesf_col[:], 1.0))
        self.blk.vector(f)

    def barrier(self):
        vals = [(s, s.n) for s in
                (self.pe, self.dve, self.act, self.pool, self.ld, self.st,
                 self.gld, self.gst)]

        def mk(eng):
            for s, v in vals:
                s.wait(eng, v)
        for sec in (self.blk.tensor, self.blk.vector, self.blk.scalar,
                    self.blk.gpsimd, self.blk.sync):
            sec(mk)

    def load(self, dst_ap, src_ap, cast=False, sem=None, **kw):
        sem = sem or (self.gld if cast else self.ld)

        def f(eng):
            if KDEBUG_SERIAL_DMA:
                sem.wait(eng)
            sem.dma(eng.dma_start(out=dst_ap, in_=src_ap, **kw))
        (self.blk.gpsimd if cast else self.blk.sync)(f)

    def store(self, dst_ap, src_ap, cast=False, **kw):
        sem = self.gst if cast else self.st

        def f(eng):
            if KDEBUG_SERIAL_DMA:
                sem.wait(eng)
            sem.dma(eng.dma_start(out=dst_ap, in_=src_ap, **kw))
        (self.blk.gpsimd if cast else self.blk.sync)(f)

    def finish(self):
        st, gst = self.st, self.gst

        def f(eng):
            st.wait(eng)
            gst.wait(eng)
        self.blk.sync(f)

    def const(self, name, arr, dt=F32, sbuf_dt=None):
        arr = np.asarray(arr, np.float32 if dt is F32 else None)
        t = self.p.din(name, arr.shape, dt)
        sb = self.p.sbuf("c_" + name, arr.shape, sbuf_dt or dt)
        self.const_inputs[self.p.pfx + name] = arr
        self.load(sb[:], t[:], cast=(sbuf_dt is not None and sbuf_dt is not dt))
        return sb



def emit_mm_tiled(b, psum_pool, out_write, lhsT_tile, rhs_tile, nc_tiles,
                  out_rows, out_cols, fsz=512, extra_rank1=None,
                  evac="dve", evac_dt=None):
    p = b.p
    with ExitStack() as ph:
        nrt = (out_rows + 127) // 128
        nft = (out_cols + fsz - 1) // fsz
        pp = [p.psum("mmp", (128, fsz), F32, stack=ph) for _ in range(2)]
        order = [(rt, ft) for rt in range(nrt) for ft in range(nft)]
        pe0, ev0 = b.pe.n, (b.dve.n if evac == "dve" else b.act.n)
        evs = b.dve if evac == "dve" else b.act

        def pe(t):
            for i, (rt, ft) in enumerate(order):
                ps = pp[i % 2]
                if i >= 2:
                    evs.wait(t, ev0 + i - 1)
                mm = None
                for kt in range(nc_tiles):
                    mm = t.matmul(ps[:], lhsT_tile(kt, rt), rhs_tile(kt, ft),
                                  start=(kt == 0),
                                  stop=(kt == nc_tiles - 1 and extra_rank1 is None))
                if extra_rank1 is not None:
                    l1, r1 = extra_rank1(rt, ft)
                    mm = t.matmul(ps[:], l1, r1, start=False, stop=True)
                b.pe.inc(mm)

        def ev(eng):
            for i, (rt, ft) in enumerate(order):
                ps = pp[i % 2]
                b.pe.wait(eng, pe0 + i + 1)
                evs.inc(out_write(eng, rt, ft, ps))

        b.blk.tensor(pe)
        (b.blk.vector if evac == "dve" else b.blk.scalar)(ev)
        b.barrier()


def ln_fm_stream(b, x_hbm_fm, xn_fm, nt=8, eps=LN_EPS, keep_x=None,
                 hilo_gamma=None):
    p = b.p
    nD = nt * 128
    with ExitStack() as ph:
        xt = p.sbuf("ln_xt", (128, 2, T), F32, stack=ph)
        x2 = p.sbuf("ln_x2", (128, T), BF16, stack=ph)
        r_mean = p.sbuf("ln_rmean", (1, T), F32, stack=ph)
        r_msq = p.sbuf("ln_rmsq", (1, T), F32, stack=ph)
        r_a = p.sbuf("ln_ra", (1, T), F32, stack=ph)
        r_b = p.sbuf("ln_rb", (1, T), F32, stack=ph)
        pa_s = p.sbuf("ln_pas", (128, T), F32, stack=ph)
        pb_s = p.sbuf("ln_pbs", (128, T), F32, stack=ph)
        onesb = p.sbuf("ln_onesb", (128, 1), BF16, stack=ph)
        ps = p.psum("ln_ps", (33, T), F32, stack=ph)
        pbc = p.psum("ln_pbc", (128, 512), F32, stack=ph)

        def vinit(v):
            b.dve.inc(v.memset(onesb[:], 1.0))
        b.blk.vector(vinit)

        ld0 = b.ld.n
        act0 = b.act.n
        pe0 = b.pe.n

        def loads1(sy):
            for j in range(nt):
                if j >= 2:
                    b.pe.wait(sy, pe0 + j - 1)
                if KDEBUG_SERIAL_DMA:
                    b.ld.wait(sy)
                b.ld.dma(sy.dma_start(out=xt[:, j % 2, :],
                                      in_=x_hbm_fm[:, j, :]))
        b.blk.sync(loads1)

        def act1b(a):
            for j in range(nt):
                b.ld.wait(a, ld0 + 16 * (j + 1))
                if j >= 1:
                    b.pe.wait(a, pe0 + j)
                b.act.inc(a.activation(x2[:], xt[:, j % 2, :], AF.Square))
        b.blk.scalar(act1b)

        def pe1(t):
            b.dve.wait(t)
            for j in range(nt):
                b.ld.wait(t, ld0 + 16 * (j + 1))
                for f in range(NF):
                    sl = slice(f * 512, (f + 1) * 512)
                    t.matmul(ps[0:1, sl], b.onesf_col[:, 0:1], xt[:, j % 2, sl],
                             start=(j == 0), stop=(j == nt - 1))
                b.act.wait(t, act0 + j + 1)
                for f in range(NF):
                    sl = slice(f * 512, (f + 1) * 512)
                    mm = t.matmul(ps[32:33, sl], onesb[:], x2[:, sl],
                                  start=(j == 0), stop=(j == nt - 1))
                b.pe.inc(mm)
        b.blk.tensor(pe1)
        b.barrier()

        def dve1(v):
            v.tensor_scalar_mul(r_mean[:], ps[0:1, :], 1.0 / nD)
            v.tensor_scalar_mul(r_msq[:], ps[32:33, :], 1.0 / nD)
            v.tensor_mul(r_a[:], r_mean[:], r_mean[:])
            b.dve.inc(v.tensor_sub(r_a[:], r_msq[:], r_a[:]))
        b.blk.vector(dve1)

        def dve1b(v):
            b.dve.inc(v.tensor_scalar_add(r_a[:], r_a[:], float(eps)))
        b.blk.vector(dve1b)

        def act2(a):
            b.dve.wait(a)
            a.activation(r_a[:], r_a[:], AF.Ln)
            b.act.inc(a.activation(r_a[:], r_a[:], AF.Exp, scale=-0.5))
        b.blk.scalar(act2)

        def dve2(v):
            b.act.wait(v)
            v.tensor_mul(r_b[:], r_mean[:], r_a[:])
            b.dve.inc(v.tensor_scalar_mul(r_b[:], r_b[:], -1.0))
        b.blk.vector(dve2)

        pe2_0 = b.pe.n
        dve3_0 = b.dve.n

        def pe2(t):
            b.dve.wait(t, dve3_0)
            for i in range(2 * NF):
                f, which = i // 2, i % 2
                sl = slice(f * 512, (f + 1) * 512)
                if i >= 1:
                    b.dve.wait(t, dve3_0 + i)
                b.pe.inc(t.matmul(pbc[:], b.ones_row_f[:],
                                  (r_a if which == 0 else r_b)[:, sl],
                                  start=True, stop=True))
        b.blk.tensor(pe2)

        def dve3(v):
            for i in range(2 * NF):
                f, which = i // 2, i % 2
                sl = slice(f * 512, (f + 1) * 512)
                b.pe.wait(v, pe2_0 + i + 1)
                dst = pa_s if which == 0 else pb_s
                b.dve.inc(v.tensor_copy(dst[:, sl], pbc[:]))
        b.blk.vector(dve3)
        b.barrier()

        ld1 = b.ld.n
        dve4_0 = b.dve.n
        kper = 5 if hilo_gamma is not None else (3 if keep_x is not None else 2)

        def loads2(sy):
            for j in range(nt):
                if j >= 2:
                    b.dve.wait(sy, dve4_0 + kper * (j - 1))
                if KDEBUG_SERIAL_DMA:
                    b.ld.wait(sy)
                b.ld.dma(sy.dma_start(out=xt[:, j % 2, :],
                                      in_=x_hbm_fm[:, j, :]))
        b.blk.sync(loads2)

        if hilo_gamma is None:
            def dve4(v):
                for j in range(nt):
                    b.ld.wait(v, ld1 + 16 * (j + 1))
                    if keep_x is not None:
                        v.tensor_copy(keep_x[:, j, :], xt[:, j % 2, :])
                    v.tensor_mul(xn_fm[:, j, :], xt[:, j % 2, :], pa_s[:])
                    b.dve.inc(v.tensor_add(xn_fm[:, j, :], xn_fm[:, j, :],
                                           pb_s[:]),
                              3 if keep_x is not None else 2)
            b.blk.vector(dve4)
        else:
            hi, lo, gcol = hilo_gamma
            tf = p.sbuf("ln_tf", (128, T), F32, stack=ph)

            def dve4(v):
                for j in range(nt):
                    b.ld.wait(v, ld1 + 16 * (j + 1))
                    v.tensor_mul(tf[:], xt[:, j % 2, :], pa_s[:])
                    v.tensor_add(tf[:], tf[:], pb_s[:])
                    v.tensor_scalar(tf[:], tf[:], gcol[:, j:j + 1], None,
                                    ALU.mult)
                    v.tensor_copy(hi[:, j, :], tf[:])
                    b.dve.inc(v.tensor_sub(lo[:, j, :], tf[:], hi[:, j, :]),
                              kper)
            b.blk.vector(dve4)
        b.barrier()



def build_l1_mixer(debug=False, upto=99, bb=None):
    H, DH = 16, 64
    SCALE = DH ** -0.5
    b = bb if bb is not None else B("l1mix")
    p = b.p

    t_x = p.din("xT", (D, T))
    t_qkvwT = p.din("qkv_wT", (D, 3 * D))
    t_qnw = p.din("q_nw", (DH, 1))
    t_qnb = p.din("q_nb", (DH, 1))
    t_knw = p.din("k_nw", (DH, 1))
    t_knb = p.din("k_nb", (DH, 1))
    t_projwT = p.din("proj_wT", (D, D))
    t_projb = p.din("proj_b", (128, 8))
    t_xn = p.dout("xnT", (D, T), BF16)
    t_s = p.dout("sT", (D, T), BF16)

    xn = p.sbuf("xn", (128, 8, T), BF16)
    qnw = p.sbuf("qnw", (128, 2), F32)
    knw = p.sbuf("knw", (128, 2), F32)
    projb = p.sbuf("projb", (128, 8), F32)
    b.init_consts()
    np_blk2T = np.zeros((2, 128), np.float32)
    np_blk2T[0, 0:64] = 1.0
    np_blk2T[1, 64:128] = 1.0
    np_blk2 = np_blk2T.T.copy()
    np_sel0 = np.zeros((2, 128), np.float32)
    np_sel0[0] = 1.0
    np_sel1 = np.zeros((2, 128), np.float32)
    np_sel1[1] = 1.0
    blk2T = b.const("c_blk2T", np_blk2T)
    blk2 = b.const("c_blk2", np_blk2, sbuf_dt=BF16)
    sel = [b.const("c_sel0", np_sel0, sbuf_dt=BF16),
           b.const("c_sel1", np_sel1, sbuf_dt=BF16)]

    b.load(qnw[0:64, 0:1], t_qnw)
    b.load(qnw[64:128, 0:1], t_qnw)
    b.load(qnw[0:64, 1:2], t_qnb)
    b.load(qnw[64:128, 1:2], t_qnb)
    b.load(knw[0:64, 0:1], t_knw)
    b.load(knw[64:128, 0:1], t_knw)
    b.load(knw[0:64, 1:2], t_knb)
    b.load(knw[64:128, 1:2], t_knb)
    b.load(projb[:], t_projb)
    b.barrier()

    ln_fm_stream(b, fm_ap(t_x), xn)
    b.store(fm_ap(t_xn), xn[:])


    qf = p.sbuf("qf", (128, 8, T), BF16)
    kf = p.sbuf("kf", (128, 8, T), BF16)
    vt = p.sbuf("vt", (128, 16, H * 65), BF16)
    yn = p.sbuf("yn", (128, 8, T), BF16)
    vt_h = vt[:].rearrange("p a (h c) -> p a h c", h=H)

    def cinit2(v):
        b.dve.inc(v.memset(vt_h[:, :, :, 64:65], 1.0))
    b.blk.vector(cinit2)
    b.barrier()

    def cp_ev(dst):
        def w(eng, rt, ft, ps):
            return eng.tensor_copy(dst[:, rt, ft * 512:(ft + 1) * 512], ps[:])
        return w

    with ExitStack() as sB:
        qkwT = p.sbuf("qkwT", (128, 8, 2 * D), BF16, stack=sB)
        b.load(qkwT[:], fm_ap(t_qkvwT)[:, :, 0:2 * D], cast=True)
        b.barrier()
        emit_mm_tiled(b, None, cp_ev(qf),
                      lambda kt, rt: qkwT[:, kt, rt * 128:(rt + 1) * 128],
                      lambda kt, ft: xn[:, kt, ft * 512:(ft + 1) * 512],
                      8, D, T)
        emit_mm_tiled(b, None, cp_ev(kf),
                      lambda kt, rt: qkwT[:, kt, D + rt * 128:
                                          D + (rt + 1) * 128],
                      lambda kt, ft: xn[:, kt, ft * 512:(ft + 1) * 512],
                      8, D, T)

    with ExitStack() as sB2:
        vwT = p.sbuf("vwT", (128, 8, D), BF16, stack=sB2)
        b.load(vwT[:], fm_ap(t_qkvwT)[:, :, 2 * D:3 * D], cast=True)
        b.barrier()

        def vev(eng, rt, ft, ps):
            return eng.tensor_copy(
                vt_h[:, rt, ft * 8:(ft + 1) * 8, 0:64],
                ps[:].rearrange("p (h c) -> p h c", h=8))
        emit_mm_tiled(b, None, vev,
                      lambda kt, rt: xn[:, kt, rt * 128:(rt + 1) * 128],
                      lambda kt, ft: vwT[:, kt, ft * 512:(ft + 1) * 512],
                      8, T, D)

    def qk_ln_full(xf, gb, is_k):
        with ExitStack() as ph:
            x2 = p.sbuf("qkln_x2", (128, T), BF16, stack=ph)
            sums = p.sbuf("qkln_sums", (2, T), F32, stack=ph)
            sqs = p.sbuf("qkln_sqs", (2, T), F32, stack=ph)
            rA = p.sbuf("qkln_rA", (2, T), F32, stack=ph)
            rB = p.sbuf("qkln_rB", (2, T), F32, stack=ph)
            pss = p.psum("qkln_ps", (2, 512), F32, stack=ph)
            pn2 = p.psum("qkln_pn", (2, 512), F32, stack=ph)
            pa = p.psum("qkln_pa", (128, 512), F32, stack=ph)
            pb = p.psum("qkln_pb", (128, 512), F32, stack=ph)
            kmcol = p.sbuf("qkln_km", (128, 1), F32, stack=ph)
            for j in range(8):
                def act1(a, j=j):
                    b.act.inc(a.activation(x2[:], xf[:, j, :], AF.Square))
                b.blk.scalar(act1)

                pe0 = b.pe.n
                dve0 = b.dve.n

                def pe1(t, j=j, pe0=pe0, dve0=dve0):
                    b.act.wait(t)
                    for f in range(NF):
                        sl = slice(f * 512, (f + 1) * 512)
                        if f >= 1:
                            b.dve.wait(t, dve0 + 2 * f)
                        t.matmul(pss[:], blk2[:], xf[:, j, sl], start=True,
                                 stop=True)
                        b.pe.inc(t.matmul(pn2[:], blk2[:], x2[:, sl],
                                          start=True, stop=True))
                b.blk.tensor(pe1)

                def dve1(v, pe0=pe0):
                    for f in range(NF):
                        sl = slice(f * 512, (f + 1) * 512)
                        b.pe.wait(v, pe0 + f + 1)
                        b.dve.inc(v.tensor_copy(sums[:, sl], pss[:]))
                        b.dve.inc(v.tensor_copy(sqs[:, sl], pn2[:]))
                b.blk.vector(dve1)

                def dve2(v):
                    v.tensor_scalar_mul(sums[:], sums[:], 1.0 / DH)
                    v.tensor_scalar_mul(sqs[:], sqs[:], 1.0 / DH)
                    v.tensor_mul(rA[:], sums[:], sums[:])
                    b.dve.inc(v.tensor_sub(rA[:], sqs[:], rA[:]))
                b.blk.vector(dve2)

                def dve2b(v):
                    b.dve.inc(v.tensor_scalar_add(rA[:], rA[:],
                                                  float(LN_EPS)))
                b.blk.vector(dve2b)

                def act2(a):
                    b.dve.wait(a)
                    a.activation(rA[:], rA[:], AF.Ln)
                    b.act.inc(a.activation(rA[:], rA[:], AF.Exp, scale=-0.5))
                b.blk.scalar(act2)

                def dve3(v):
                    b.act.wait(v)
                    v.tensor_mul(rB[:], sums[:], rA[:])
                    b.dve.inc(v.tensor_scalar_mul(rB[:], rB[:], -1.0))
                b.blk.vector(dve3)

                pe2_0 = b.pe.n
                dve4_0 = b.dve.n

                def pe2(t, pe2_0=pe2_0, dve4_0=dve4_0):
                    b.dve.wait(t, dve4_0)
                    for f in range(NF):
                        sl = slice(f * 512, (f + 1) * 512)
                        if f >= 1:
                            b.dve.wait(t, dve4_0 + 3 * f)
                        t.matmul(pa[:], blk2T[:], rA[:, sl], start=True,
                                 stop=True)
                        b.pe.inc(t.matmul(pb[:], blk2T[:], rB[:, sl],
                                          start=True, stop=True))
                b.blk.tensor(pe2)

                def dve4(v, j=j, pe2_0=pe2_0, gb=gb):
                    for f in range(NF):
                        sl = slice(f * 512, (f + 1) * 512)
                        b.pe.wait(v, pe2_0 + f + 1)
                        v.tensor_mul(xf[:, j, sl], xf[:, j, sl], pa[:])
                        v.tensor_add(xf[:, j, sl], xf[:, j, sl], pb[:])
                        b.dve.inc(v.tensor_scalar(xf[:, j, sl], xf[:, j, sl],
                                                  gb[:, 0:1], gb[:, 1:2],
                                                  ALU.mult, ALU.add), 3)
                b.blk.vector(dve4)
                b.barrier()

                if is_k:
                    def dve6(v, j=j):
                        b.dve.inc(v.tensor_reduce(kmcol[:], xf[:, j, :],
                                                   AX.X, ALU.add))
                        b.dve.wait(v)
                        b.dve.inc(v.tensor_scalar_mul(kmcol[:], kmcol[:],
                                                      1.0 / T))
                        b.dve.wait(v)
                        b.dve.inc(v.tensor_scalar(xf[:, j, :], xf[:, j, :],
                                                  kmcol[:], None,
                                                  ALU.subtract))
                    b.blk.vector(dve6)
                    b.barrier()
    qk_ln_full(qf, qnw, False)
    qk_ln_full(kf, knw, True)

    if upto <= 1:
        t_dq = p.dout("dbg_q", (D, T), BF16)
        t_dk = p.dout("dbg_k", (D, T), BF16)
        b.store(fm_ap(t_dq), qf[:])
        b.store(fm_ap(t_dk), kf[:])
        b.finish()
        return b

    with ExitStack() as sD:
        mneg = p.sbuf("mneg", (2, T), BF16, stack=sD)
        x2m = p.sbuf("mh_x2", (128, T), BF16, stack=sD)
        qn2 = p.sbuf("mh_qn2", (2, T), F32, stack=sD)
        kn2 = p.sbuf("mh_kn2", (2, T), F32, stack=sD)
        kmx = p.sbuf("mh_kmx", (2, 1), F32, stack=sD)
        pS = [p.psum("at_ps", (128, 512), F32, stack=sD) for _ in range(2)]
        pY = p.psum("at_py", (65, T), F32, stack=sD)
        pR = p.psum("at_pr", (64, 512), F32, stack=sD)
        es = [p.sbuf("at_es", (128, 512), BF16, stack=sD) for _ in range(2)]
        rrow = p.sbuf("at_rrow", (1, T), F32, stack=sD)
        rbc = p.sbuf("at_rbc", (64, 512), F32, stack=sD)

        for jh in range(8):
            for which, (src, dst2) in enumerate(((qf, qn2), (kf, kn2))):
                def act1(a, jh=jh, src=src):
                    b.act.inc(a.activation(x2m[:], src[:, jh, :], AF.Square))
                b.blk.scalar(act1)
                pe0 = b.pe.n
                dve0 = b.dve.n

                def pe1(t, pe0=pe0, dve0=dve0):
                    b.act.wait(t)
                    for f in range(NF):
                        if f >= 1:
                            b.dve.wait(t, dve0 + f)
                        b.pe.inc(t.matmul(pS[0][0:2, :], blk2[:],
                                          x2m[:, f * 512:(f + 1) * 512],
                                          start=True, stop=True))
                b.blk.tensor(pe1)

                def dve1(v, pe0=pe0, dst2=dst2):
                    for f in range(NF):
                        b.pe.wait(v, pe0 + f + 1)
                        b.dve.inc(v.tensor_copy(
                            dst2[0:2, f * 512:(f + 1) * 512], pS[0][0:2, :]))
                b.blk.vector(dve1)
                b.barrier()

            def dve3(v):
                b.dve.inc(v.tensor_reduce(kmx[:], kn2[:], AX.X, ALU.max))
            b.blk.vector(dve3)

            def act3(a):
                b.dve.wait(a)
                b.act.inc(a.activation(kn2[:], qn2[:], AF.Sqrt, scale=kmx[:]))
            b.blk.scalar(act3)

            def dve4(v):
                b.act.wait(v)
                b.dve.inc(v.tensor_scalar_mul(mneg[:], kn2[:], -1.0))
            b.blk.vector(dve4)
            b.barrier()

            for sh in range(2):
                h = 2 * jh + sh
                base = 64 * sh
                it = [(tk, ff) for tk in range(16) for ff in range(4)]
                pe0, act0 = b.pe.n, b.act.n

                def pe_a(t, jh=jh, sh=sh, base=base, it=it, pe0=pe0,
                         act0=act0, h=h):
                    def y_mm(pi):
                        ptk, pff = it[pi]
                        b.act.wait(t, act0 + pi + 1)
                        b.pe.inc(t.matmul(
                            pY[:, pff * 512:(pff + 1) * 512],
                            vt_h[:, ptk, h, :], es[pi % 2][:],
                            start=(ptk == 0), stop=(ptk == 15)))
                    for i, (tk, ff) in enumerate(it):
                        ps = pS[i % 2]
                        if i >= 2:
                            b.act.wait(t, act0 + i - 1)
                        t.matmul(ps[:], kf[base:base + 64, jh,
                                           tk * 128:(tk + 1) * 128],
                                 qf[base:base + 64, jh,
                                    ff * 512:(ff + 1) * 512],
                                 start=True, stop=False)
                        b.pe.inc(t.matmul(ps[:], sel[sh][:],
                                          mneg[:, ff * 512:(ff + 1) * 512],
                                          start=False, stop=True))
                        if i >= 1:
                            y_mm(i - 1)
                    y_mm(len(it) - 1)

                def act_a(a, it=it, pe0=pe0, act0=act0):
                    for i in range(len(it)):
                        cnt = pe0 + (2 * i if i >= 1 else 1)
                        b.pe.wait(a, cnt)
                        b.act.inc(a.activation(es[i % 2][:], pS[i % 2][:],
                                               AF.Exp, scale=float(SCALE)))
                b.blk.tensor(pe_a)
                b.blk.scalar(act_a)
                b.barrier()

                def act_n(a):
                    a.activation(rrow[:], pY[64:65, :], AF.Ln)
                    b.act.inc(a.activation(rrow[:], rrow[:], AF.Exp,
                                           scale=-1.0))
                b.blk.scalar(act_n)

                pe1_0 = b.pe.n
                dve1_0 = b.dve.n

                def pe_n(t, pe1_0=pe1_0, dve1_0=dve1_0):
                    b.act.wait(t)
                    for f in range(NF):
                        if f >= 1:
                            b.dve.wait(t, dve1_0 + 2 * f)
                        b.pe.inc(t.matmul(pR[:], b.ones_row_f[0:1, 0:64],
                                          rrow[0:1, f * 512:(f + 1) * 512],
                                          start=True, stop=True))

                def dve_n(v, jh=jh, base=base, pe1_0=pe1_0):
                    for f in range(NF):
                        sl = slice(f * 512, (f + 1) * 512)
                        b.pe.wait(v, pe1_0 + f + 1)
                        v.tensor_copy(rbc[:], pR[:])
                        b.dve.inc(v.tensor_mul(yn[base:base + 64, jh, sl],
                                               pY[0:64, sl], rbc[:]), 2)
                b.blk.tensor(pe_n)
                b.blk.vector(dve_n)
                b.barrier()

    if debug:
        t_dq = p.dout("dbg_q", (D, T), BF16)
        t_dk = p.dout("dbg_k", (D, T), BF16)
        t_dyn = p.dout("dbg_yn", (D, T), BF16)
        t_dvt = p.dout("dbg_vt", (128, 16 * H * 65), BF16)
        b.store(fm_ap(t_dq), qf[:])
        b.store(fm_ap(t_dk), kf[:])
        b.store(fm_ap(t_dyn), yn[:])
        b.store(t_dvt.rearrange("p (a c) -> p a c", a=16), vt[:])
        b.barrier()

    with ExitStack() as sE:
        projwT = p.sbuf("projwT", (128, 8, D), BF16, stack=sE)
        b.load(projwT[:], fm_ap(t_projwT), cast=True)
        b.barrier()
        sfm = xn

        def pev(eng, rt, ft, ps):
            return eng.tensor_scalar_add(sfm[:, rt, ft * 512:(ft + 1) * 512],
                                         ps[:], projb[:, rt:rt + 1])
        emit_mm_tiled(b, None, pev,
                      lambda kt, rt: projwT[:, kt, rt * 128:(rt + 1) * 128],
                      lambda kt, ft: yn[:, kt, ft * 512:(ft + 1) * 512],
                      8, D, T, evac="dve")
        b.store(fm_ap(t_s), sfm[:])
    if bb is None:
        b.finish()
    return b



def kernel(**inputs):
    raise NotImplementedError


def ln_stats_resident(b, x_fm, nt, pa_s, pb_s, eps=LN_EPS):
    p = b.p
    nD = nt * 128
    with ExitStack() as ph:
        x2 = p.sbuf("lnr_x2", (128, T), BF16, stack=ph)
        r_mean = p.sbuf("lnr_rm", (1, T), F32, stack=ph)
        r_msq = p.sbuf("lnr_rq", (1, T), F32, stack=ph)
        r_a = p.sbuf("lnr_ra", (1, T), F32, stack=ph)
        r_b = p.sbuf("lnr_rb", (1, T), F32, stack=ph)
        onesb = p.sbuf("lnr_ones", (128, 1), BF16, stack=ph)
        ps = p.psum("lnr_ps", (33, T), F32, stack=ph)
        pbc = p.psum("lnr_pbc", (128, 512), F32, stack=ph)

        def vinit(v):
            b.dve.inc(v.memset(onesb[:], 1.0))
        b.blk.vector(vinit)

        act0 = b.act.n

        def act1(a):
            for j in range(nt):
                if j >= 1:
                    b.pe.wait(a, pe0 + j)
                b.act.inc(a.activation(x2[:], x_fm[:, j, :], AF.Square))
        pe0 = b.pe.n

        def pe1(t):
            b.dve.wait(t)
            for j in range(nt):
                for f in range(NF):
                    sl = slice(f * 512, (f + 1) * 512)
                    t.matmul(ps[0:1, sl], onesb[:], x_fm[:, j, sl],
                             start=(j == 0), stop=(j == nt - 1))
                b.act.wait(t, act0 + j + 1)
                mm = None
                for f in range(NF):
                    sl = slice(f * 512, (f + 1) * 512)
                    mm = t.matmul(ps[32:33, sl], onesb[:], x2[:, sl],
                                  start=(j == 0), stop=(j == nt - 1))
                b.pe.inc(mm)
        b.blk.scalar(act1)
        b.blk.tensor(pe1)
        b.barrier()

        def dve1(v):
            v.tensor_scalar_mul(r_mean[:], ps[0:1, :], 1.0 / nD)
            v.tensor_scalar_mul(r_msq[:], ps[32:33, :], 1.0 / nD)
            v.tensor_mul(r_a[:], r_mean[:], r_mean[:])
            v.tensor_sub(r_a[:], r_msq[:], r_a[:])
            b.dve.inc(v.tensor_scalar_add(r_a[:], r_a[:], float(eps)))
        b.blk.vector(dve1)

        def act2(a):
            b.dve.wait(a)
            a.activation(r_a[:], r_a[:], AF.Ln)
            b.act.inc(a.activation(r_a[:], r_a[:], AF.Exp, scale=-0.5))
        b.blk.scalar(act2)

        def dve2(v):
            b.act.wait(v)
            v.tensor_mul(r_b[:], r_mean[:], r_a[:])
            b.dve.inc(v.tensor_scalar_mul(r_b[:], r_b[:], -1.0))
        b.blk.vector(dve2)

        pe2_0 = b.pe.n
        dve3_0 = b.dve.n

        def pe2(t):
            b.dve.wait(t, dve3_0)
            for i in range(2 * NF):
                f, which = i // 2, i % 2
                sl = slice(f * 512, (f + 1) * 512)
                if i >= 1:
                    b.dve.wait(t, dve3_0 + i)
                b.pe.inc(t.matmul(pbc[:], b.ones_row_f[:],
                                  (r_a if which == 0 else r_b)[:, sl],
                                  start=True, stop=True))
        b.blk.tensor(pe2)

        def dve3(v):
            for i in range(2 * NF):
                f, which = i // 2, i % 2
                sl = slice(f * 512, (f + 1) * 512)
                b.pe.wait(v, pe2_0 + i + 1)
                dst = pa_s if which == 0 else pb_s
                b.dve.inc(v.tensor_copy(dst[:, sl], pbc[:]))
        b.blk.vector(dve3)
        b.barrier()



def build_l2_ffn(bb=None):
    b = bb if bb is not None else B("l2ffn")
    p = b.p
    FH = 4096

    t_x = p.din("xT", (D, T))
    t_xn = p.din("xnT", (D, T), BF16)
    t_s = p.din("sT", (D, T), BF16)
    t_q1 = p.din("fc1_q", (D, FH), I32)
    t_s1 = p.din("fc1_scales", (8, FH))
    t_b1 = p.din("fc1_bias", (128, 32))
    t_q2 = p.din("fc2_q", (FH, D), I32)
    t_s2 = p.din("fc2_scales", (32, D))
    t_b2 = p.din("fc2_bias", (128, 8))
    t_lng = p.din("ln_g", (128, 8))
    t_lnb = p.din("ln_b", (128, 8))
    t_out = p.dout("xoutT", (D, T))

    b.init_consts()

    xn = p.sbuf("xn", (128, 8, T), BF16)
    h2 = p.sbuf("h2", (128, 8, T), BF16)
    b1c = p.sbuf("b1c", (128, 32), F32)
    b2c = p.sbuf("b2c", (128, 8), F32)
    lng = p.sbuf("lng", (128, 8), F32)
    lnb = p.sbuf("lnb", (128, 8), F32)

    b.load(xn[:], fm_ap(t_xn))
    b.load(b1c[:], t_b1)
    b.load(b2c[:], t_b2)
    b.load(lng[:], t_lng)
    b.load(lnb[:], t_lnb)
    b.barrier()

    with ExitStack() as ph:
        pa_s = p.sbuf("pa_s", (128, T), F32, stack=ph)
        pb_s = p.sbuf("pb_s", (128, T), F32, stack=ph)
        ln_stats_resident(b, xn, 8, pa_s, pb_s)

        def dveh(v):
            for j in range(8):
                v.tensor_mul(h2[:, j, :], xn[:, j, :], pa_s[:])
                v.tensor_add(h2[:, j, :], h2[:, j, :], pb_s[:])
                b.dve.inc(v.tensor_scalar(h2[:, j, :], h2[:, j, :],
                                          lng[:, j:j + 1], lnb[:, j:j + 1],
                                          ALU.mult, ALU.add), 3)
        b.blk.vector(dveh)
        b.barrier()

    with ExitStack() as ph0:
        b2h = p.sbuf("b2h", (128, 8), F32, stack=ph0)

        def dveb(v):
            b.dve.inc(v.tensor_scalar_mul(b2h[:], b2c[:], 0.5))
            b.dve.wait(v)
            for j in range(8):
                b.dve.inc(v.tensor_scalar_add(xn[:, j, :], xn[:, j, :],
                                              b2h[:, j:j + 1]))
        b.blk.vector(dveb)
        b.barrier()

        t_ytmp = p.dtmp("l2f_ytmp", (FH, T), BF16)

        def dq(w, t_sc, nkt, ncols):
            with ExitStack() as ph:
                sbc = p.sbuf("dq_sbc", (128, 512), BF16, stack=ph)
                srow = p.sbuf("dq_srow", (1, 512), F32, stack=ph)
                pq = p.psum("dq_ps", (128, 512), F32, stack=ph)
                for kt in range(nkt):
                    for f in range(ncols // 512):
                        sl = slice(f * 512, (f + 1) * 512)
                        b.load(srow[:], t_sc[kt:kt + 1, sl])
                        b.barrier()

                        def pe1(t):
                            b.pe.inc(t.matmul(pq[:], b.ones_row_f[:],
                                              srow[:], start=True, stop=True))
                        b.blk.tensor(pe1)

                        def dve1(v, kt=kt, sl=sl):
                            b.pe.wait(v)
                            b.dve.inc(v.tensor_copy(sbc[:], pq[:]))
                            v.tensor_scalar_add(w[:, kt, sl], w[:, kt, sl],
                                                -128.0)
                            b.dve.inc(v.tensor_mul(w[:, kt, sl], w[:, kt, sl],
                                                   sbc[:]), 2)
                        b.blk.vector(dve1)
                        b.barrier()

        TC = 512
        with ExitStack() as ph:
            w1 = p.sbuf("w1", (128, 8, FH), BF16, stack=ph)
            b.load(w1[:], fm_ap(t_q1), cast=True)
            b.barrier()
            dq(w1, t_s1, 8, FH)
            ych = p.sbuf("ych", (128, 32, TC), BF16, stack=ph)
            py = [p.psum("l2f_py", (128, 512), F32, stack=ph)
                  for _ in range(2)]
            for tc in range(T // TC):
                tsl = slice(tc * TC, (tc + 1) * TC)
                pe0 = b.pe.n
                act0 = b.act.n

                def pe1(t, tsl=tsl, pe0=pe0, act0=act0):
                    for nt_ in range(32):
                        ps = py[nt_ % 2]
                        if nt_ >= 2:
                            b.act.wait(t, act0 + nt_ - 1)
                        mm = None
                        for kt in range(8):
                            mm = t.matmul(ps[:],
                                          w1[:, kt, nt_ * 128:(nt_ + 1) * 128],
                                          h2[:, kt, tsl],
                                          start=(kt == 0), stop=(kt == 7))
                        b.pe.inc(mm)
                b.blk.tensor(pe1)

                def act1(a, pe0=pe0):
                    for nt_ in range(32):
                        b.pe.wait(a, pe0 + nt_ + 1)
                        b.act.inc(a.activation(ych[:, nt_, :],
                                               py[nt_ % 2][:], AF.Silu,
                                               bias=b1c[:, nt_:nt_ + 1]))
                b.blk.scalar(act1)
                b.barrier()
                b.store(fm_ap(t_ytmp)[:, :, tsl], ych[:])
                b.barrier()

        with ExitStack() as ph:
            w2 = p.sbuf("w2", (128, 32, D), BF16, stack=ph)
            b.load(w2[:], fm_ap(t_q2), cast=True)
            b.barrier()
            dq(w2, t_s2, 32, D)
            TC2 = 256
            ych2 = p.sbuf("ych2", (128, 32, TC2), BF16, stack=ph)
            xch = p.sbuf("xch", (128, 8, TC2), F32, stack=ph)
            sch = p.sbuf("sch", (128, 8, TC2), BF16, stack=ph)
            och = p.sbuf("och", (128, 8, TC2), F32, stack=ph)
            gch = p.sbuf("gch", (128, TC2), BF16, stack=ph)
            tch = p.sbuf("tch", (128, TC2), F32, stack=ph)
            po = p.psum("l2f_po", (128, 8, TC2), F32, stack=ph)

            for tc in range(T // TC2):
                tsl = slice(tc * TC2, (tc + 1) * TC2)
                b.load(ych2[:], fm_ap(t_ytmp)[:, :, tsl])
                b.load(xch[:], fm_ap(t_x)[:, :, tsl])
                b.load(sch[:], fm_ap(t_s)[:, :, tsl])
                b.barrier()

                def pe2(t):
                    for dt_ in range(8):
                        mm = None
                        for kt in range(32):
                            mm = t.matmul(po[:, dt_, :],
                                          w2[:, kt, dt_ * 128:(dt_ + 1) * 128],
                                          ych2[:, kt, :],
                                          start=(kt == 0), stop=(kt == 31))
                        b.pe.inc(mm)
                b.blk.tensor(pe2)
                b.barrier()

                for dt_ in range(8):
                    def dvec(v, dt_=dt_, tsl=tsl):
                        b.dve.inc(v.scalar_tensor_tensor(
                            tch[:], po[:, dt_, :], 0.5, xn[:, dt_, tsl],
                            ALU.mult, ALU.add))
                    b.blk.vector(dvec)

                    def actc(a, dt_=dt_):
                        b.dve.wait(a)
                        b.act.inc(a.activation(gch[:], tch[:], AF.Silu))
                    b.blk.scalar(actc)

                    def dvec2(v, dt_=dt_):
                        b.act.wait(v)
                        v.tensor_mul(och[:, dt_, :], sch[:, dt_, :], gch[:])
                        b.dve.inc(v.tensor_add(och[:, dt_, :], och[:, dt_, :],
                                               xch[:, dt_, :]), 2)
                    b.blk.vector(dvec2)
                b.barrier()
                b.store(fm_ap(t_out)[:, :, tsl], och[:])
                b.barrier()

    if bb is None:
        b.finish()
    return b


FP4_THRESH = [0.25, 0.75, 1.25, 1.75, 2.5, 2.5, 3.5, 3.5, 5.0, 5.0, 5.0, 5.0]


def emit_fp4_quant(b, x_ap, xq_ap, ncols, scr, in_dt=F32):
    nb = ncols // 16
    bm, inv, axn, acc, sgn = (scr["bm"], scr["inv"], scr["axn"], scr["acc"],
                              scr["sgn"])

    def dve1(v):
        b.dve.inc(v.tensor_reduce(bm[:, 0:nb], x_ap.rearrange(
            "p (a c) -> p a c", c=16), AX.X, ALU.max,
            apply_absolute_value=True))
        v.tensor_scalar_mul(bm[:, 0:nb], bm[:, 0:nb], 1.0 / 6.0)
        b.dve.inc(v.tensor_scalar_max(bm[:, 0:nb], bm[:, 0:nb], 1e-12))
    b.blk.vector(dve1)

    def act1(a):
        b.dve.wait(a)
        a.activation(inv[:, 0:nb], bm[:, 0:nb], AF.Ln)
        a.activation(inv[:, 0:nb], inv[:, 0:nb], AF.Exp, scale=-1.0)
        b.act.inc(a.activation(sgn[:, 0:ncols], x_ap, AF.Sign))
    b.blk.scalar(act1)

    def dve2(v):
        b.act.wait(v)
        invb = inv[:, 0:nb].rearrange("p (a u) -> p a u", u=1).broadcast_to(
            (128, nb, 16))
        v.tensor_tensor(axn[:, 0:ncols].rearrange("p (a c) -> p a c", c=16),
                        x_ap.rearrange("p (a c) -> p a c", c=16), invb,
                        ALU.mult)
        b.dve.inc(v.nop())
    b.blk.vector(dve2)

    def act2(a):
        b.dve.wait(a)
        b.act.inc(a.activation(axn[:, 0:ncols], axn[:, 0:ncols], AF.Abs))
    b.blk.scalar(act2)

    def dve3(v):
        b.act.wait(v)
        v.tensor_scalar(acc[:, 0:ncols], axn[:, 0:ncols], FP4_THRESH[0], None,
                        ALU.is_gt)
        for th in FP4_THRESH[1:]:
            v.scalar_tensor_tensor(acc[:, 0:ncols], axn[:, 0:ncols], float(th),
                                   acc[:, 0:ncols], ALU.is_gt, ALU.add)
        v.tensor_scalar_mul(bm[:, 0:nb], bm[:, 0:nb], 0.5)
        b.dve.inc(v.tensor_mul(acc[:, 0:ncols], acc[:, 0:ncols],
                               sgn[:, 0:ncols]))
        b.dve.wait(v)
        bmb = bm[:, 0:nb].rearrange("p (a u) -> p a u", u=1).broadcast_to(
            (128, nb, 16))
        b.dve.inc(v.tensor_tensor(
            xq_ap.rearrange("p (a c) -> p a c", c=16),
            acc[:, 0:ncols].rearrange("p (a c) -> p a c", c=16), bmb,
            ALU.mult))
    b.blk.vector(dve3)


def mk_fp4_scratch(p, ncols, stack):
    return {
        "bm": p.sbuf("q_bm", (128, ncols // 16), F32, stack=stack),
        "inv": p.sbuf("q_inv", (128, ncols // 16), F32, stack=stack),
        "axn": p.sbuf("q_axn", (128, ncols), F32, stack=stack),
        "acc": p.sbuf("q_acc", (128, ncols), F32, stack=stack),
        "sgn": p.sbuf("q_sgn", (128, ncols), F32, stack=stack),
    }


def emit_transpose(b, src_ap_fn, dst_ap_fn, nrt, nct, ident, ptile):
    pe0 = b.pe.n
    dve0 = b.dve.n
    n = nrt * nct

    def pet(t):
        for k in range(n):
            i, j = k // nct, k % nct
            if k >= 1:
                b.dve.wait(t, dve0 + k)
            b.pe.inc(t.transpose(ptile[:], src_ap_fn(i, j), ident[:]))

    def dvet(v):
        for k in range(n):
            i, j = k // nct, k % nct
            b.pe.wait(v, pe0 + k + 1)
            b.dve.inc(v.tensor_copy(dst_ap_fn(j, i), ptile[:]))
    b.blk.tensor(pet)
    b.blk.vector(dvet)
    b.barrier()


def build_l1_ffn(bb=None):
    IH = 2816
    NI = IH // 128
    b = bb if bb is not None else B("l1ffn")
    p = b.p

    t_x = p.din("xT", (D, T))
    t_xn = p.din("xnT", (D, T), BF16)
    t_s = p.din("sT", (D, T), BF16)
    t_wg = p.din("w_gate", (IH, D))
    t_wu = p.din("w_up", (IH, D))
    t_wd = p.din("w_down", (D, IH))
    t_out = p.dout("xoutT", (D, T))
    t_hq = p.dtmp("l1f_hq", (T, IH), BF16)

    b.init_consts()
    ident = b.const("c_ident", np.eye(128, dtype=np.float32), sbuf_dt=BF16)

    xqT = p.sbuf("xqT", (128, 8, T), BF16)

    with ExitStack() as ph:
        xn = p.sbuf("xn", (128, 8, T), BF16, stack=ph)
        xtm = p.sbuf("xtm", (128, 16, D), BF16, stack=ph)
        ptile = p.psum("ptile", (128, 128), BF16, stack=ph)
        b.load(xn[:], fm_ap(t_xn))
        b.barrier()
        emit_transpose(b, lambda i, j: xn[:, i, j * 128:(j + 1) * 128],
                       lambda j, i: xtm[:, j, i * 128:(i + 1) * 128],
                       8, 16, ident, ptile)
        scr = mk_fp4_scratch(p, D, ph)
        for tt in range(16):
            emit_fp4_quant(b, xtm[:, tt, :], xtm[:, tt, :], D, scr)
            b.barrier()
        emit_transpose(b, lambda i, j: xtm[:, i, j * 128:(j + 1) * 128],
                       lambda j, i: xqT[:, j, i * 128:(i + 1) * 128],
                       16, 8, ident, ptile)

    def quant_weight_T(t_w, nrows_t, ncols, wqT):
        with ExitStack() as ph:
            wrow = p.sbuf("w_raw", (128, ncols), F32, stack=ph)
            wqb = p.sbuf("w_qb", (128, ncols), BF16, stack=ph)
            ptile = p.psum("ptile", (128, 128), BF16, stack=ph)
            scr = mk_fp4_scratch(p, ncols, ph)
            for rt in range(nrows_t):
                b.load(wrow[:], t_w.rearrange("(a p) c -> p a c", p=128)[:, rt, :])
                b.barrier()
                emit_fp4_quant(b, wrow[:], wqb[:], ncols, scr)
                b.barrier()
                emit_transpose(b, lambda i, j, : wqb[:, j * 128:(j + 1) * 128],
                               lambda j, i, rt=rt: wqT[:, j, rt * 128:(rt + 1) * 128],
                               1, ncols // 128, ident, ptile)

    with ExitStack() as phB:
        wgT = p.sbuf("wgT", (128, 8, IH), BF16, stack=phB)
        wuT = p.sbuf("wuT", (128, 8, IH), BF16, stack=phB)
        quant_weight_T(t_wg, NI, D, wgT)
        quant_weight_T(t_wu, NI, D, wuT)

        with ExitStack() as ph:
            gt = p.sbuf("gt", (128, IH), F32, stack=ph)
            ut = p.sbuf("ut", (128, IH), F32, stack=ph)
            ht = p.sbuf("ht", (128, IH), F32, stack=ph)
            hq = p.sbuf("hqt", (128, IH), BF16, stack=ph)
            pg = [p.psum("l1f_pg", (128, 512), F32, stack=ph) for _ in range(2)]
            scr = mk_fp4_scratch(p, IH, ph)
            NFI = IH // 256
            for tt in range(16):
                pe0 = b.pe.n
                dve0 = b.dve.n
                order = [(w, f) for w in range(2) for f in range(NFI)]

                def pe1(t, tt=tt, pe0=pe0, dve0=dve0, order=order):
                    for i, (w, f) in enumerate(order):
                        ps = pg[i % 2]
                        if i >= 2:
                            b.dve.wait(t, dve0 + i - 1)
                        mm = None
                        wsrc = wgT if w == 0 else wuT
                        for kt in range(8):
                            mm = t.matmul(
                                ps[:, 0:256],
                                xqT[:, kt, tt * 128:(tt + 1) * 128],
                                wsrc[:, kt, f * 256:(f + 1) * 256],
                                start=(kt == 0), stop=(kt == 7))
                        b.pe.inc(mm)
                b.blk.tensor(pe1)

                def dve1(v, pe0=pe0, order=order):
                    for i, (w, f) in enumerate(order):
                        b.pe.wait(v, pe0 + i + 1)
                        dst = gt if w == 0 else ut
                        b.dve.inc(v.tensor_copy(
                            dst[:, f * 256:(f + 1) * 256], pg[i % 2][:, 0:256]))
                b.blk.vector(dve1)
                b.barrier()

                def acth(a):
                    b.act.inc(a.activation(ht[:], gt[:], AF.Silu))
                b.blk.scalar(acth)

                def dveh(v):
                    b.act.wait(v)
                    b.dve.inc(v.tensor_mul(ht[:], ht[:], ut[:]))
                b.blk.vector(dveh)
                b.barrier()
                emit_fp4_quant(b, ht[:], hq[:], IH, scr)
                b.barrier()
                b.store(t_hq.rearrange("(a p) c -> p a c", p=128)[:, tt, :],
                        hq[:])
                b.barrier()

    with ExitStack() as phC:
        wdT = p.sbuf("wdT", (128, NI, D), BF16, stack=phC)
        quant_weight_T(t_wd, 8, IH, wdT)

        TC = 512
        with ExitStack() as ph:
            htm = p.sbuf("htm", (128, 4, IH), BF16, stack=ph)
            hqT = p.sbuf("hqT", (128, NI, TC), BF16, stack=ph)
            xch = p.sbuf("xch", (128, 8, TC), F32, stack=ph)
            sch = p.sbuf("sch", (128, 8, TC), BF16, stack=ph)
            och = p.sbuf("och", (128, 8, TC), F32, stack=ph)
            gch = p.sbuf("gch", (128, TC), BF16, stack=ph)
            ptile = p.psum("ptile", (128, 128), BF16, stack=ph)
            po = p.psum("l1f_po", (128, 4, TC), F32, stack=ph)
            for tc in range(T // TC):
                tsl = slice(tc * TC, (tc + 1) * TC)
                b.load(htm[:], t_hq.rearrange("(a p) c -> p a c", p=128)
                       [:, tc * 4:(tc + 1) * 4, :])
                b.load(xch[:], fm_ap(t_x)[:, :, tsl])
                b.load(sch[:], fm_ap(t_s)[:, :, tsl])
                b.barrier()
                emit_transpose(
                    b,
                    lambda i, j: htm[:, i, j * 128:(j + 1) * 128],
                    lambda j, i: hqT[:, j, i * 128:(i + 1) * 128],
                    4, NI, ident, ptile)

                for half in range(2):
                    def pe2(t, half=half):
                        for q_ in range(4):
                            dt_ = half * 4 + q_
                            mm = None
                            for kt in range(NI):
                                mm = t.matmul(
                                    po[:, q_, :],
                                    wdT[:, kt, dt_ * 128:(dt_ + 1) * 128],
                                    hqT[:, kt, :],
                                    start=(kt == 0), stop=(kt == NI - 1))
                            b.pe.inc(mm)
                    b.blk.tensor(pe2)
                    b.barrier()

                    for q_ in range(4):
                        dt_ = half * 4 + q_

                        def actc(a, q_=q_):
                            b.act.inc(a.activation(
                                gch[:], po[:, q_, :], AF.Silu))
                        b.blk.scalar(actc)

                        def dvec2(v, dt_=dt_):
                            b.act.wait(v)
                            v.tensor_mul(och[:, dt_, :], sch[:, dt_, :],
                                         gch[:])
                            b.dve.inc(v.tensor_add(och[:, dt_, :],
                                                   och[:, dt_, :],
                                                   xch[:, dt_, :]), 2)
                        b.blk.vector(dvec2)
                    b.barrier()
                b.store(fm_ap(t_out)[:, :, tsl], och[:])
                b.barrier()
    if bb is None:
        b.finish()
    return b


def build_l0_ffn(debug=False, bb=None):
    ET = 4096
    NET = ET // 128
    FH = 4096
    b = bb if bb is not None else B("l0ffn")
    p = b.p

    t_x = p.din("xT", (D, T))
    t_xn = p.din("xnT", (D, T), BF16)
    t_s = p.din("sT", (D, T), BF16)
    t_gx = p.din("gamma_x", (128, 8))
    t_gs = p.din("gamma_slot", (1, D))
    t_se = p.din("slot_embeds", (ET, D))
    t_w1 = p.din("w1", (D, FH))
    t_w2 = p.din("w2", (FH, D))
    t_out = p.dout("xoutT", (D, T))
    mk = p.dout if debug else p.dtmp
    t_lt = mk("l0f_lt", (ET, T))
    t_dt = mk("l0f_dt", (T, ET), BF16)
    t_eb = mk("l0f_eb", (ET, T), BF16)
    t_proc = mk("l0f_proc", (ET, D), BF16)

    b.init_consts()
    ident = b.const("c_ident", np.eye(128, dtype=np.float32), sbuf_dt=BF16)

    se_hbm = t_se.rearrange("(a p) c -> p a c", p=128)
    lt_hbm = t_lt.rearrange("(a p) c -> p a c", p=128)
    dt_hbm = t_dt.rearrange("(a p) c -> p a c", p=128)
    eb_hbm = t_eb.rearrange("(a p) c -> p a c", p=128)
    proc_hbm = t_proc.rearrange("(a p) c -> p a c", p=128)

    xrt = p.sbuf("xrt", (128, 16, D), BF16)
    rbc = p.sbuf("rbc", (128, T), F32)

    def bcast_row(src_row, dst_bc, ncols=T):
        with ExitStack() as ph:
            pbc = p.psum("bc_ps", (128, 512), F32, stack=ph)
            pe0 = b.pe.n
            dve0 = b.dve.n

            def pe(t):
                for f in range(ncols // 512):
                    if f >= 1:
                        b.dve.wait(t, dve0 + f)
                    b.pe.inc(t.matmul(pbc[:], b.ones_row_f[:],
                                      src_row[:, f * 512:(f + 1) * 512],
                                      start=True, stop=True))

            def dv(v):
                for f in range(ncols // 512):
                    b.pe.wait(v, pe0 + f + 1)
                    b.dve.inc(v.tensor_copy(
                        dst_bc[:, f * 512:(f + 1) * 512], pbc[:]))
            b.blk.tensor(pe)
            b.blk.vector(dv)
            b.barrier()

    phXP1 = ExitStack()
    xr_hi = p.sbuf("xr_hi", (128, 8, T), BF16, stack=phXP1)
    xr_lo = p.sbuf("xr_lo", (128, 8, T), BF16, stack=phXP1)
    with ExitStack() as ph0:
        gx = p.sbuf("gx", (128, 8), F32, stack=ph0)
        ptile = p.psum("ptileX", (128, 128), BF16, stack=ph0)
        b.load(gx[:], t_gx)
        b.barrier()
        ln_fm_stream(b, fm_ap(t_x), None, eps=0.0,
                     hilo_gamma=(xr_hi, xr_lo, gx))
        emit_transpose(b, lambda i, j: xr_hi[:, i, j * 128:(j + 1) * 128],
                       lambda j, i: xrt[:, j, i * 128:(i + 1) * 128],
                       8, 16, ident, ptile)

    with ExitStack() as p1:
        se = p.sbuf("se", (128, D), F32, stack=p1)
        sq_t = p.sbuf("sq_t", (128, D), F32, stack=p1)
        se_hi = p.sbuf("se_hi", (128, D), BF16, stack=p1)
        se_lo = p.sbuf("se_lo", (128, D), BF16, stack=p1)
        seT_hi = p.sbuf("seT_hi", (128, 8, 128), BF16, stack=p1)
        seT_lo = p.sbuf("seT_lo", (128, 8, 128), BF16, stack=p1)
        gsrow = p.sbuf("gsrow", (1, D), F32, stack=p1)
        gsbc = p.sbuf("gsbc", (128, D), F32, stack=p1)
        scol = p.sbuf("scol", (128, 1), F32, stack=p1)
        lt = p.sbuf("lt", (128, T), F32, stack=p1)
        dtile = p.sbuf("dtile", (128, T), BF16, stack=p1)
        dtt = p.sbuf("dtt", (128, 16, 128), BF16, stack=p1)
        nmax = p.sbuf("nmax", (128, 1), F32, stack=p1)
        rsum = p.sbuf("rsum", (128, 1), F32, stack=p1)
        plt = p.psum("plt", (128, 512), F32, stack=p1)
        ptile = p.psum("ptile1", (128, 128), BF16, stack=p1)

        b.load(gsrow[:], t_gs)
        b.barrier()
        bcast_row(gsrow, gsbc, D)

        for et in range(NET):
            b.load(se[:], se_hbm[:, et, :])
            b.barrier()

            def act3(a):
                b.act.inc(a.activation(sq_t[:], se[:], AF.Square))
            b.blk.scalar(act3)

            def dv2(v):
                b.act.wait(v)
                b.dve.inc(v.tensor_reduce(scol[:], sq_t[:], AX.X, ALU.add))
            b.blk.vector(dv2)

            def act4(a):
                b.dve.wait(a)
                a.activation(scol[:], scol[:], AF.Ln)
                b.act.inc(a.activation(scol[:], scol[:], AF.Exp, scale=-0.5))
            b.blk.scalar(act4)

            def dv3(v):
                b.act.wait(v)
                b.dve.inc(v.tensor_scalar_mul(scol[:], scol[:], 32.0))
                b.dve.wait(v)
                v.tensor_scalar(se[:], se[:], scol[:], None, ALU.mult)
                v.tensor_mul(se[:], se[:], gsbc[:])
                v.tensor_copy(se_hi[:], se[:])
                b.dve.inc(v.tensor_sub(se_lo[:], se[:], se_hi[:]), 4)
            b.blk.vector(dv3)
            b.barrier()
            emit_transpose(b, lambda i, j: se_hi[:, j * 128:(j + 1) * 128],
                           lambda j, i: seT_hi[:, j, :],
                           1, 8, ident, ptile)
            emit_transpose(b, lambda i, j: se_lo[:, j * 128:(j + 1) * 128],
                           lambda j, i: seT_lo[:, j, :],
                           1, 8, ident, ptile)

            pe3_0 = b.pe.n
            dve4_0 = b.dve.n

            def pe3(t, pe3_0=pe3_0, dve4_0=dve4_0):
                for f in range(NF):
                    sl = slice(f * 512, (f + 1) * 512)
                    if f >= 1:
                        b.dve.wait(t, dve4_0 + f)
                    mm = None
                    for kt in range(8):
                        t.matmul(plt[:], seT_hi[:, kt, :], xr_hi[:, kt, sl],
                                 start=(kt == 0), stop=False)
                        t.matmul(plt[:], seT_hi[:, kt, :], xr_lo[:, kt, sl],
                                 start=False, stop=False)
                        mm = t.matmul(plt[:], seT_lo[:, kt, :],
                                      xr_hi[:, kt, sl],
                                      start=False, stop=(kt == 7))
                    b.pe.inc(mm)
            b.blk.tensor(pe3)

            def dve4(v, pe3_0=pe3_0):
                for f in range(NF):
                    sl = slice(f * 512, (f + 1) * 512)
                    b.pe.wait(v, pe3_0 + f + 1)
                    b.dve.inc(v.tensor_copy(lt[:, sl], plt[:]))
            b.blk.vector(dve4)
            b.barrier()
            b.store(lt_hbm[:, et, :], lt[:])

            def dv5(v):
                b.dve.inc(v.tensor_reduce(nmax[:], lt[:], AX.X, ALU.max,
                                          negate=True))
            b.blk.vector(dv5)

            def act5(a):
                b.dve.wait(a)
                b.act.inc(a.activation(dtile[:], lt[:], AF.Exp,
                                       bias=nmax[:], accum_out=rsum[:]))
            b.blk.scalar(act5)

            def act6(a):
                a.activation(rsum[:], rsum[:], AF.Ln)
                b.act.inc(a.activation(rsum[:], rsum[:], AF.Exp, scale=-1.0))
            b.blk.scalar(act6)

            def dv6(v):
                b.act.wait(v)
                b.dve.inc(v.tensor_scalar(dtile[:], dtile[:], rsum[:],
                                          None, ALU.mult))
            b.blk.vector(dv6)
            b.barrier()
            emit_transpose(b, lambda i, j: dtile[:, j * 128:(j + 1) * 128],
                           lambda j, i: dtt[:, j, :],
                           1, 16, ident, ptile)
            b.store(dt_hbm[:, :, et * 128:(et + 1) * 128], dtt[:])
            b.barrier()

    phXP1.close()

    with ExitStack() as ph:
        brow = p.sbuf("brow", (1, T), F32, stack=ph)
        rrow = p.sbuf("rrow", (1, T), F32, stack=ph)
        bbc = p.sbuf("bbc", (128, T), F32, stack=ph)
        lt = p.sbuf("lt2", (128, T), F32, stack=ph)
        etile = p.sbuf("etile", (128, T), BF16, stack=ph)
        srow = p.sbuf("srow", (1, T), F32, stack=ph)
        onesb = p.sbuf("onesb2", (128, 1), BF16, stack=ph)
        pss = p.psum("pss2", (1, T), F32, stack=ph)

        def vinit(v):
            b.dve.inc(v.memset(onesb[:], 1.0))
        b.blk.vector(vinit)
        b.barrier()

        def lse_pass(scale, sub_bound, out_row, store_eb):
            for et in range(NET):
                b.load(lt[:], lt_hbm[:, et, :])
                b.barrier()
                if sub_bound:
                    def dvs(v):
                        b.dve.inc(v.tensor_sub(lt[:], lt[:], bbc[:]))
                    b.blk.vector(dvs)

                def acte(a, et=et):
                    b.dve.wait(a)
                    b.act.inc(a.activation(etile[:], lt[:], AF.Exp,
                                           scale=float(scale)))
                b.blk.scalar(acte)
                b.barrier()

                def pes(t, et=et):
                    mm = None
                    for f in range(NF):
                        sl = slice(f * 512, (f + 1) * 512)
                        mm = t.matmul(pss[0:1, sl], onesb[:], etile[:, sl],
                                      start=(et == 0), stop=(et == NET - 1))
                    b.pe.inc(mm)
                b.blk.tensor(pes)
                if store_eb:
                    b.store(eb_hbm[:, et, :], etile[:])
                b.barrier()

            def dvr(v):
                b.dve.inc(v.tensor_copy(out_row[:], pss[0:1, :]))
            b.blk.vector(dvr)
            b.barrier()

        lse_pass(1.0 / 64, False, srow, False)

        def actb1(a):
            b.act.inc(a.activation(brow[:], srow[:], AF.Ln, scale=1.0))
        b.blk.scalar(actb1)

        def dvb1(v):
            b.act.wait(v)
            b.dve.inc(v.tensor_scalar_mul(brow[:], brow[:], 64.0))
        b.blk.vector(dvb1)
        b.barrier()
        bcast_row(brow, bbc)

        lse_pass(1.0 / 8, True, srow, False)

        def actb2(a):
            b.act.inc(a.activation(srow[:], srow[:], AF.Ln))
        b.blk.scalar(actb2)

        def dvb2(v):
            b.act.wait(v)
            v.tensor_scalar_mul(srow[:], srow[:], 8.0)
            v.tensor_add(brow[:], brow[:], srow[:])
            b.dve.inc(v.tensor_scalar_add(brow[:], brow[:], -40.0))
        b.blk.vector(dvb2)
        b.barrier()
        bcast_row(brow, bbc)

        lse_pass(1.0, True, srow, True)

        def dvr2(v):
            b.dve.inc(v.reciprocal(rrow[:], srow[:]))
        b.blk.vector(dvr2)
        b.barrier()
        bcast_row(rrow, rbc)
        if debug:
            t_dbgr = p.dout("dbg_r", (4, T))
            b.store(t_dbgr[0:1, :], srow[:])
            b.store(t_dbgr[1:2, :], rrow[:])
            b.store(t_dbgr[2:3, :], brow[:])
            b.store(t_dbgr[3:4, :], rbc[0:1, :])
            b.barrier()

    with ExitStack() as ph:
        w1s = p.sbuf("w1s", (128, 8, FH), BF16, stack=ph)
        w2s = p.sbuf("w2s", (128, 32, D), BF16, stack=ph)
        b.load(w1s[:], fm_ap(t_w1), cast=True)
        b.load(w2s[:], fm_ap(t_w2), cast=True)
        ECH = 256
        dts = p.sbuf("dts", (128, 16, ECH), BF16, stack=ph)
        slT = p.sbuf("slT", (128, 8, ECH), BF16, stack=ph)
        hT = p.sbuf("hT", (128, 32, ECH), BF16, stack=ph)
        prc = p.sbuf("prc", (128, 2, D), BF16, stack=ph)
        pp = [p.psum("p2p", (128, 512), F32, stack=ph) for _ in range(2)]
        b.barrier()

        for ec in range(ET // ECH):
            esl = slice(ec * ECH, (ec + 1) * ECH)
            b.load(dts[:], dt_hbm[:, :, esl])
            b.barrier()

            pe0 = b.pe.n
            dve0 = b.dve.n

            def pe1(t, pe0=pe0, dve0=dve0):
                for dt_ in range(8):
                    ps = pp[dt_ % 2]
                    if dt_ >= 2:
                        b.dve.wait(t, dve0 + dt_ - 1)
                    mm = None
                    for st in range(16):
                        mm = t.matmul(ps[:, 0:256],
                                      xrt[:, st, dt_ * 128:(dt_ + 1) * 128],
                                      dts[:, st, :],
                                      start=(st == 0), stop=(st == 15))
                    b.pe.inc(mm)
            b.blk.tensor(pe1)

            def dv1(v, pe0=pe0):
                for dt_ in range(8):
                    b.pe.wait(v, pe0 + dt_ + 1)
                    b.dve.inc(v.tensor_copy(slT[:, dt_, :],
                                            pp[dt_ % 2][:, 0:256]))
            b.blk.vector(dv1)
            b.barrier()

            pe0 = b.pe.n
            act0 = b.act.n

            def pe2(t, pe0=pe0, act0=act0):
                for ht_ in range(32):
                    ps = pp[ht_ % 2]
                    if ht_ >= 2:
                        b.act.wait(t, act0 + ht_ - 1)
                    mm = None
                    for kt in range(8):
                        mm = t.matmul(ps[:, 0:256],
                                      w1s[:, kt, ht_ * 128:(ht_ + 1) * 128],
                                      slT[:, kt, :],
                                      start=(kt == 0), stop=(kt == 7))
                    b.pe.inc(mm)
            b.blk.tensor(pe2)

            def ac2(a, pe0=pe0):
                for ht_ in range(32):
                    b.pe.wait(a, pe0 + ht_ + 1)
                    b.act.inc(a.activation(hT[:, ht_, :],
                                           pp[ht_ % 2][:, 0:256], AF.Gelu))
            b.blk.scalar(ac2)
            b.barrier()

            pe0 = b.pe.n
            dve0 = b.dve.n
            order = [(es, f) for es in range(2) for f in range(2)]

            def pe3(t, pe0=pe0, dve0=dve0, order=order):
                for i, (es, f) in enumerate(order):
                    ps = pp[i % 2]
                    if i >= 2:
                        b.dve.wait(t, dve0 + i - 1)
                    mm = None
                    for kt in range(32):
                        mm = t.matmul(ps[:], hT[:, kt, es * 128:(es + 1) * 128],
                                      w2s[:, kt, f * 512:(f + 1) * 512],
                                      start=(kt == 0), stop=(kt == 31))
                    b.pe.inc(mm)
            b.blk.tensor(pe3)

            def dv3(v, pe0=pe0, order=order):
                for i, (es, f) in enumerate(order):
                    b.pe.wait(v, pe0 + i + 1)
                    b.dve.inc(v.tensor_copy(
                        prc[:, es, f * 512:(f + 1) * 512], pp[i % 2][:]))
            b.blk.vector(dv3)
            b.barrier()
            b.store(proc_hbm[:, ec * 2:(ec + 1) * 2, :], prc[:])
            b.barrier()

    t_dbgg = p.dout("dbg_g", (D, T), BF16) if debug else None
    with ExitStack() as ph:
        ebt = p.sbuf("ebt", (128, 2, T), BF16, stack=ph)
        pct = p.sbuf("pct", (128, 2, D), BF16, stack=ph)
        xch = p.sbuf("xch3", (128, 2, T), F32, stack=ph)
        sch = p.sbuf("sch3", (128, 2, T), BF16, stack=ph)
        och = p.sbuf("och3", (128, 2, T), F32, stack=ph)
        gch = p.sbuf("gch3", (128, T), F32, stack=ph)
        pout = p.psum("pout", (128, 2, T), F32, stack=ph)

        for rd in range(4):
            dpair = (2 * rd, 2 * rd + 1)
            ld0 = b.ld.n
            pe0 = b.pe.n

            def loads(sy, rd=rd, ld0=ld0, pe0=pe0):
                for et in range(NET):
                    if et >= 1:
                        b.pe.wait(sy, pe0 + et)
                    if KDEBUG_SERIAL_DMA:
                        b.ld.wait(sy)
                    b.ld.dma(sy.dma_start(out=ebt[:, et % 2, :],
                                          in_=eb_hbm[:, et, :]))
                    if KDEBUG_SERIAL_DMA:
                        b.ld.wait(sy)
                    b.ld.dma(sy.dma_start(out=pct[:, et % 2, :],
                                          in_=proc_hbm[:, et, :]))
            b.blk.sync(loads)

            def pe1(t, rd=rd, ld0=ld0, pe0=pe0):
                for et in range(NET):
                    b.ld.wait(t, ld0 + 32 * (et + 1))
                    mm = None
                    for dd in range(2):
                        for f in range(NF):
                            sl = slice(f * 512, (f + 1) * 512)
                            mm = t.matmul(
                                pout[:, dd, sl],
                                pct[:, et % 2,
                                    (2 * rd + dd) * 128:(2 * rd + dd + 1) * 128],
                                ebt[:, et % 2, sl],
                                start=(et == 0), stop=(et == NET - 1))
                    b.pe.inc(mm)
            b.blk.tensor(pe1)
            b.barrier()

            b.load(xch[:], fm_ap(t_x)[:, 2 * rd:2 * rd + 2, :])
            b.load(sch[:], fm_ap(t_s)[:, 2 * rd:2 * rd + 2, :])
            b.barrier()
            for dd in range(2):
                def dvc(v, dd=dd):
                    b.dve.inc(v.tensor_mul(gch[:], pout[:, dd, :], rbc[:]))
                b.blk.vector(dvc)

                def acc(a):
                    b.dve.wait(a)
                    b.act.inc(a.activation(gch[:], gch[:], AF.Silu))
                b.blk.scalar(acc)

                def dvc2(v, dd=dd):
                    b.act.wait(v)
                    v.tensor_mul(och[:, dd, :], sch[:, dd, :], gch[:])
                    b.dve.inc(v.tensor_add(och[:, dd, :], och[:, dd, :],
                                           xch[:, dd, :]), 2)
                b.blk.vector(dvc2)
                if debug:
                    b.store(fm_ap(t_dbgg)[:, 2 * rd + dd, :], gch[:], cast=True)
                b.barrier()
            b.store(fm_ap(t_out)[:, 2 * rd:2 * rd + 2, :], och[:])
            b.barrier()
    if bb is None:
        b.finish()
    return b


def build_l3_mixer(bb=None):
    H, DH, L = 16, 64, 64
    DHE = 32
    SCALE = DH ** -0.5
    b = bb if bb is not None else B("l3mix")
    p = b.p

    t_x = p.din("xT", (D, T))
    t_wqd = p.din("wq_down", (D, D))
    t_wkd = p.din("wk_down", (D, D))
    t_wvd = p.din("wv_down", (D, D))
    t_wquc = p.din("wq_up_c", (L, DHE))
    t_wque = p.din("wq_up_e", (L, DHE))
    t_wkuc = p.din("wk_up_c", (L, DHE))
    t_wkue = p.din("wk_up_e", (L, DHE))
    t_wvu = p.din("wv_up", (L, DH))
    t_wc = p.din("wc", (D, D))
    t_xn = p.dout("xnT", (D, T), BF16)
    t_s = p.dout("sT", (D, T), BF16)

    inv_freq = 1.0 / 10000.0 ** (np.arange(0, DHE, 2, dtype=np.float64) / DHE)
    freqs = np.arange(T, dtype=np.float64)[:, None] * inv_freq[None, :]
    emb = np.concatenate([freqs, freqs], axis=-1)
    cosT = np.cos(emb).T.astype(np.float32)
    sinT = np.sin(emb).T.astype(np.float32)
    M = np.zeros((DHE, DHE), np.float32)
    for i in range(DHE // 2):
        M[2 * i, 2 * i + 1] = -1.0
        M[2 * i + 1, 2 * i] = 1.0
    masks = []
    for dlt in range(4):
        mk = (np.arange(512)[None, :] >= (dlt * 128 + np.arange(128))[:, None])
        masks.append(mk.astype(np.float32))

    b.init_consts()
    cosb = b.const("c_cosT", cosT, sbuf_dt=BF16)
    sinb = b.const("c_sinT", sinT, sbuf_dt=BF16)
    rotm = b.const("c_rotm", M.T.copy(), sbuf_dt=BF16)
    cmask = [b.const(f"c_cmask{i}", masks[i], sbuf_dt=BF16) for i in range(4)]

    xn = p.sbuf("xn", (128, 8, T), BF16)

    ln_fm_stream(b, fm_ap(t_x), xn)
    b.store(fm_ap(t_xn), xn[:])


    qh = p.sbuf("qh", (128, 8, T), BF16)
    kh = p.sbuf("kh", (128, 8, T), BF16)
    vt = p.sbuf("vt", (128, 16, H * 65), BF16)
    vt_h = vt[:].rearrange("p a (h c) -> p a h c", h=H)

    def cinit2(v):
        b.dve.inc(v.memset(vt_h[:, :, :, 64:65], 1.0))
    b.blk.vector(cinit2)
    b.barrier()

    def lat_matmul(wT_t, dst):
        with ExitStack() as ph:
            w = p.sbuf("latw", (128, 8, D), BF16, stack=ph)
            b.load(w[:], fm_ap(wT_t), cast=True)
            b.barrier()

            def ev(eng, rt, ft, ps):
                return eng.tensor_copy(dst[:, rt, ft * 512:(ft + 1) * 512],
                                       ps[:])
            emit_mm_tiled(b, None, ev,
                          lambda kt, rt: w[:, kt, rt * 128:(rt + 1) * 128],
                          lambda kt, ft: xn[:, kt, ft * 512:(ft + 1) * 512],
                          8, D, T)

    for which, (t_down, dst, t_uc, t_ue) in enumerate(
            ((t_wqd, qh, t_wquc, t_wque), (t_wkd, kh, t_wkuc, t_wkue))):
        phA = ExitStack()
        lat = p.sbuf("lat", (128, 8, T), BF16, stack=phA)
        lat_matmul(t_down, lat)

        upw = p.sbuf("upw", (128, 2 * DHE), BF16, stack=phA)
        esb = p.sbuf("esb", (32, T), BF16, stack=phA)
        rsb = p.sbuf("rsb", (32, T), BF16, stack=phA)
        rop = p.sbuf("rop", (32, T), BF16, stack=phA)
        pup = p.psum("pup", (32, 512), F32, stack=phA)
        prt = p.psum("prt", (32, 512), F32, stack=phA)
        if True:
            b.load(upw[0:64, 0:DHE], t_uc, cast=True)
            b.load(upw[64:128, 0:DHE], t_uc, cast=True)
            b.load(upw[0:64, DHE:2 * DHE], t_ue, cast=True)
            b.load(upw[64:128, DHE:2 * DHE], t_ue, cast=True)
            b.barrier()
            for h in range(H):
                jh, sh = h // 2, h % 2
                base = 64 * sh
                lath = lat[base:base + 64, jh, :]
                pe0 = b.pe.n
                dve0 = b.dve.n

                def pec(t, lath=lath, base=base, pe0=pe0, dve0=dve0):
                    for f in range(NF):
                        if f >= 1:
                            b.dve.wait(t, dve0 + f)
                        b.pe.inc(t.matmul(
                            pup[:], upw[base:base + 64, 0:DHE],
                            lath[:, f * 512:(f + 1) * 512],
                            start=True, stop=True))
                b.blk.tensor(pec)

                def dvc(v, dst=dst, base=base, jh=jh, pe0=pe0):
                    for f in range(NF):
                        sl = slice(f * 512, (f + 1) * 512)
                        b.pe.wait(v, pe0 + f + 1)
                        b.dve.inc(v.tensor_copy(dst[base:base + 32, jh, sl],
                                                pup[:]))
                b.blk.vector(dvc)
                b.barrier()
                pe0 = b.pe.n
                dve0 = b.dve.n

                def pee(t, lath=lath, base=base, pe0=pe0, dve0=dve0):
                    for f in range(NF):
                        sl = slice(f * 512, (f + 1) * 512)
                        if f >= 1:
                            b.dve.wait(t, dve0 + 4 * f - 3)
                        b.pe.inc(t.matmul(pup[:],
                                          upw[base:base + 64, DHE:2 * DHE],
                                          lath[:, sl], start=True, stop=True))
                        b.dve.wait(t, dve0 + 4 * f + 1)
                        b.pe.inc(t.matmul(prt[:], rotm[:], esb[:, sl],
                                          start=True, stop=True))

                def dve_e(v, pe0=pe0):
                    for f in range(NF):
                        sl = slice(f * 512, (f + 1) * 512)
                        b.pe.wait(v, pe0 + 2 * f + 1)
                        b.dve.inc(v.tensor_copy(esb[:, sl], pup[:]))
                        b.pe.wait(v, pe0 + 2 * f + 2)
                        v.tensor_mul(rsb[:, sl], esb[:, sl], cosb[:, sl])
                        v.tensor_mul(rop[:, sl], prt[:], sinb[:, sl])
                        b.dve.inc(v.tensor_add(rsb[:, sl], rsb[:, sl],
                                               rop[:, sl]), 3)
                b.blk.tensor(pee)
                b.blk.vector(dve_e)
                b.barrier()
                if base + 32 == 32:
                    def dvp(v, dst=dst, jh=jh):
                        b.dve.inc(v.tensor_copy(dst[32:64, jh, :], rsb[:]))
                    b.blk.vector(dvp)
                else:
                    b.store(dst[96:128, jh, :], rsb[:])
                b.barrier()
        phA.close()

    with ExitStack() as phB:
        vlat = p.sbuf("vlat", (128, 8, T), BF16, stack=phB)
        lat_matmul(t_wvd, vlat)
        vupw = p.sbuf("vupw", (128, DH), BF16, stack=phB)
        pv = p.psum("pv", (128, 64), F32, stack=phB)
        b.load(vupw[0:64, :], t_wvu, cast=True)
        b.load(vupw[64:128, :], t_wvu, cast=True)
        b.barrier()
        for h in range(H):
            jh, sh = h // 2, h % 2
            base = 64 * sh
            pe0 = b.pe.n
            dve0 = b.dve.n

            def pev(t, h=h, jh=jh, base=base, pe0=pe0, dve0=dve0):
                for tt in range(16):
                    if tt >= 1:
                        b.dve.wait(t, dve0 + tt)
                    b.pe.inc(t.matmul(
                        pv[:], vlat[base:base + 64, jh,
                                    tt * 128:(tt + 1) * 128],
                        vupw[base:base + 64, :], start=True, stop=True))
            b.blk.tensor(pev)

            def dvv(v, h=h, pe0=pe0):
                for tt in range(16):
                    b.pe.wait(v, pe0 + tt + 1)
                    b.dve.inc(v.tensor_copy(vt_h[:, tt, h, 0:64], pv[:]))
            b.blk.vector(dvv)
            b.barrier()

    yn = p.sbuf("yn", (128, 8, T), BF16)
    with ExitStack() as phC:
        pS = [p.psum("l3_ps", (128, 512), F32, stack=phC) for _ in range(2)]
        pY = p.psum("l3_py", (65, T), F32, stack=phC)
        es = [p.sbuf("l3_es", (128, 512), BF16, stack=phC) for _ in range(2)]
        rrow = p.sbuf("l3_rrow", (1, T), F32, stack=phC)
        rbch = p.sbuf("l3_rbc", (64, 512), F32, stack=phC)
        pR = p.psum("l3_pr", (64, 512), F32, stack=phC)

        for h in range(H):
            jh, sh = h // 2, h % 2
            base = 64 * sh
            it = [(tk, ff) for ff in range(4) for tk in range(4 * (ff + 1))]
            pe0, act0, dve0 = b.pe.n, b.act.n, b.dve.n

            def pe_a(t, jh=jh, base=base, it=it, pe0=pe0, act0=act0,
                     dve0=dve0, h=h):
                def y_mm(pi):
                    ptk, pff = it[pi]
                    diag = ptk >= 4 * pff
                    if diag:
                        b.dve.wait(t, dve_counts[pi])
                    else:
                        b.act.wait(t, act_counts[pi])
                    b.pe.inc(t.matmul(
                        pY[:, pff * 512:(pff + 1) * 512],
                        vt_h[:, ptk, h, :], es[pi % 2][:],
                        start=(ptk == 0), stop=(ptk == 4 * (pff + 1) - 1)))
                for i, (tk, ff) in enumerate(it):
                    ps = pS[i % 2]
                    if i >= 2:
                        pi2 = i - 2
                        if it[pi2][0] >= 4 * it[pi2][1]:
                            b.dve.wait(t, dve_counts[pi2])
                        else:
                            b.act.wait(t, act_counts[pi2])
                    b.pe.inc(t.matmul(ps[:], kh[base:base + 64, jh,
                                               tk * 128:(tk + 1) * 128],
                                      qh[base:base + 64, jh,
                                         ff * 512:(ff + 1) * 512],
                                      start=True, stop=True))
                    if i >= 1:
                        y_mm(i - 1)
                y_mm(len(it) - 1)

            act_counts = {}
            dve_counts = {}
            na, nd = act0, dve0
            for i, (tk, ff) in enumerate(it):
                na += 1
                act_counts[i] = na
                if tk >= 4 * ff:
                    nd += 1
                    dve_counts[i] = nd

            def act_a(a, it=it, pe0=pe0):
                for i, (tk, ff) in enumerate(it):
                    cnt = pe0 + (2 * i if i >= 1 else 1)
                    b.pe.wait(a, cnt)
                    b.act.inc(a.activation(es[i % 2][:], pS[i % 2][:], AF.Exp,
                                           scale=float(SCALE)))

            def dve_a(v, it=it):
                for i, (tk, ff) in enumerate(it):
                    if tk >= 4 * ff:
                        dlt = tk - 4 * ff
                        b.act.wait(v, act_counts[i])
                        b.dve.inc(v.tensor_mul(es[i % 2][:], es[i % 2][:],
                                               cmask[dlt][:]))
            b.blk.tensor(pe_a)
            b.blk.scalar(act_a)
            b.blk.vector(dve_a)
            b.barrier()

            def dvr(v):
                b.dve.inc(v.reciprocal(rrow[:], pY[64:65, :]))
            b.blk.vector(dvr)
            pe1_0 = b.pe.n
            dve1_0 = b.dve.n

            def pe_n(t, pe1_0=pe1_0, dve1_0=dve1_0):
                b.dve.wait(t, dve1_0)
                for f in range(NF):
                    if f >= 1:
                        b.dve.wait(t, dve1_0 + 2 * f)
                    b.pe.inc(t.matmul(pR[:], b.ones_row_f[0:1, 0:64],
                                      rrow[0:1, f * 512:(f + 1) * 512],
                                      start=True, stop=True))

            def dve_n(v, jh=jh, base=base, pe1_0=pe1_0):
                for f in range(NF):
                    sl = slice(f * 512, (f + 1) * 512)
                    b.pe.wait(v, pe1_0 + f + 1)
                    v.tensor_copy(rbch[:], pR[:])
                    b.dve.inc(v.tensor_mul(yn[base:base + 64, jh, sl],
                                           pY[0:64, sl], rbch[:]), 2)
            b.blk.tensor(pe_n)
            b.blk.vector(dve_n)
            b.barrier()

    with ExitStack() as phD:
        wc = p.sbuf("wc_s", (128, 8, D), BF16, stack=phD)
        b.load(wc[:], fm_ap(t_wc), cast=True)
        b.barrier()
        sfm = xn

        def pev(eng, rt, ft, ps):
            return eng.tensor_copy(sfm[:, rt, ft * 512:(ft + 1) * 512], ps[:])
        emit_mm_tiled(b, None, pev,
                      lambda kt, rt: wc[:, kt, rt * 128:(rt + 1) * 128],
                      lambda kt, ft: yn[:, kt, ft * 512:(ft + 1) * 512],
                      8, D, T)
        b.store(fm_ap(t_s), sfm[:])
    if bb is None:
        b.finish()
    return b


def build_l3_ffn(bb=None):
    NP = 2048
    b = bb if bb is not None else B("l3ffn")
    p = b.p

    t_x = p.din("xT", (D, T))
    t_s = p.din("sT", (D, T), BF16)
    t_phi = p.din("phi", (D, NP))
    t_w = p.din("We", (64, D, D))
    t_eb = p.din("eb", (64, D))
    t_out = p.dout("xoutT", (D, T))

    cc_in = p.dtmp("cc_in", (T, NP))
    cc_out = p.dtmp("cc_out", (T, NP))
    t_yb = p.dtmp("l3f_y", (NP, D), BF16)
    t_ce = p.dtmp("l3f_ce", (NP, T), BF16)

    b.init_consts()

    ident = b.const("c_ident3", np.eye(128, dtype=np.float32), sbuf_dt=BF16)
    xnh = p.sbuf("xnh", (128, 8, T), BF16)
    onescol = p.sbuf("onescol", (128, 8), F32)
    rrow = p.sbuf("l3f_rrow", (1, T), F32)
    rbc = p.sbuf("l3f_rbc", (128, T), F32)
    phEarly = ExitStack()
    xnl = p.sbuf("xnl", (128, 8, T), BF16, stack=phEarly)

    def vinit(v):
        b.dve.inc(v.memset(onescol[:], 1.0))
    b.blk.vector(vinit)

    ln_fm_stream(b, fm_ap(t_x), None, hilo_gamma=(xnh, xnl, onescol))

    phih = p.sbuf("phih", (128, 8, NP), BF16, stack=phEarly)
    phil = p.sbuf("phil", (128, 8, NP), BF16, stack=phEarly)
    with ExitStack() as ph:
        phf = p.sbuf("phf", (128, NP), F32, stack=ph)
        for j in range(8):
            b.load(phf[:], fm_ap(t_phi)[:, j, :])
            b.barrier()

            def dv(v, j=j):
                v.tensor_copy(phih[:, j, :], phf[:])
                b.dve.inc(v.tensor_sub(phil[:, j, :], phf[:], phih[:, j, :]),
                          2)
            b.blk.vector(dv)
            b.barrier()

    def bcast_row(src_row, dst_bc, ncols=T):
        with ExitStack() as ph:
            pbc = p.psum("bc_ps", (128, 512), F32, stack=ph)
            pe0 = b.pe.n
            dve0 = b.dve.n

            def pe(t):
                for f in range(ncols // 512):
                    if f >= 1:
                        b.dve.wait(t, dve0 + f)
                    b.pe.inc(t.matmul(pbc[:], b.ones_row_f[:],
                                      src_row[:, f * 512:(f + 1) * 512],
                                      start=True, stop=True))

            def dv(v):
                for f in range(ncols // 512):
                    b.pe.wait(v, pe0 + f + 1)
                    b.dve.inc(v.tensor_copy(
                        dst_bc[:, f * 512:(f + 1) * 512], pbc[:]))
            b.blk.tensor(pe)
            b.blk.vector(dv)
            b.barrier()

    with ExitStack() as ph:
        etile = p.sbuf("etile", (128, NP), F32, stack=ph)
        plt = p.psum("plt", (128, 512), F32, stack=ph)
        for mt in range(16):
            pe0 = b.pe.n
            act0 = b.act.n

            def pe1(t, mt=mt, pe0=pe0, act0=act0):
                for f in range(NP // 512):
                    sl = slice(f * 512, (f + 1) * 512)
                    if f >= 1:
                        b.act.wait(t, act0 + f)
                    mm = None
                    for kt in range(8):
                        t.matmul(plt[:], xnh[:, kt, mt * 128:(mt + 1) * 128],
                                 phih[:, kt, sl], start=(kt == 0), stop=False)
                        t.matmul(plt[:], xnh[:, kt, mt * 128:(mt + 1) * 128],
                                 phil[:, kt, sl], start=False, stop=False)
                        mm = t.matmul(plt[:],
                                      xnl[:, kt, mt * 128:(mt + 1) * 128],
                                      phih[:, kt, sl], start=False,
                                      stop=(kt == 7))
                    b.pe.inc(mm)
            b.blk.tensor(pe1)

            def act1(a, mt=mt, pe0=pe0):
                for f in range(NP // 512):
                    sl = slice(f * 512, (f + 1) * 512)
                    b.pe.wait(a, pe0 + f + 1)
                    b.act.inc(a.activation(etile[:, sl], plt[:], AF.Exp))
            b.blk.scalar(act1)
            b.barrier()
            b.store(cc_in.rearrange("(a p) c -> p a c", p=128)[:, mt, :],
                    etile[:])
            b.barrier()

    ccs = p.sem("cc")

    def gcc(g):
        g.collective_compute(
            "AllReduce", ALU.add,
            replica_groups=[list(range(NCORES))],
            ins=[cc_in.opt()], outs=[cc_out.opt()],
        ).then_inc(ccs.h, 1)
        ccs.n += 1
    b.blk.gpsimd(gcc)

    def wcc(eng):
        ccs.wait(eng)
    for sec in (b.blk.sync, b.blk.tensor, b.blk.vector, b.blk.scalar,
                b.blk.gpsimd):
        sec(wcc)

    with ExitStack() as ph:
        ce = p.sbuf("ce", (128, T), BF16, stack=ph)
        cf = p.sbuf("cf", (128, T), F32, stack=ph)
        onesb = p.sbuf("onesb3", (128, 1), BF16, stack=ph)
        plt = p.psum("plt2", (128, 512), F32, stack=ph)
        pcs = [p.psum(f"pcs{i}", (1, 512), F32, stack=ph) for i in range(4)]

        def vinit2(v):
            b.dve.inc(v.memset(onesb[:], 1.0))
        b.blk.vector(vinit2)
        b.barrier()
        ce_hbm = t_ce.rearrange("(a p) c -> p a c", p=128)
        for nt_ in range(16):
            pe0 = b.pe.n
            dve0 = b.dve.n

            def pe1(t, nt_=nt_, pe0=pe0, dve0=dve0):
                for f in range(NF):
                    sl = slice(f * 512, (f + 1) * 512)
                    if f >= 1:
                        b.dve.wait(t, dve0 + f)
                    mm = None
                    for kt in range(8):
                        t.matmul(plt[:], phih[:, kt, nt_ * 128:(nt_ + 1) * 128],
                                 xnh[:, kt, sl], start=(kt == 0), stop=False)
                        t.matmul(plt[:], phih[:, kt, nt_ * 128:(nt_ + 1) * 128],
                                 xnl[:, kt, sl], start=False, stop=False)
                        mm = t.matmul(plt[:],
                                      phil[:, kt, nt_ * 128:(nt_ + 1) * 128],
                                      xnh[:, kt, sl], start=False,
                                      stop=(kt == 7))
                    b.pe.inc(mm)
            b.blk.tensor(pe1)

            def dv1(v, pe0=pe0):
                for f in range(NF):
                    sl = slice(f * 512, (f + 1) * 512)
                    b.pe.wait(v, pe0 + f + 1)
                    b.dve.inc(v.tensor_scalar_add(cf[:, sl], plt[:], -45.0))
            b.blk.vector(dv1)

            def act1(a):
                b.dve.wait(a)
                b.act.inc(a.activation(ce[:], cf[:], AF.Exp))
            b.blk.scalar(act1)
            b.barrier()

            def pe2(t, nt_=nt_):
                mm = None
                for f in range(NF):
                    sl = slice(f * 512, (f + 1) * 512)
                    mm = t.matmul(pcs[f][:], onesb[:], ce[:, sl],
                                  start=(nt_ == 0), stop=(nt_ == 15))
                b.pe.inc(mm)
            b.blk.tensor(pe2)
            b.store(ce_hbm[:, nt_, :], ce[:])
            b.barrier()

        def dvc(v):
            for f in range(NF):
                b.dve.inc(v.tensor_copy(rrow[:, f * 512:(f + 1) * 512],
                                        pcs[f][:]))
            b.dve.wait(v)
            b.dve.inc(v.reciprocal(rrow[:], rrow[:]))
        b.blk.vector(dvc)
        b.barrier()
        bcast_row(rrow, rbc)

    phEarly.close()

    with ExitStack() as ph:
        ech = p.sbuf("ech", (128, 16, 512), F32, stack=ph)
        dch = p.sbuf("dch", (128, 16, 512), BF16, stack=ph)
        xtm = p.sbuf("xtm", (128, 16, D), BF16, stack=ph)
        ptile = p.psum("ptile3", (128, 128), BF16, stack=ph)
        b.barrier()
        emit_transpose(b, lambda i, j: xnh[:, i, j * 128:(j + 1) * 128],
                       lambda j, i: xtm[:, j, i * 128:(i + 1) * 128],
                       8, 16, ident, ptile)

        slT = p.sbuf("slT", (128, 8, 512), BF16, stack=ph)
        wexp = p.sbuf("wexp", (128, 2, 8, D), BF16, stack=ph)
        ebrow = p.sbuf("ebrow", (1, D), BF16, stack=ph)
        ysb = p.sbuf("ysb", (32, D), BF16, stack=ph)
        psl = [p.psum("psl", (128, 512), F32, stack=ph) for _ in range(2)]
        pye = p.psum("pye", (32, 512), F32, stack=ph)

        cc_out_r = cc_out.rearrange("(a p) c -> p a c", p=128)
        cc_in_r = cc_in.rearrange("(a p) c -> p a c", p=128)
        for ec in range(4):
            esl = slice(ec * 512, (ec + 1) * 512)
            for mt in range(16):
                b.load(ech[:, mt, :], cc_out_r[:, mt, esl])
            b.barrier()

            def dvr(v):
                for mt in range(16):
                    b.dve.inc(v.nop())
            def actr(a):
                for mt in range(16):
                    a.activation(ech[:, mt, :], ech[:, mt, :], AF.Ln)
                    b.act.inc(a.activation(ech[:, mt, :], ech[:, mt, :],
                                           AF.Exp, scale=-1.0))
            b.blk.scalar(actr)
            b.barrier()
            def lds(g):
                for mt in range(16):
                    if KDEBUG_SERIAL_DMA:
                        b.gld.wait(g)
                    b.gld.dma(g.dma_start(out=dch[:, mt, :],
                                          in_=cc_in_r[:, mt, esl]))
            b.blk.gpsimd(lds)
            b.barrier()

            def dvd(v):
                for mt in range(16):
                    b.dve.inc(v.tensor_mul(dch[:, mt, :], dch[:, mt, :],
                                           ech[:, mt, :]))
            b.blk.vector(dvd)
            b.barrier()

            pe0 = b.pe.n
            dve0 = b.dve.n

            def pesl(t, pe0=pe0, dve0=dve0):
                for dt_ in range(8):
                    ps = psl[dt_ % 2]
                    if dt_ >= 2:
                        b.dve.wait(t, dve0 + dt_ - 1)
                    mm = None
                    for mt in range(16):
                        mm = t.matmul(ps[:], xtm[:, mt, dt_ * 128:(dt_ + 1) * 128],
                                      dch[:, mt, :],
                                      start=(mt == 0), stop=(mt == 15))
                    b.pe.inc(mm)
            b.blk.tensor(pesl)

            def dvsl(v, pe0=pe0):
                for dt_ in range(8):
                    b.pe.wait(v, pe0 + dt_ + 1)
                    b.dve.inc(v.tensor_copy(slT[:, dt_, :], psl[dt_ % 2][:]))
            b.blk.vector(dvsl)
            b.barrier()

            for ei in range(16):
                n = ec * 16 + ei
                buf = ei % 2
                b.load(wexp[:, buf, :, :], fm_ap(t_w[n]), cast=True)
                b.load(ebrow[:], t_eb[n:n + 1, :], cast=True)
                b.barrier()
                pe0 = b.pe.n
                dve0 = b.dve.n

                def pey(t, n=n, ei=ei, buf=buf, pe0=pe0, dve0=dve0):
                    for f in range(2):
                        sl = slice(f * 512, (f + 1) * 512)
                        if f >= 1:
                            b.dve.wait(t, dve0 + f)
                        mm = None
                        for kt in range(8):
                            mm = t.matmul(pye[:],
                                          slT[:, kt, ei * 32:(ei + 1) * 32],
                                          wexp[:, buf, kt, sl],
                                          start=(kt == 0), stop=False)
                        mm = t.matmul(pye[:], b.ones_row_b[0:1, 0:32],
                                      ebrow[:, sl], start=False, stop=True)
                        b.pe.inc(mm)
                b.blk.tensor(pey)

                def dvy(v, pe0=pe0):
                    for f in range(2):
                        sl = slice(f * 512, (f + 1) * 512)
                        b.pe.wait(v, pe0 + f + 1)
                        b.dve.inc(v.tensor_copy(ysb[:, sl], pye[:]))
                b.blk.vector(dvy)
                b.barrier()
                b.store(t_yb[n * 32:(n + 1) * 32, :], ysb[:])
                b.barrier()

    with ExitStack() as ph:
        ebt = p.sbuf("ebt3", (128, 2, T), BF16, stack=ph)
        pct = p.sbuf("pct3", (128, 2, D), BF16, stack=ph)
        xch = p.sbuf("xch4", (128, 2, T), F32, stack=ph)
        sch = p.sbuf("sch4", (128, 2, T), BF16, stack=ph)
        och = p.sbuf("och4", (128, 2, T), F32, stack=ph)
        gch = p.sbuf("gch4", (128, T), F32, stack=ph)
        pout = p.psum("pout3", (128, 2, T), F32, stack=ph)
        yb_hbm = t_yb.rearrange("(a p) c -> p a c", p=128)
        for rd in range(4):
            ld0 = b.ld.n
            pe0 = b.pe.n

            def loads(sy, rd=rd, ld0=ld0, pe0=pe0):
                for et in range(16):
                    if et >= 1:
                        b.pe.wait(sy, pe0 + et)
                    if KDEBUG_SERIAL_DMA:
                        b.ld.wait(sy)
                    b.ld.dma(sy.dma_start(out=ebt[:, et % 2, :],
                                          in_=ce_hbm[:, et, :]))
                    if KDEBUG_SERIAL_DMA:
                        b.ld.wait(sy)
                    b.ld.dma(sy.dma_start(out=pct[:, et % 2, :],
                                          in_=yb_hbm[:, et, :]))
            b.blk.sync(loads)

            def pe1(t, rd=rd, ld0=ld0, pe0=pe0):
                for et in range(16):
                    b.ld.wait(t, ld0 + 32 * (et + 1))
                    mm = None
                    for dd in range(2):
                        for f in range(NF):
                            sl = slice(f * 512, (f + 1) * 512)
                            mm = t.matmul(
                                pout[:, dd, sl],
                                pct[:, et % 2,
                                    (2 * rd + dd) * 128:(2 * rd + dd + 1) * 128],
                                ebt[:, et % 2, sl],
                                start=(et == 0), stop=(et == 15))
                    b.pe.inc(mm)
            b.blk.tensor(pe1)
            b.barrier()

            b.load(xch[:], fm_ap(t_x)[:, 2 * rd:2 * rd + 2, :])
            b.load(sch[:], fm_ap(t_s)[:, 2 * rd:2 * rd + 2, :])
            b.barrier()
            for dd in range(2):
                def dvc2(v, dd=dd):
                    b.dve.inc(v.tensor_mul(gch[:], pout[:, dd, :], rbc[:]))
                b.blk.vector(dvc2)

                def acc(a):
                    b.dve.wait(a)
                    b.act.inc(a.activation(gch[:], gch[:], AF.Silu))
                b.blk.scalar(acc)

                def dvc3(v, dd=dd):
                    b.act.wait(v)
                    v.tensor_mul(och[:, dd, :], sch[:, dd, :], gch[:])
                    b.dve.inc(v.tensor_add(och[:, dd, :], och[:, dd, :],
                                           xch[:, dd, :]), 2)
                b.blk.vector(dvc3)
                b.barrier()
            b.store(fm_ap(t_out)[:, 2 * rd:2 * rd + 2, :], och[:])
            b.barrier()
    if bb is None:
        b.finish()
    return b


F16 = mybir.dt.float16


def build_l0_mixer(debug=False, bb=None):
    DI = 2048
    NS = 16
    DS = 64
    b = bb if bb is not None else B("l0mix")
    p = b.p

    t_x = p.din("xT", (D, T))
    t_win = p.din("w_in", (D, 2 * DI))
    t_cw = p.din("conv_w", (128, 64))
    t_cb = p.din("conv_b", (128, NS))
    t_wx = p.din("w_x", (DI, 132))
    t_alog = p.din("a_log", (DI, DS))
    t_dtb = p.din("dt_bias", (128, NS))
    t_dco = p.din("dcoef", (128, NS))
    t_wout = p.din("w_out", (DI, D))
    t_xn = p.dout("xnT", (D, T), BF16)
    t_s = p.dout("sT", (D, T), BF16)
    t_y = (p.dout if debug else p.dtmp)("l0_y", (DI, T), BF16)
    t_dbg = p.dout("l0_dbg", (130, T)) if debug else None

    b.init_consts()
    ident = b.const("c_ident0", np.eye(128, dtype=np.float32), sbuf_dt=BF16)
    identf = b.const("c_identf", np.eye(128, dtype=np.float32))
    np_seln = np.zeros((64, 64, 128), np.float32)
    for _n in range(64):
        np_seln[_n, _n, :] = 1.0
    seln = b.const("c_seln", np_seln, sbuf_dt=BF16)

    xn = p.sbuf("xn", (128, 8, T), BF16)
    ln_fm_stream(b, fm_ap(t_x), xn)
    b.store(fm_ap(t_xn), xn[:])


    TP = T + 3
    cwcol = p.sbuf("cwcol", (128, 64), F32)
    cbcol = p.sbuf("cbcol", (128, NS), F32)
    dtbcol = p.sbuf("dtbcol", (128, NS), F32)
    dcocol = p.sbuf("dcocol", (128, NS), F32)
    scanScope = ExitStack()
    xc = p.sbuf("xc", (128, NS, TP), BF16, stack=scanScope)
    acol = p.sbuf("acol", (128, NS, DS), F32, stack=scanScope)
    srow = p.sbuf("srow", (1, T), F32, stack=scanScope)
    btr = p.sbuf("btr", (64, T), BF16, stack=scanScope)
    ctr = p.sbuf("ctr", (64, T), BF16, stack=scanScope)

    b.load(acol[:], t_alog.rearrange("(a p) c -> p a c", p=128))
    b.load(cwcol[:], t_cw)
    b.load(cbcol[:], t_cb)
    b.load(dtbcol[:], t_dtb)
    b.load(dcocol[:], t_dco)
    b.barrier()

    def acta(a):
        b.act.inc(a.activation(acol[:].rearrange("p a c -> p (a c)"),
                               acol[:].rearrange("p a c -> p (a c)"), AF.Exp))
    b.blk.scalar(acta)

    def dvea(v):
        b.act.wait(v)
        b.dve.inc(v.tensor_scalar_mul(acol[:].rearrange("p a c -> p (a c)"),
                                      acol[:].rearrange("p a c -> p (a c)"),
                                      -1.0))
    b.blk.vector(dvea)
    b.barrier()

    with ExitStack() as ph:
        win = p.sbuf("win", (128, 8, DI), BF16, stack=ph)
        xs = p.sbuf("xs", (128, 8, TP), BF16, stack=ph)
        b.load(win[:], fm_ap(t_win)[:, :, 0:DI], cast=True)
        b.barrier()

        def zinit(v):
            v.memset(xs[:, :, 0:3], 0.0)
            b.dve.inc(v.memset(xc[:, :, 0:3], 0.0))
        b.blk.vector(zinit)

        cacc = p.sbuf("cacc", (128, T), F32, stack=ph)
        for hg in range(2):
            def xsev(eng, rt, ft, ps, hg=hg):
                return eng.tensor_copy(
                    xs[:, rt, 3 + ft * 512:3 + (ft + 1) * 512], ps[:])
            emit_mm_tiled(b, None, xsev,
                          lambda kt, rt, hg=hg: win[:, kt, hg * 1024 +
                                                    rt * 128:(hg * 1024 +
                                                              (rt + 1) * 128)],
                          lambda kt, ft: xn[:, kt, ft * 512:(ft + 1) * 512],
                          8, 1024, T)

            for st8 in range(8):
                st = hg * 8 + st8
                def dvc(v, st=st, st8=st8):
                    v.tensor_scalar(cacc[:], xs[:, st8, 0:T],
                                    cwcol[:, 4 * st:4 * st + 1], None,
                                    ALU.mult)
                    ins = None
                    for j in range(1, 4):
                        ins = v.scalar_tensor_tensor(
                            cacc[:], xs[:, st8, j:j + T],
                            cwcol[:, 4 * st + j:4 * st + j + 1], cacc[:],
                            ALU.mult, ALU.add)
                    b.dve.inc(ins)
                b.blk.vector(dvc)

                def acc_(a, st=st):
                    b.dve.wait(a)
                    b.act.inc(a.activation(xc[:, st, 3:TP], cacc[:], AF.Silu,
                                           bias=cbcol[:, st:st + 1]))
                b.blk.scalar(acc_)
                b.barrier()

    with ExitStack() as ph:
        wx = p.sbuf("wx", (128, NS, 132), BF16, stack=ph)
        sst = p.sbuf("sst", (128, 132), F32, stack=ph)
        pss = p.psum("pssm", (128, 132), F32, stack=ph)
        ptile = p.psum("ptile0", (128, 128), BF16, stack=ph)
        psr = p.psum("psr", (128, 128), BF16, stack=ph)
        b.load(wx[:], t_wx.rearrange("(a p) c -> p a c", p=128), cast=True)
        b.barrier()
        sstb = p.sbuf("sstb", (128, 132), BF16, stack=ph)
        psf = p.psum("psf", (1, 128), F32, stack=ph)
        for tt in range(16):
            pe0 = b.pe.n

            def pe1(t, tt=tt, pe0=pe0):
                mm = None
                for kt in range(NS):
                    mm = t.matmul(pss[:], xc[:, kt, 3 + tt * 128:
                                             3 + (tt + 1) * 128],
                                  wx[:, kt, :], start=(kt == 0),
                                  stop=(kt == NS - 1))
                b.pe.inc(mm)
            b.blk.tensor(pe1)

            def dv1(v, pe0=pe0):
                b.pe.wait(v, pe0 + 1)
                v.tensor_copy(sst[:], pss[:])
                b.dve.inc(v.tensor_copy(sstb[:], sst[:]), 2)
            b.blk.vector(dv1)
            b.barrier()

            pe1_0 = b.pe.n

            def pe2(t, tt=tt, pe1_0=pe1_0):
                t.transpose(psr[0:128, 0:128], sstb[:, 0:128], ident[:])
                b.pe.inc(t.transpose(psf[:], sst[:, 128:129], identf[:]), 2)
            b.blk.tensor(pe2)

            def dv3(v, tt=tt, pe1_0=pe1_0):
                b.pe.wait(v, pe1_0 + 2)
                v.tensor_copy(btr[:, tt * 128:(tt + 1) * 128],
                              psr[0:64, 0:128])
                v.tensor_copy(ctr[:, tt * 128:(tt + 1) * 128],
                              psr[64:128, 0:128])
                b.dve.inc(v.tensor_copy(srow[:, tt * 128:(tt + 1) * 128],
                                        psf[:]), 3)
            b.blk.vector(dv3)
            b.barrier()

    sbc = p.sbuf("sbc", (128, T), F32, stack=scanScope)
    with ExitStack() as ph:
        pbc = p.psum("pbc0", (128, 512), F32, stack=ph)
        pe0 = b.pe.n
        dve0 = b.dve.n

        def peb(t):
            for f in range(NF):
                if f >= 1:
                    b.dve.wait(t, dve0 + f)
                b.pe.inc(t.matmul(pbc[:], b.ones_row_f[:],
                                  srow[:, f * 512:(f + 1) * 512],
                                  start=True, stop=True))

        def dvb(v):
            for f in range(NF):
                b.pe.wait(v, pe0 + f + 1)
                b.dve.inc(v.tensor_copy(sbc[:, f * 512:(f + 1) * 512],
                                        pbc[:]))
        b.blk.tensor(peb)
        b.blk.vector(dvb)
        b.barrier()

    with ExitStack() as ph:
        lam = p.sbuf("lam", (128, T), F32, stack=ph)
        bn = p.sbuf("bn", (128, T), F32, stack=ph)
        hh = p.sbuf("hh", (128, T), F32, stack=ph)
        yac = p.sbuf("yac", (128, T), F32, stack=ph)
        ybf = p.sbuf("ybf", (128, T), BF16, stack=ph)
        pbb = p.psum("pbb", (128, 4, 512), F32, stack=ph)
        pcc = p.psum("pcc", (128, 4, 512), F32, stack=ph)
        y_hbm = t_y.rearrange("(a p) c -> p a c", p=128)

        dtxs = p.sbuf("dtxs", (128, T), BF16, stack=ph)
        dts = p.sbuf("dts0", (128, T), F32, stack=ph)
        u2s = p.sbuf("u2s", (128, T), F32, stack=ph)
        for st in range(NS):
            def dva(v, st=st):
                b.dve.inc(v.tensor_scalar(dts[:], sbc[:],
                                          dtbcol[:, st:st + 1], None,
                                          ALU.add))
            b.blk.vector(dva)

            def aca(a):
                b.dve.wait(a)
                a.activation(u2s[:], dts[:], AF.Abs)
                b.act.inc(a.activation(u2s[:], u2s[:], AF.Exp, scale=-1.0))
            b.blk.scalar(aca)

            def dvb2(v):
                b.act.wait(v)
                b.dve.inc(v.tensor_scalar_add(u2s[:], u2s[:], 1.0))
            b.blk.vector(dvb2)

            def acb(a):
                b.dve.wait(a)
                b.act.inc(a.activation(u2s[:], u2s[:], AF.Ln))
            b.blk.scalar(acb)

            def dvc2(v, st=st):
                b.act.wait(v)
                v.tensor_scalar_max(dts[:], dts[:], 0.0)
                v.tensor_add(dts[:], dts[:], u2s[:])
                b.dve.inc(v.tensor_mul(dtxs[:], dts[:], xc[:, st, 3:TP]), 3)
            b.blk.vector(dvc2)
            b.barrier()
            for n in range(DS):
                pe0 = b.pe.n
                dve0 = b.dve.n
                act0 = b.act.n

                def peb2(t, st=st, n=n, pe0=pe0, dve0=dve0):
                    b.dve.wait(t, dve0)
                    for f in range(NF):
                        sl = slice(f * 512, (f + 1) * 512)
                        t.matmul(pbb[:, f, :], seln[:, n, :],
                                 btr[:, sl], start=True, stop=True)
                        mm = t.matmul(pcc[:, f, :], seln[:, n, :],
                                      ctr[:, sl], start=True, stop=True)
                    b.pe.inc(mm)

                def acl(a, st=st, n=n, pe0=pe0, act0=act0, dve0=dve0):
                    b.dve.wait(a, dve0)
                    b.act.inc(a.activation(lam[:], dts[:], AF.Exp,
                                           scale=acol[:, st, n:n + 1]))

                def dvs(v, st=st, n=n, pe0=pe0, act0=act0, dve0=dve0):
                    b.pe.wait(v, pe0 + 1)
                    b.act.wait(v, act0 + 1)
                    for f in range(NF):
                        sl = slice(f * 512, (f + 1) * 512)
                        v.tensor_mul(bn[:, sl], dtxs[:, sl], pbb[:, f, :])
                    v.tensor_tensor_scan(hh[:], lam[:], bn[:], 0.0,
                                         ALU.mult, ALU.add)
                    for f in range(NF):
                        sl = slice(f * 512, (f + 1) * 512)
                        v.tensor_mul(hh[:, sl], hh[:, sl], pcc[:, f, :])
                    if n == 0:
                        b.dve.inc(v.tensor_copy(yac[:], hh[:]))
                    else:
                        b.dve.inc(v.tensor_add(yac[:], yac[:], hh[:]))
                b.blk.tensor(peb2)
                b.blk.scalar(acl)
                b.blk.vector(dvs)
            def dvy(v, st=st):
                v.scalar_tensor_tensor(yac[:], xc[:, st, 3:TP],
                                       dcocol[:, st:st + 1], yac[:],
                                       ALU.mult, ALU.add)
                b.dve.inc(v.tensor_copy(ybf[:], yac[:]), 2)
            b.blk.vector(dvy)
            b.barrier()
            b.store(y_hbm[:, st, :], ybf[:])
            b.barrier()

    if debug:
        b.store(t_dbg[0:64, :], btr[:], cast=True)
        b.store(t_dbg[64:128, :], ctr[:], cast=True)
        b.store(t_dbg[128:129, :], srow[:])
        b.store(t_dbg[129:130, :], sbc[0:1, :])
        b.barrier()
    scanScope.close()

    with ExitStack() as ph:
        winz = p.sbuf("winz", (128, 8, DI), BF16, stack=ph)
        wout = p.sbuf("wout", (128, NS, D), BF16, stack=ph)
        ych = p.sbuf("ych0", (128, NS, 512), BF16, stack=ph)
        zch = p.sbuf("zch0", (128, NS, 512), BF16, stack=ph)
        sch = p.sbuf("sch0", (128, 8, 512), BF16, stack=ph)
        pz = [p.psum("pz", (128, 512), F32, stack=ph) for _ in range(2)]
        b.load(winz[:], fm_ap(t_win)[:, :, DI:2 * DI], cast=True)
        b.load(wout[:], t_wout.rearrange("(a p) c -> p a c", p=128),
               cast=True)
        b.barrier()
        for tc in range(T // 512):
            tsl = slice(tc * 512, (tc + 1) * 512)
            b.load(ych[:], y_hbm[:, :, tsl])
            b.barrier()
            pe0 = b.pe.n
            act0 = b.act.n

            def pez(t, tc=tc, pe0=pe0, act0=act0):
                for ct in range(NS):
                    ps = pz[ct % 2]
                    if ct >= 2:
                        b.act.wait(t, act0 + ct - 1)
                    mm = None
                    for kt in range(8):
                        mm = t.matmul(ps[:],
                                      winz[:, kt, ct * 128:(ct + 1) * 128],
                                      xn[:, kt, tsl],
                                      start=(kt == 0), stop=(kt == 7))
                    b.pe.inc(mm)
            b.blk.tensor(pez)

            def acz(a, pe0=pe0):
                for ct in range(NS):
                    b.pe.wait(a, pe0 + ct + 1)
                    b.act.inc(a.activation(zch[:, ct, :], pz[ct % 2][:],
                                           AF.Silu))
            b.blk.scalar(acz)
            b.barrier()

            def dvg(v):
                for ct in range(NS):
                    b.dve.inc(v.tensor_mul(ych[:, ct, :], ych[:, ct, :],
                                           zch[:, ct, :]))
            b.blk.vector(dvg)
            b.barrier()

            pe0 = b.pe.n
            dve0 = b.dve.n

            def peo(t, pe0=pe0, dve0=dve0):
                for dt_ in range(8):
                    ps = pz[dt_ % 2]
                    if dt_ >= 2:
                        b.dve.wait(t, dve0 + dt_ - 1)
                    mm = None
                    for kt in range(NS):
                        mm = t.matmul(ps[:],
                                      wout[:, kt, dt_ * 128:(dt_ + 1) * 128],
                                      ych[:, kt, :],
                                      start=(kt == 0), stop=(kt == NS - 1))
                    b.pe.inc(mm)
            b.blk.tensor(peo)

            def dvo(v, pe0=pe0):
                for dt_ in range(8):
                    b.pe.wait(v, pe0 + dt_ + 1)
                    b.dve.inc(v.tensor_copy(sch[:, dt_, :], pz[dt_ % 2][:]))
            b.blk.vector(dvo)
            b.barrier()
            b.store(fm_ap(t_s)[:, :, tsl], sch[:])
            b.barrier()
    if bb is None:
        b.finish()
    return b



def build_l2_mixer(debug=False, upto=None, bb=None):
    Q = 32
    NCH = T // Q
    b = bb if bb is not None else B("l2mix")
    p = b.p

    t_x = p.din("xT", (D, T))
    t_maa = p.din("maa_x", (128, 8))
    t_wkvrg = p.din("wkvrg", (128, 40))
    t_tmw1 = p.din("tm_w1", (D, 160))
    t_tmw2 = p.din("tm_w2f", (5, 32, D))
    t_td = p.din("td", (128, 8))
    t_tdw1 = p.din("td_w1", (D, 64))
    t_tdw2 = p.din("td_w2", (64, D))
    t_u = p.din("u_col", (128, 8))
    t_wr = p.din("wr", (D, D))
    t_wk = p.din("wk", (D, D))
    t_wv = p.din("wv", (D, D))
    t_wg = p.din("wg", (D, D))
    t_wo = p.din("wo", (D, D))
    t_lnw = p.din("lnw", (1, D))
    t_lnb = p.din("lnb", (1, D))
    t_xn = p.dout("xnT", (D, T), BF16)
    t_s = p.dout("sT", (D, T), BF16)
    t_rf = p.dtmp("l2_rf", (D, T), BF16)
    t_kf = p.dtmp("l2_kf", (D, T), BF16)
    t_ktm = p.dtmp("l2_ktm", (T, D), BF16)
    t_vtm = p.dtmp("l2_vtm", (T, D), BF16)
    t_gtm = p.dtmp("l2_gtm", (T, D), BF16)
    t_ew = p.dtmp("l2_ew", (D, T), F16)
    t_y = p.dtmp("l2_y", (T, D), BF16)
    if debug:
        t_dbg = p.dout("l2_dbg", (D, T), F32)
        t_y = p.dout("l2_yd", (T, D), BF16)
        t_ew = p.dout("l2_ewd", (D, T), F16)
        t_gtm = p.dout("l2_gtmd", (T, D), BF16)
        t_rf = p.dout("l2_rfd", (D, T), BF16)
        t_kf = p.dout("l2_kfd", (D, T), BF16)
        t_vtm = p.dout("l2_vtmd", (T, D), BF16)

    b.init_consts()
    ident = b.const("c_ident2", np.eye(128, dtype=np.float32), sbuf_dt=BF16)
    smask_np = (np.arange(32)[:, None] < np.arange(32)[None, :])
    smask = b.const("c_smask", smask_np.astype(np.float32), sbuf_dt=BF16)

    maacol = p.sbuf("maacol", (128, 8), F32)
    wkvcol = p.sbuf("wkvcol", (128, 40), F32)
    tdcol = p.sbuf("tdcol", (128, 8), F32)
    ucol = p.sbuf("ucol", (128, 8), F32)
    b.load(maacol[:], t_maa)
    b.load(wkvcol[:], t_wkvrg)
    b.load(tdcol[:], t_td)
    b.load(ucol[:], t_u)

    xn = p.sbuf("xn", (128, 8, T), BF16)
    ln_fm_stream(b, fm_ap(t_x), xn)
    b.store(fm_ap(t_xn), xn[:])

    if upto == "ln":
        b.finish()
        return b

    phX = ExitStack()
    sx = p.sbuf("sx", (128, 8, T), BF16, stack=phX)
    mfb = [p.sbuf(f"mf{f}", (32, T), BF16, stack=phX) for f in range(5)]

    def dvsx(v):
        for j in range(8):
            v.tensor_scalar_mul(sx[:, j, 0:1], xn[:, j, 0:1], -1.0)
            b.dve.inc(v.tensor_sub(sx[:, j, 1:T], xn[:, j, 0:T - 1],
                                   xn[:, j, 1:T]), 2)
    b.blk.vector(dvsx)
    b.barrier()

    with ExitStack() as ph:
        xxx = p.sbuf("xxx", (128, 8, T), BF16, stack=ph)
        w1s = p.sbuf("tm1", (128, 8, 160), BF16, stack=ph)
        pm = p.psum("pm", (32, 512), F32, stack=ph)
        b.load(w1s[:], t_tmw1.rearrange("(a p) c -> p a c", p=128), cast=True)
        b.barrier()

        def dvx(v):
            for j in range(8):
                b.dve.inc(v.scalar_tensor_tensor(
                    xxx[:, j, :], sx[:, j, :], maacol[:, j:j + 1], xn[:, j, :],
                    ALU.mult, ALU.add))
        b.blk.vector(dvx)
        b.barrier()

        for f5 in range(5):
            pe0, act0 = b.pe.n, b.act.n

            def pem(t, f5=f5, act0=act0):
                for ft in range(NF):
                    sl = slice(ft * 512, (ft + 1) * 512)
                    if ft >= 1:
                        b.act.wait(t, act0 + ft)
                    mm = None
                    for kt in range(8):
                        mm = t.matmul(pm[:], w1s[:, kt, f5 * 32:(f5 + 1) * 32],
                                      xxx[:, kt, sl], start=(kt == 0),
                                      stop=(kt == 7))
                    b.pe.inc(mm)

            def acm(a, f5=f5, pe0=pe0):
                for ft in range(NF):
                    sl = slice(ft * 512, (ft + 1) * 512)
                    b.pe.wait(a, pe0 + ft + 1)
                    b.act.inc(a.activation(mfb[f5][:, sl], pm[:], AF.Tanh))
            b.blk.tensor(pem)
            b.blk.scalar(acm)
            b.barrier()


    if upto == "mblocks":
        b.finish()
        return b

    with ExitStack() as ph:
        tmw2s = p.sbuf("tm2", (32, 5, D), BF16, stack=ph)
        xf = p.sbuf("xf2", (128, 8, T), BF16, stack=ph)
        wmat = p.sbuf("wmat", (128, 8, D), BF16, stack=ph)
        stg = p.sbuf("stg", (128, D), BF16, stack=ph)
        stg3 = p.sbuf("stg3", (128, T), BF16, stack=ph)
        pdx = [p.psum("pdx", (128, 512), F32, stack=ph) for _ in range(2)]
        b.load(tmw2s[:], t_tmw2.rearrange("f a c -> a f c"), cast=True)
        b.barrier()

        def make_xf(f5):
            pe0, dve0 = b.pe.n, b.dve.n

            def ped(t, f5=f5, dve0=dve0):
                for i in range(8 * NF):
                    dt_, ft = i // NF, i % NF
                    sl = slice(ft * 512, (ft + 1) * 512)
                    if i >= 2:
                        b.dve.wait(t, dve0 + 2 * (i - 1))
                    b.pe.inc(t.matmul(pdx[i % 2][:],
                                      tmw2s[:, f5, dt_ * 128:(dt_ + 1) * 128],
                                      mfb[f5][:, sl], start=True, stop=True))

            def dvd(v, f5=f5, pe0=pe0):
                for i in range(8 * NF):
                    dt_, ft = i // NF, i % NF
                    sl = slice(ft * 512, (ft + 1) * 512)
                    b.pe.wait(v, pe0 + i + 1)
                    v.scalar_tensor_tensor(
                        xf[:, dt_, sl], pdx[i % 2][:],
                        wkvcol[:, f5 * 8 + dt_:f5 * 8 + dt_ + 1],
                        sx[:, dt_, sl], ALU.add, ALU.mult)
                    b.dve.inc(v.tensor_add(xf[:, dt_, sl], xf[:, dt_, sl],
                                           xn[:, dt_, sl]), 2)
            b.blk.tensor(ped)
            b.blk.vector(dvd)
            b.barrier()

        def proj_tm(dst_hbm, t_w, act=None):
            b.load(wmat[:], fm_ap(t_w), cast=True)
            b.barrier()
            dst_r = dst_hbm.rearrange("(a p) c -> p a c", p=128)
            for tt in range(16):
                pe0, ev0 = b.pe.n, (b.act.n if act else b.dve.n)

                def pep(t, tt=tt, ev0=ev0):
                    evs = b.act if act else b.dve
                    for ft in range(2):
                        sl = slice(ft * 512, (ft + 1) * 512)
                        if ft >= 1:
                            evs.wait(t, ev0 + ft)
                        mm = None
                        for kt in range(8):
                            mm = t.matmul(pdx[0][:],
                                          xf[:, kt, tt * 128:(tt + 1) * 128],
                                          wmat[:, kt, sl], start=(kt == 0),
                                          stop=(kt == 7))
                        b.pe.inc(mm)
                b.blk.tensor(pep)

                if act:
                    def acp(a, tt=tt, pe0=pe0):
                        for ft in range(2):
                            sl = slice(ft * 512, (ft + 1) * 512)
                            b.pe.wait(a, pe0 + ft + 1)
                            b.act.inc(a.activation(stg[:, sl], pdx[0][:], act))
                    b.blk.scalar(acp)
                else:
                    def dvp(v, tt=tt, pe0=pe0):
                        for ft in range(2):
                            sl = slice(ft * 512, (ft + 1) * 512)
                            b.pe.wait(v, pe0 + ft + 1)
                            b.dve.inc(v.tensor_copy(stg[:, sl], pdx[0][:]))
                    b.blk.vector(dvp)
                b.barrier()
                b.store(dst_r[:, tt, :], stg[:])
                b.barrier()

        def proj_fm(dst_hbm, t_w):
            b.load(wmat[:], fm_ap(t_w), cast=True)
            b.barrier()
            dst_r = fm_ap(dst_hbm)
            for rt in range(8):
                pe0, dve0 = b.pe.n, b.dve.n

                def pep(t, rt=rt, dve0=dve0):
                    for ft in range(NF):
                        sl = slice(ft * 512, (ft + 1) * 512)
                        if ft >= 2:
                            b.dve.wait(t, dve0 + ft - 1)
                        mm = None
                        for kt in range(8):
                            mm = t.matmul(pdx[ft % 2][:],
                                          wmat[:, kt, rt * 128:(rt + 1) * 128],
                                          xf[:, kt, sl], start=(kt == 0),
                                          stop=(kt == 7))
                        b.pe.inc(mm)

                def dvp(v, rt=rt, pe0=pe0):
                    for ft in range(NF):
                        sl = slice(ft * 512, (ft + 1) * 512)
                        b.pe.wait(v, pe0 + ft + 1)
                        b.dve.inc(v.tensor_copy(stg3[:, sl], pdx[ft % 2][:]))
                b.blk.tensor(pep)
                b.blk.vector(dvp)
                b.barrier()
                b.store(dst_r[:, rt, :], stg3[:])
                b.barrier()

        make_xf(4)
        proj_tm(t_gtm, t_wg, act=AF.Silu)

        if upto == "g":
            b.finish()
            return b

        make_xf(0)
        with ExitStack() as ph2:
            tdw1s = p.sbuf("tdw1", (128, 8, 64), BF16, stack=ph2)
            tdw2s = p.sbuf("tdw2", (64, D), BF16, stack=ph2)
            th = p.sbuf("th", (64, T), BF16, stack=ph2)
            ewst = p.sbuf("ewst", (128, 8, T), F16, stack=ph2)
            pt4 = p.psum("pt4", (64, T), F32, stack=ph2)
            b.load(tdw1s[:], t_tdw1.rearrange("(a p) c -> p a c", p=128),
                   cast=True)
            b.load(tdw2s[:], t_tdw2, cast=True)
            b.barrier()
            pe0 = b.pe.n

            def pet(t):
                mm = None
                for ft in range(NF):
                    sl = slice(ft * 512, (ft + 1) * 512)
                    for kt in range(8):
                        mm = t.matmul(pt4[:, sl], tdw1s[:, kt, :],
                                      xf[:, kt, sl], start=(kt == 0),
                                      stop=(kt == 7))
                b.pe.inc(mm)

            def act_t(a, pe0=pe0):
                b.pe.wait(a, pe0 + 1)
                b.act.inc(a.activation(th[:], pt4[:], AF.Tanh))
            b.blk.tensor(pet)
            b.blk.scalar(act_t)
            b.barrier()

            pe0, act0 = b.pe.n, b.act.n

            def pew(t, act0=act0):
                for i in range(8 * NF):
                    dt_, ft = i // NF, i % NF
                    sl = slice(ft * 512, (ft + 1) * 512)
                    if i >= 2:
                        b.act.wait(t, act0 + i - 1)
                    b.pe.inc(t.matmul(pdx[i % 2][:],
                                      tdw2s[:, dt_ * 128:(dt_ + 1) * 128],
                                      th[:, sl], start=True, stop=True))

            def acw(a, pe0=pe0):
                for i in range(8 * NF):
                    dt_, ft = i // NF, i % NF
                    sl = slice(ft * 512, (ft + 1) * 512)
                    b.pe.wait(a, pe0 + i + 1)
                    b.act.inc(a.activation(ewst[:, dt_, sl], pdx[i % 2][:],
                                           AF.Exp,
                                           bias=tdcol[:, dt_:dt_ + 1]))
            b.blk.tensor(pew)
            b.blk.scalar(acw)
            b.barrier()
            b.store(fm_ap(t_ew), ewst[:])
            b.barrier()

        if upto == "ew":
            b.finish()
            return b

        make_xf(1)
        proj_fm(t_kf, t_wk)
        proj_tm(t_ktm, t_wk)
        make_xf(2)
        proj_tm(t_vtm, t_wv)
        make_xf(3)
        proj_fm(t_rf, t_wr)

    phX.close()

    if upto == "kvr":
        b.finish()
        return b

    with ExitStack() as ph:
        S = p.sbuf("S", (128, 8, 64), F32, stack=ph)
        Sb = p.sbuf("Sb", (128, 8, 64), BF16, stack=ph)
        rch = p.sbuf("rch", (128, 8, Q), BF16, stack=ph)
        kch = p.sbuf("kch", (128, 8, Q), BF16, stack=ph)
        ewch = p.sbuf("ewch", (128, 8, Q), F32, stack=ph)
        ktmc = p.sbuf("ktmc", (32, D), BF16, stack=ph)
        vtmc = p.sbuf("vtmc", (32, D), BF16, stack=ph)
        zro = p.sbuf("zro", (128, 8, Q), F32, stack=ph)
        cumI = p.sbuf("cumI", (128, 8, Q), F32, stack=ph)
        zqc = p.sbuf("zqc", (128, 8, Q), F32, stack=ph)
        e2f32 = p.sbuf("e2f32", (128, 8, Q), F32, stack=ph)
        e3f32 = p.sbuf("e3f32", (128, 8, Q), F32, stack=ph)
        e4f32 = p.sbuf("e4f32", (128, 8, Q), F32, stack=ph)
        e1 = p.sbuf("e1", (128, 8, Q), BF16, stack=ph)
        e2 = p.sbuf("e2", (128, 8, Q), BF16, stack=ph)
        e3 = p.sbuf("e3", (128, 8, Q), BF16, stack=ph)
        e4f_ = p.sbuf("e4f", (128, 8, Q), BF16, stack=ph)
        deccol = p.sbuf("deccol", (128, 8), F32, stack=ph)
        rt1 = p.sbuf("rt1", (128, 8, Q), BF16, stack=ph)
        rt2 = p.sbuf("rt2", (128, 8, Q), BF16, stack=ph)
        kt1 = p.sbuf("kt1", (128, 8, Q), BF16, stack=ph)
        q1 = p.sbuf("q1", (128, 8, Q), BF16, stack=ph)
        e4t = p.sbuf("e4t", (32, D), BF16, stack=ph)
        q1t = p.sbuf("q1t", (32, D), BF16, stack=ph)
        kth = p.sbuf("kth", (32, D), BF16, stack=ph)
        dtm = p.sbuf("dtm", (32, 16), F32, stack=ph)
        msbt = p.sbuf("msbt", (32, 32), BF16, stack=ph)
        ystg = p.sbuf("ystg", (32, D), BF16, stack=ph)
        pT = [p.psum("pT2", (32, 128), BF16, stack=ph) for _ in range(2)]
        pM = p.psum("pM", (32, 32), F32, stack=ph)
        pYr = p.psum("pYr", (32, 64), F32, stack=ph)
        pKV = p.psum("pKV", (64, 64), F32, stack=ph)

        def vz(v):
            v.memset(S[:], 0.0)
            v.memset(Sb[:], 0.0)
            b.dve.inc(v.memset(zro[:], 0.0))
        b.blk.vector(vz)
        b.barrier()

        ew_r = fm_ap(t_ew)
        rf_r = fm_ap(t_rf)
        kf_r = fm_ap(t_kf)
        for c in range(NCH):
            csl = slice(c * Q, (c + 1) * Q)
            b.load(rch[:], rf_r[:, :, csl])
            b.load(kch[:], kf_r[:, :, csl])
            b.load(ewch[:], ew_r[:, :, csl], cast=True)
            b.load(ktmc[:], t_ktm[c * Q:(c + 1) * Q, :])
            b.load(vtmc[:], t_vtm[c * Q:(c + 1) * Q, :])
            b.barrier()

            def dvB(v, c=c):
                for j in range(8):
                    v.tensor_tensor_scan(cumI[:, j, :], zro[:, j, :],
                                         ewch[:, j, :], 0.0, ALU.add, ALU.add)
                for j in range(8):
                    v.tensor_sub(zqc[:, j, :], cumI[:, j, :], ewch[:, j, :])
                for j in range(8):
                    mid = cumI[:, j, 15:16]
                    tot = cumI[:, j, Q - 1:Q]
                    v.tensor_scalar(e2f32[:, j, :], cumI[:, j, :], mid, 38.0,
                                    ALU.subtract, ALU.min)
                    v.tensor_scalar(e3f32[:, j, :], zqc[:, j, :], mid, -38.0,
                                    ALU.subtract, ALU.max)
                    v.tensor_scalar(e4f32[:, j, :], cumI[:, j, :], tot, None,
                                    ALU.subtract)
                b.dve.inc(v.nop())
            b.blk.vector(dvB)
            b.barrier()

            def acC(a, c=c):
                fl = lambda x: x[:].rearrange("p a c -> p (a c)")
                a.activation(fl(e2), fl(e2f32), AF.Exp)
                a.activation(fl(e3), fl(e3f32), AF.Exp, scale=-1.0)
                a.activation(fl(e4f_), fl(e4f32), AF.Exp)
                a.activation(fl(e1), fl(zqc), AF.Exp, scale=-1.0)
                b.act.inc(a.activation(
                    deccol[:], cumI[:, :, Q - 1:Q].rearrange("p a c -> p (a c)"),
                    AF.Exp, scale=-1.0))
            b.blk.scalar(acC)
            b.barrier()

            def dvD(v, c=c):
                fl = lambda x: x[:].rearrange("p a c -> p (a c)")
                v.tensor_mul(fl(rt1), fl(rch), fl(e1))
                v.tensor_mul(fl(rt2), fl(rch), fl(e3))
                v.tensor_mul(fl(kt1), fl(kch), fl(e2))
                for j in range(8):
                    v.scalar_tensor_tensor(q1[:, j, :], rch[:, j, :],
                                           ucol[:, j:j + 1], kch[:, j, :],
                                           ALU.mult, ALU.mult)
                b.dve.inc(v.nop())
            b.blk.vector(dvD)
            b.barrier()

            pe0, dve0 = b.pe.n, b.dve.n

            def peE(t, c=c, dve0=dve0):
                for i in range(16):
                    if i >= 2:
                        b.dve.wait(t, dve0 + i - 1)
                    src = e4f_[:, i, :] if i < 8 else q1[:, i - 8, :]
                    b.pe.inc(t.transpose(pT[i % 2][:], src, ident[:]))

            def dvE(v, c=c, pe0=pe0):
                for i in range(16):
                    b.pe.wait(v, pe0 + i + 1)
                    dst = e4t if i < 8 else q1t
                    j = i % 8
                    b.dve.inc(v.tensor_copy(dst[:, j * 128:(j + 1) * 128],
                                            pT[i % 2][:]))
                v.tensor_reduce(dtm[:], q1t[:].rearrange(
                    "p (h k) -> p h k", k=64), AX.X, ALU.add)
                b.dve.inc(v.tensor_mul(kth[:], ktmc[:], e4t[:]))
            b.blk.tensor(peE)
            b.blk.vector(dvE)
            b.barrier()

            peG, dveG = b.pe.n, b.dve.n

            def peH(t, c=c, peG=peG, dveG=dveG):
                for h in range(16):
                    pb = 64 * (h % 2)
                    j = h // 2
                    hs = slice(64 * h, 64 * (h + 1))
                    if h >= 1:
                        b.dve.wait(t, dveG + 4 * h - 3)
                    b.pe.inc(t.matmul(pM[:], kt1[pb:pb + 64, j, :],
                                      rt2[pb:pb + 64, j, :],
                                      start=True, stop=True))
                    if h >= 1:
                        b.dve.wait(t, dveG + 4 * h - 2)
                    b.pe.inc(t.matmul(pYr[:], rt1[pb:pb + 64, j, :],
                                      Sb[pb:pb + 64, j, :],
                                      start=True, stop=False))
                    b.dve.wait(t, dveG + 4 * h + 1)
                    b.pe.inc(t.matmul(pYr[:], msbt[:], vtmc[:, hs],
                                      start=False, stop=True))
                    b.pe.inc(t.matmul(pKV[:], kth[:, hs], vtmc[:, hs],
                                      start=True, stop=True))

            def dvH(v, c=c, peG=peG):
                for h in range(16):
                    pb = 64 * (h % 2)
                    j = h // 2
                    hs = slice(64 * h, 64 * (h + 1))
                    b.pe.wait(v, peG + 4 * h + 1)
                    b.dve.inc(v.tensor_mul(msbt[:], pM[:], smask[:, 0:32]))
                    b.pe.wait(v, peG + 4 * h + 3)
                    b.dve.inc(v.scalar_tensor_tensor(
                        ystg[:, hs], vtmc[:, hs], dtm[:, h:h + 1], pYr[:],
                        ALU.mult, ALU.add))
                    b.pe.wait(v, peG + 4 * h + 4)
                    b.dve.inc(v.scalar_tensor_tensor(
                        S[pb:pb + 64, j, :], S[pb:pb + 64, j, :],
                        deccol[pb:pb + 64, j:j + 1], pKV[:],
                        ALU.mult, ALU.add))
                    b.dve.inc(v.tensor_copy(Sb[pb:pb + 64, j, :],
                                            S[pb:pb + 64, j, :]))
            b.blk.tensor(peH)
            b.blk.vector(dvH)
            b.barrier()
            b.store(t_y[c * Q:(c + 1) * Q, :], ystg[:])


    if upto == "scan":
        b.finish()
        return b

    with ExitStack() as ph:
        yt = p.sbuf("yt", (128, D), BF16, stack=ph)
        gt = p.sbuf("gt", (128, D), BF16, stack=ph)
        zt = p.sbuf("zt", (128, D), BF16, stack=ph)
        sq = p.sbuf("sq", (128, D), F32, stack=ph)
        musum = p.sbuf("musum", (128, 16), F32, stack=ph)
        sqsum = p.sbuf("sqsum", (128, 16), F32, stack=ph)
        mu = p.sbuf("mu", (128, 16), F32, stack=ph)
        va = p.sbuf("va", (128, 16), F32, stack=ph)
        rstd = p.sbuf("rstd", (128, 16), F32, stack=ph)
        lnwbc = p.sbuf("lnwbc", (128, D), BF16, stack=ph)
        lnbbc = p.sbuf("lnbbc", (128, D), BF16, stack=ph)
        lnrow = p.sbuf("lnrow", (1, D), F32, stack=ph)
        lnrow2 = p.sbuf("lnrow2", (1, D), F32, stack=ph)
        zfm = p.sbuf("zfm", (128, 8, T), BF16, stack=ph)
        pbc = p.psum("pbc", (128, 512), F32, stack=ph)
        pT2 = [p.psum("pT3", (128, 128), BF16, stack=ph) for _ in range(2)]
        b.load(lnrow[:], t_lnw)
        b.load(lnrow2[:], t_lnb)
        b.barrier()
        pe0, dve0 = b.pe.n, b.dve.n

        def pebc(t, dve0=dve0):
            for i in range(4):
                row = lnrow if i < 2 else lnrow2
                if i >= 1:
                    b.dve.wait(t, dve0 + i)
                b.pe.inc(t.matmul(pbc[:], b.ones_row_f[:, 0:128],
                                  row[:, (i % 2) * 512:(i % 2 + 1) * 512],
                                  start=True, stop=True))

        def dvbc(v, pe0=pe0):
            for i in range(4):
                dst = lnwbc if i < 2 else lnbbc
                b.pe.wait(v, pe0 + i + 1)
                b.dve.inc(v.tensor_copy(
                    dst[:, (i % 2) * 512:(i % 2 + 1) * 512], pbc[:]))
        b.blk.tensor(pebc)
        b.blk.vector(dvbc)
        b.barrier()

        y_r = t_y.rearrange("(a p) c -> p a c", p=128)
        g_r = t_gtm.rearrange("(a p) c -> p a c", p=128)
        for tt in range(16):
            b.load(yt[:], y_r[:, tt, :])
            b.load(gt[:], g_r[:, tt, :])
            b.barrier()

            def dv1(v, tt=tt):
                v.tensor_mul(sq[:], yt[:], yt[:])
                v.tensor_reduce(musum[:], yt[:].rearrange(
                    "p (h k) -> p h k", k=64), AX.X, ALU.add)
                v.tensor_reduce(sqsum[:], sq[:].rearrange(
                    "p (h k) -> p h k", k=64), AX.X, ALU.add)
                v.tensor_scalar_mul(mu[:], musum[:], 1.0 / 64)
                v.tensor_scalar_mul(va[:], sqsum[:], 1.0 / 64)
                v.tensor_mul(sq[:, 0:16], mu[:], mu[:])
                v.tensor_sub(va[:], va[:], sq[:, 0:16])
                b.dve.inc(v.tensor_scalar_add(va[:], va[:], LN_EPS))
            b.blk.vector(dv1)
            b.barrier()

            def ac1(a, tt=tt):
                a.activation(rstd[:], va[:], AF.Ln)
                b.act.inc(a.activation(rstd[:], rstd[:], AF.Exp, scale=-0.5))
            b.blk.scalar(ac1)
            b.barrier()

            def dv2(v, tt=tt):
                for h in range(16):
                    hs = slice(64 * h, 64 * (h + 1))
                    v.tensor_scalar(zt[:, hs], yt[:, hs], mu[:, h:h + 1],
                                    rstd[:, h:h + 1], ALU.subtract, ALU.mult)
                v.tensor_mul(zt[:], zt[:], lnwbc[:])
                v.tensor_add(zt[:], zt[:], lnbbc[:])
                b.dve.inc(v.tensor_mul(zt[:], zt[:], gt[:]))
            b.blk.vector(dv2)
            b.barrier()

            pe0, dve0 = b.pe.n, b.dve.n

            def pez(t, tt=tt, dve0=dve0):
                for j in range(8):
                    if j >= 2:
                        b.dve.wait(t, dve0 + j - 1)
                    b.pe.inc(t.transpose(pT2[j % 2][:],
                                         zt[:, j * 128:(j + 1) * 128],
                                         ident[:]))

            def dvz(v, tt=tt, pe0=pe0):
                for j in range(8):
                    b.pe.wait(v, pe0 + j + 1)
                    b.dve.inc(v.tensor_copy(
                        zfm[:, j, tt * 128:(tt + 1) * 128], pT2[j % 2][:]))
            b.blk.tensor(pez)
            b.blk.vector(dvz)
            b.barrier()

        if debug:
            b.store(fm_ap(t_dbg), zfm[:], cast=True)

        wo_s = p.sbuf("wo_s", (128, 8, D), BF16, stack=ph)
        sfm = p.sbuf("sfm", (128, 8, T), BF16, stack=ph)
        b.load(wo_s[:], fm_ap(t_wo), cast=True)
        b.barrier()
        emit_mm_tiled(
            b, None,
            lambda eng, rt, ft, ps: eng.tensor_copy(
                sfm[:, rt, ft * 512:(ft + 1) * 512], ps[:]),
            lambda kt, rt: wo_s[:, kt, rt * 128:(rt + 1) * 128],
            lambda kt, ft: zfm[:, kt, ft * 512:(ft + 1) * 512],
            8, D, T)
        b.store(fm_ap(t_s), sfm[:])
    if bb is None:
        b.finish()
    return b


_BUILDERS = {}


def build_fused():
    b = B("mega")
    p = b.p
    cur = p.din("xT", (D, T))
    stages = [(build_l0_mixer, build_l0_ffn), (build_l1_mixer, build_l1_ffn),
              (build_l2_mixer, build_l2_ffn), (build_l3_mixer, build_l3_ffn)]
    for li, (mixfn, ffnfn) in enumerate(stages):
        pfx = f"l{li}m_"
        with ExitStack() as stg:
            p.pfx, p.default_stack, p.out_as_tmp = pfx, stg, True
            p.bound = {"xT": cur}
            mixfn(bb=b)
            b.barrier()
        xn_ap = p.douts.get(pfx + "xnT")
        s_ap = p.douts[pfx + "sT"]
        pfx = f"l{li}f_"
        with ExitStack() as stg:
            p.pfx, p.default_stack = pfx, stg
            p.out_as_tmp = (li != 3)
            p.bound = {"xT": cur, "xnT": xn_ap, "sT": s_ap}
            ffnfn(bb=b)
            b.barrier()
        cur = p.douts[pfx + "xoutT"]
    p.pfx, p.default_stack, p.bound, p.out_as_tmp = "", None, {}, False
    b.finish()
    return b


def _get_fused():
    if "mega" not in _BUILDERS:
        _BUILDERS["mega"] = build_fused()
    return _BUILDERS["mega"]


def kernel(**ins):
    from concourse.bass_utils import run_bass_kernel_spmd
    bl = _get_fused()
    f32 = np.float32
    a32 = lambda k: np.asarray(ins[k], f32)
    col = lambda a, nt=8: np.ascontiguousarray(
        np.asarray(a, f32).reshape(nt, 128).T)

    x = a32("L0_attn_x")
    xT = [np.ascontiguousarray(x[c].T) for c in range(8)]

    wx = np.zeros((2048, 132), f32)
    wx[:, 0:129] = a32("L0_attn_W_x")
    shared = {
        "l0m": {
            "w_in": a32("L0_attn_W_in"),
            "conv_w": np.ascontiguousarray(
                a32("L0_attn_conv_w").reshape(16, 128, 4)
                .transpose(1, 0, 2).reshape(128, 64)),
            "conv_b": col(ins["L0_attn_conv_b"], 16),
            "w_x": wx,
            "a_log": a32("L0_attn_A_log"),
            "dt_bias": col(ins["L0_attn_dt_bias"], 16),
            "dcoef": col(ins["L0_attn_D"], 16),
            "w_out": a32("L0_attn_W_out"),
        },
        "l0f": {
            "gamma_x": col(ins["L0_ffn_gamma_x"]),
            "gamma_slot": a32("L0_ffn_gamma_slot").reshape(1, 1024),
            "slot_embeds": a32("L0_ffn_slot_embeds").reshape(4096, 1024),
            "w1": a32("L0_ffn_w1"),
            "w2": a32("L0_ffn_w2"),
        },
        "l1m": {
            "qkv_wT": np.ascontiguousarray(a32("L1_attn_qkv_w").T),
            "q_nw": a32("L1_attn_q_norm_w").reshape(64, 1),
            "q_nb": a32("L1_attn_q_norm_b").reshape(64, 1),
            "k_nw": a32("L1_attn_k_norm_w").reshape(64, 1),
            "k_nb": a32("L1_attn_k_norm_b").reshape(64, 1),
            "proj_wT": np.ascontiguousarray(a32("L1_attn_proj_w").T),
            "proj_b": col(ins["L1_attn_proj_b"]),
        },
        "l1f": {
            "w_gate": a32("L1_ffn_w_gate"),
            "w_up": a32("L1_ffn_w_up"),
            "w_down": a32("L1_ffn_w_down"),
        },
        "l2m": {
            "maa_x": col(ins["L2_attn_time_maa_x"]),
            "wkvrg": np.ascontiguousarray(
                a32("L2_attn_time_maa_wkvrg").reshape(5, 8, 128)
                .transpose(2, 0, 1).reshape(128, 40)),
            "tm_w1": a32("L2_attn_tm_w1"),
            "tm_w2f": a32("L2_attn_tm_w2"),
            "td": col(ins["L2_attn_time_decay"]),
            "td_w1": a32("L2_attn_td_w1"),
            "td_w2": a32("L2_attn_td_w2"),
            "u_col": col(a32("L2_attn_time_faaaa").reshape(1024)),
            "wr": a32("L2_attn_Wr"),
            "wk": a32("L2_attn_Wk"),
            "wv": a32("L2_attn_Wv"),
            "wg": a32("L2_attn_Wg"),
            "wo": a32("L2_attn_Wo"),
            "lnw": a32("L2_attn_ln_w").reshape(1, 1024),
            "lnb": a32("L2_attn_ln_b").reshape(1, 1024),
        },
        "l2f": {
            "fc1_q": np.asarray(ins["L2_ffn_fc1_q"], np.int32),
            "fc1_scales": a32("L2_ffn_fc1_scales"),
            "fc1_bias": col(ins["L2_ffn_fc1_bias"], 32),
            "fc2_q": np.asarray(ins["L2_ffn_fc2_q"], np.int32),
            "fc2_scales": a32("L2_ffn_fc2_scales"),
            "fc2_bias": col(ins["L2_ffn_fc2_bias"]),
            "ln_g": col(ins["L2_ffn_ln_gamma"]),
            "ln_b": col(ins["L2_ffn_ln_beta"]),
        },
        "l3m": {
            "wq_down": a32("L3_attn_Wq_down"),
            "wk_down": a32("L3_attn_Wk_down"),
            "wv_down": a32("L3_attn_Wv_down"),
            "wq_up_c": a32("L3_attn_Wq_up_c"),
            "wq_up_e": a32("L3_attn_Wq_up_e"),
            "wk_up_c": a32("L3_attn_Wk_up_c"),
            "wk_up_e": a32("L3_attn_Wk_up_e"),
            "wv_up": a32("L3_attn_Wv_up"),
            "wc": a32("L3_attn_Wc"),
        },
        "l3f": {
            "phi": a32("L3_ffn_phi").reshape(1024, 2048),
            "We": a32("L3_ffn_W"),
            "eb": a32("L3_ffn_b"),
        },
    }

    wm = {}
    for sk, d in shared.items():
        for k, v in d.items():
            wm[f"{sk}_{k}"] = v
    wm.update(bl.const_inputs)
    im = []
    for c in range(8):
        d = dict(wm)
        d["xT"] = xT[c]
        im.append(d)
    res = run_bass_kernel_spmd(bl.nc, im, core_ids=list(range(8)))
    out = np.stack([np.ascontiguousarray(
        np.asarray(res.results[c]["l3f_xoutT"]).T) for c in range(8)])
    return out.astype(np.float32)
